# version 16
# baseline (speedup 1.0000x reference)
import contextlib
import numpy as np
import ml_dtypes
import concourse.bass as bass
import concourse.mybir as mybir
from concourse.bass_utils import run_bass_kernel_spmd

F32 = mybir.dt.float32
BF16 = mybir.dt.bfloat16
AF = mybir.ActivationFunctionType
ALU = mybir.AluOpType
BF = ml_dtypes.bfloat16

NCORES = 8
D = 2048
DS = 64
T = 2048
L = 128
NCH = T // L
NDCH = D // 128
C_GATE = 8.0
LN_EPS = 1e-5
RMS_EPS = 1e-6
STEMP = 1.0 + 1e-6

ENGINES = ("sync", "vector", "scalar", "tensor", "gpsimd")

DEBUG_TAPS = ()
OWN_WAIT = False
PREP_CUT = 99
PH4_CUT = 99


class Sem:
    __slots__ = ("h", "n")

    def __init__(self, h):
        self.h = h
        self.n = 0


class Bank:

    __slots__ = ("t", "sem", "pending")

    def __init__(self, t, sem=None):
        self.t = t
        self.sem = sem
        self.pending = {}


def _replay(nc, q, semst, semcc):
    if semst.n > 0:
        q["gpsimd"].append(lambda e, v=semst.n, s=semst: e.wait_ge(s.h, v))
    with nc.Block() as block:
        @block.sync
        def _(e):
            for f in q["sync"]:
                f(e)

        @block.vector
        def _(e):
            for f in q["vector"]:
                f(e)

        @block.scalar
        def _(e):
            for f in q["scalar"]:
                f(e)

        @block.tensor
        def _(e):
            for f in q["tensor"]:
                f(e)

        @block.gpsimd
        def _(e):
            for f in q["gpsimd"]:
                f(e)
    return None


def build_program(taps=(), phases=4):
    nc = bass.Bass("TRN2", target_bir_lowering=False, debug=False)

    def din(name, shape, dtype=F32):
        return nc.dram_tensor(name, list(shape), dtype, kind="ExternalInput").ap()

    def dout(name, shape, dtype=F32):
        return nc.dram_tensor(name, list(shape), dtype, kind="ExternalOutput").ap()

    xf = din("xf", [D, T])
    wkq = din("wkq", [128, NDCH * 128])
    wav = din("wav", [128, NDCH * 128])
    wo = din("wo", [DS, D])
    w2 = din("w2", [NDCH * 128, NDCH * 128])
    wab = din("wab", [DS, 1])
    lam = din("lam", [DS, 1])
    normw = din("normw", [128, DS])
    sw2c = din("sw2c", [128, NDCH])
    b2c = din("b2c", [128, NDCH])
    identf = din("identf", [128, 128])
    mask_s = din("mask_s", [128, 128])
    mask_i = din("mask_i", [128, 128])
    alphasel = din("alphasel", [DS, NCORES])
    outfm = dout("outfm", [D, T])

    tapdefs = {
        "knn": ([DS, T], F32), "qnn": ([DS, T], F32), "vs": ([DS, T], F32),
        "alp": ([DS, T], F32), "lamI": ([DS, T], F32),
        "seg1": ([DS, 128], F32), "gath1": ([DS, NCORES * 128], F32),
        "bt": ([128, 2048], BF16), "t1n": ([128, 2048], BF16),
        "et1h": ([64, NCH * DS], BF16), "f1h": ([64, NCH * DS], BF16),
        "vtmf": ([128, NCH * DS], F32), "m1": ([DS, L], BF16),
        "hstart1": ([DS, DS], F32), "scol": ([128, NCH], F32),
        "ytm": ([128, NCH * DS], F32), "ynfm": ([DS, T], BF16),
        "ktil": ([DS, T], BF16),
        "x2h0": ([128, NDCH * 1024], F32),
    }
    tap_aps = {t: dout("tap_" + t, *tapdefs[t]) for t in taps}

    w2bf = nc.dram_tensor("w2bf", [NDCH * 128, NDCH * 128], BF16,
                          kind="Internal").ap()
    cc_in = [nc.dram_tensor(f"cc_in{p}", [DS, 128], F32, kind="Internal")
             for p in range(2)]
    cc_out = [nc.dram_tensor(f"cc_out{p}", [NCORES * DS, 128], F32, kind="Internal")
              for p in range(2)]

    with contextlib.ExitStack() as ST:
        E = ST.enter_context
        sb = lambda name, shape, dtype: E(nc.sbuf_tensor(name, list(shape), dtype))

        identb = sb("identb", [128, 128], BF16)
        identf_t = sb("identf_t", [64, 64], F32)
        maskS = sb("maskS", [128, 128], F32)
        maskI = sb("maskI", [128, 128], F32)
        asel = sb("asel", [DS, NCORES], F32)
        oasel = sb("oasel", [DS, NCORES], F32)
        onescol_b = sb("onescol_b", [128, 1], BF16)
        onesrow_f = sb("onesrow_f", [128, 128], F32)
        normw_t = sb("normw_t", [128, DS], F32)
        sw2_t = sb("sw2_t", [128, NDCH], F32)
        b2_t = sb("b2_t", [128, NDCH], F32)
        wab_t = sb("wab_t", [DS, 1], F32)
        lam_t = sb("lam_t", [DS, 1], F32)
        loga8 = sb("loga8", [DS, 1], F32)
        zeros64 = sb("zeros64", [DS, L], F32)
        cbias = sb("cbias", [128, 3], F32)
        wkq_b = sb("wkq_b", [128, NDCH * 128], BF16)
        wav_b = sb("wav_b", [128, NDCH * 128], BF16)
        wo_b = sb("wo_b", [DS, D], BF16)
        negS_kq = sb("negS_kq", [1, 128], BF16)
        negS_av = sb("negS_av", [1, 128], BF16)
        rowA = sb("rowA", [128, 4096], F32)
        rowC = sb("rowC", [128, 4096], BF16)
        xbf = [sb(f"xbf{i}", [128, T], BF16) for i in range(2)]
        Ktil_s = sb("Ktil_s", [DS, T], BF16)
        Qtil_s = sb("Qtil_s", [DS, T], BF16)
        Qhat_s = sb("Qhat_s", [DS, T], BF16)
        M1_t = sb("M1_t", [DS, L], BF16)
        scol = sb("scol", [128, NCH], F32)
        ynfm = sb("ynfm", [DS, T], BF16)
        HS = sb("HS", [DS, 128], F32)
        Hacc = sb("Hacc", [DS, DS], F32)
        segT = sb("segT", [DS, 128], F32)
        tmp64a = sb("tmp64a", [DS, DS], F32)
        tmp64b = sb("tmp64b", [DS, DS], F32)
        tmp64c = sb("tmp64c", [DS, DS], F32)
        HSh = sb("HSh", [DS, 128], BF16)
        HSl = sb("HSl", [DS, 128], BF16)
        Hah = sb("Hah", [DS, DS], BF16)
        Hal = sb("Hal", [DS, DS], BF16)
        H0lb = sb("H0lb", [DS, DS], BF16)
        onesrow_b = sb("onesrow_b", [128, 128], BF16)
        H0b = sb("H0b", [DS, DS], BF16)
        D_b = sb("D_b", [128, DS], BF16)
        U_b = sb("U_b", [128, DS], BF16)
        e_f = sb("e_f", [128, DS], F32)
        errcol = sb("errcol", [128, 1], F32)
        sm1c = sb("sm1c", [128, 1], F32)
        ytm_t = sb("ytm_t", [128, DS], F32)
        ysq = sb("ysq", [128, DS], F32)
        mscol = sb("mscol", [128, 1], F32)
        rstdc = sb("rstdc", [128, 1], F32)
        yn1 = sb("yn1", [128, DS], F32)
        ynb = sb("ynb", [128, DS], BF16)
        wbfm_b = sb("wbfm_b", [DS, L], BF16)
        vb_t = sb("vb_t", [DS, L], BF16)
        Gt_b = sb("Gt_b", [128, DS], BF16)
        Wb2tm_b = sb("Wb2tm_b", [128, DS], BF16)
        la_t = sb("la_t", [DS, T], F32)
        alp = sb("alp", [DS, T], F32)
        cums = sb("cums", [DS, T], F32)
        lamI = sb("lamI", [DS, T], F32)
        w2t = [wkq_b, wav_b]
        mrs2_b = rowC[0:1, 2048:3072]
        negS_w2 = rowC[0:1, 0:2048]

        AR = sb("AR", [128, 32768], BF16)
        ARF = AR[:].bitcast(F32)
        Bt_s = lambda c: AR[:, c * 128: c * 128 + 128]
        T1n_s = lambda c: AR[:, 2048 + c * 128: 2048 + c * 128 + 128]
        T1t_s = lambda c: AR[:, 4096 + c * 128: 4096 + c * 128 + 128]
        T2t_s = lambda c: AR[:, 6144 + c * 128: 6144 + c * 128 + 128]
        P1m_s = lambda c: AR[:, 8192 + c * 128: 8192 + c * 128 + 128]
        Ph1m_s = lambda c: AR[:, 10240 + c * 128: 10240 + c * 128 + 128]
        Ktm_s = lambda c: AR[:, 12288 + c * DS: 12288 + c * DS + DS]
        Wbtm_s = lambda c: AR[:, 13312 + c * DS: 13312 + c * DS + DS]
        Vtmb_s = lambda c: AR[:, 14336 + c * DS: 14336 + c * DS + DS]
        Vtmf_s = lambda c: ARF[:, 7680 + c * DS: 7680 + c * DS + DS]
        diagG_s = lambda c: ARF[0:64, 8704 + c * DS: 8704 + c * DS + DS]
        ET1h_s = lambda c: AR[0:64, 19456 + c * DS: 19456 + c * DS + DS]
        ET1l_s = lambda c: AR[0:64, 20480 + c * DS: 20480 + c * DS + DS]
        F1h_s = lambda c: AR[0:64, 21504 + c * DS: 21504 + c * DS + DS]
        F1l_s = lambda c: AR[0:64, 22528 + c * DS: 22528 + c * DS + DS]
        ET2h_s = lambda c: AR[0:64, 23552 + c * DS: 23552 + c * DS + DS]
        ET2l_s = lambda c: AR[0:64, 24576 + c * DS: 24576 + c * DS + DS]
        F2h_s = lambda c: AR[0:64, 25600 + c * DS: 25600 + c * DS + DS]
        F2l_s = lambda c: AR[0:64, 26624 + c * DS: 26624 + c * DS + DS]
        ET1_s = (ET1h_s, ET1l_s, F1h_s, F1l_s)
        ET2_s = (ET2h_s, ET2l_s, F2h_s, F2l_s)
        x2_s = lambda o: ARF[:, o * 1024: o * 1024 + 1024]
        Bn_b = AR[:, 27648:27776]
        R0n_b = AR[:, 27776:27904]
        R0t_b = AR[:, 27904:28032]
        B2n_b = AR[:, 28032:28160]
        B2t_b = AR[:, 28160:28288]
        B4n_b = AR[:, 28288:28416]
        R1n_b = AR[:, 28416:28544]
        R1t_b = AR[:, 28544:28672]
        Xn_b = AR[:, 28672:28800]
        T2n_b = AR[:, 28800:28928]
        Dlt_b = AR[:, 28928:29056]
        H0b_s = lambda c: AR[0:64, 29056 + c * DS: 29056 + c * DS + DS]
        H0l_s = lambda c: AR[0:64, 30080 + c * DS: 30080 + c * DS + DS]
        stmp = ARF[0:64, 15552:15616]
        Db_x = [D_b[:, :], AR[:, 31232:31296]]
        Ub_x = [U_b[:, :], AR[:, 31296:31360]]
        ef_x = [e_f[:, :], ARF[:, 15680:15744]]
        err_x = [errcol[:, :], ARF[:, 15744:15745]]
        Dlt_x = [Dlt_b, AR[:, 31490:31618]]
        Xn_x = [Xn_b, AR[:, 31618:31746]]
        T2n_x = [T2n_b, AR[:, 31746:31874]]
        Wb2_x = [Wb2tm_b[:, :], AR[:, 31874:31938]]
        Gt_x = [Gt_b[:, :], AR[:, 31938:32002]]
        sm1_x = [sm1c[:, :], ARF[:, 16001:16002]]
        t64b_x = [tmp64b[:, :], ARF[0:64, 16002:16066]]
        t64c_x = [tmp64c[:, :], ARF[0:64, 16066:16130]]
        t64a_x = [tmp64a[:, :], ARF[0:64, 16130:16194]]
        ytm_x = [ytm_t[:, :], ARF[:, 16194:16258]]
        yn1_x = [yn1[:, :], ARF[:, 16258:16322]]
        ynb_x = [ynb[:, :], AR[:, 32644:32708]]
        ms_x = [mscol[:, :], ARF[:, 16354:16355]]
        rs_x = [rstdc[:, :], ARF[:, 16355:16356]]
        xsq = [AR[:, 2048:3072], AR[:, 3072:4096]]
        ksq_b = AR[0:64, 0:2048]
        gath = rowA[0:64, 2048:3072]

        SCR = sb("SCR", [128, 10240], F32)
        SCRB = SCR[:].bitcast(BF16)
        kraw = SCR[0:64, 0:T]
        qraw = SCR[0:64, 2048:2048 + T]
        vs_f = SCR[0:64, 4096:4096 + T]
        knn = SCR[0:64, 6144:6144 + T]
        qnn = SCR[0:64, 8192:8192 + T]
        vraw_h = SCR[0:64, 6144:6144 + 1024]
        prraw_h = SCR[0:64, 7168:7168 + 1024]
        rs1b_h = SCR[0:64, 8192:8192 + 1024]
        x2n_v = lambda o: SCRB[:, o * 1024: o * 1024 + 1024]
        x2b_t = [SCRB[:, 16384 + i * 512: 16384 + (i + 1) * 512] for i in range(2)]
        x2sq_t = [SCRB[:, 17408 + i * 512: 17408 + (i + 1) * 512] for i in range(2)]
        rs2b = SCR[:, 9216:9216 + 512]
        xrl = [xbf[i][:].bitcast(F32)[:, j * 512:(j + 1) * 512]
               for i in range(2) for j in range(2)]
        out_bufs = [SCR[:, 9728:9728 + 512], rowA[:, 3072:3584],
                    rowA[:, 3584:4096]]

        semld = Sem(E(nc.semaphore("semld")))
        semw = Sem(E(nc.semaphore("semw")))
        semx = [Sem(E(nc.semaphore(f"semx{i}"))) for i in range(2)]
        semw2 = [Sem(E(nc.semaphore(f"semw2{i}"))) for i in range(2)]
        semxr = [Sem(E(nc.semaphore(f"semxr{i}"))) for i in range(4)]
        semo = [Sem(E(nc.semaphore(f"semo{i}"))) for i in range(3)]
        semw2b = [Sem(E(nc.semaphore(f"semw2b{i}"))) for i in range(2)]
        semst = Sem(E(nc.semaphore("semst")))
        semcc = Sem(E(nc.semaphore("semcc")))
        semcl = Sem(E(nc.semaphore("semcl")))
        sempe = Sem(E(nc.semaphore("sempe")))
        semv = Sem(E(nc.semaphore("semv")))
        sema = Sem(E(nc.semaphore("sema")))
        semg = Sem(E(nc.semaphore("semg")))
        bsem = [Sem(E(nc.semaphore(f"bk{i}"))) for i in range(8)]
        PB0 = Bank(E(nc.psum_tensor("PB0", [128, 512], F32)), bsem[0])
        PB1 = Bank(E(nc.psum_tensor("PB1", [128, 512], F32)), bsem[1])
        PS0 = Bank(E(nc.psum_tensor("PS0", [128, 512], F32)), bsem[2])
        PS1 = Bank(E(nc.psum_tensor("PS1", [128, 512], F32)), bsem[3])
        PS2 = Bank(E(nc.psum_tensor("PS2", [128, 512], F32)), bsem[4])
        PROW = Bank(E(nc.psum_tensor("PROW", [128, 512], F32)), bsem[5])
        PTR = Bank(E(nc.psum_tensor("PTR", [128, 512], F32)), bsem[6])
        PC = Bank(E(nc.psum_tensor("PC", [64, 512], F32)), bsem[7])

        q = {e: [] for e in ENGINES}
        OWN = {"vector": semv, "scalar": sema, "gpsimd": semg, "tensor": sempe}
        bank_by_sem = {id(b.sem): b
                       for b in (PB0, PB1, PS0, PS1, PS2, PROW, PTR, PC)}

        def wown(eng):
            wait(eng, OWN[eng])

        def wait(eng, sem, val=None):
            v = sem.n if val is None else val
            if v > 0:
                q[eng].append(lambda e, s=sem, v=v: e.wait_ge(s.h, v))

        def emit(eng, make, inc=None, inck=1):
            own = OWN.get(eng)
            dma_like = inck == 16 or (inc is not None and inc in
                                      (semcc, semcl, semst, semld))
            if eng in ("vector", "scalar") or (eng == "gpsimd" and not dma_like):
                if OWN_WAIT:
                    wait(eng, own)
                own.n += 1
                tgt, k = own, 1
                if inc is not None and id(inc) in bank_by_sem:
                    bank_by_sem[id(inc)].pending[id(own)] = (own, own.n)
            elif eng == "tensor":
                own.n += 1
                tgt, k = own, 1
            else:
                if inc is not None:
                    inc.n += inck
                tgt, k = inc, inck

            def f(e, make=make, tgt=tgt, k=k):
                ins = make(e)
                if tgt is not None:
                    ins.then_inc(tgt.h, k)
            q[eng].append(f)
            return tgt.n if tgt is not None else None

        def pe_bank(bank, ncons=1):
            for s, v in bank.pending.values():
                wait("tensor", s, v)
            bank.pending = {}
            return bank.t

        def csl(t, c, w=L):
            return t[:, c * w:(c + 1) * w]

        def hilo(src_ap, hi_ap, lo_ap, tmpf_ap):
            wait("scalar", semv)
            emit("scalar", lambda e, s=src_ap, h=hi_ap: e.activation(
                h, s, AF.Copy), sema, 1)
            wown("scalar")
            emit("scalar", lambda e, h=hi_ap, t=tmpf_ap: e.activation(
                t, h, AF.Copy), sema, 1)
            wait("vector", sema)
            wown("vector")
            emit("vector", lambda e, s=src_ap, t=tmpf_ap, l=lo_ap: e.tensor_tensor(
                l, s, t, ALU.subtract), semv, 1)
            return semv.n

        emit("sync", lambda e: e.dma_start(out=identf_t[:], in_=identf[0:64, 0:64]),
             semld, 16)
        emit("sync", lambda e: e.dma_start(out=maskS[:], in_=mask_s), semld, 16)
        emit("sync", lambda e: e.dma_start(out=maskI[:], in_=mask_i), semld, 16)
        emit("sync", lambda e: e.dma_start(out=asel[:], in_=alphasel), semld, 16)
        emit("sync", lambda e: e.dma_start(out=normw_t[:], in_=normw), semld, 16)
        emit("sync", lambda e: e.dma_start(out=sw2_t[:], in_=sw2c), semld, 16)
        emit("sync", lambda e: e.dma_start(out=b2_t[:], in_=b2c), semld, 16)
        emit("sync", lambda e: e.dma_start(out=wab_t[:], in_=wab), semld, 16)
        emit("sync", lambda e: e.dma_start(out=lam_t[:], in_=lam), semld, 16)
        consts_ld = semld.n
        emit("gpsimd", lambda e: e.dma_start(out=identb[:], in_=identf), semw, 16)
        emit("gpsimd", lambda e: e.dma_start(out=wkq_b[:], in_=wkq), semw, 16)
        emit("gpsimd", lambda e: e.dma_start(out=wav_b[:], in_=wav), semw, 16)
        emit("gpsimd", lambda e: e.dma_start(out=wo_b[:], in_=wo), semw, 16)
        wsmall_ld = semw.n
        emit("gpsimd", lambda e: e.memset(onescol_b[:], 1.0), semg, 1)
        emit("gpsimd", lambda e: e.memset(onesrow_f[:], 1.0), semg, 1)
        emit("gpsimd", lambda e: e.memset(zeros64[:], 0.0), semg, 1)
        emit("gpsimd", lambda e: e.memset(cbias[:, 0:1], 1e-8), semg, 1)
        emit("gpsimd", lambda e: e.memset(cbias[:, 1:2], LN_EPS), semg, 1)
        emit("gpsimd", lambda e: e.memset(cbias[:, 2:3], RMS_EPS), semg, 1)
        emit("gpsimd", lambda e: e.memset(onesrow_b[:], 1.0), semg, 1)
        memset_done = semg.n

        wait("scalar", semld, consts_ld)
        wait("scalar", semg, memset_done)
        emit("scalar", lambda e: e.activation(loga8[:], lam_t[:], AF.Sigmoid), sema, 1)
        wown("scalar")
        emit("scalar", lambda e: e.activation(loga8[:], loga8[:], AF.Ln,
                                              bias=cbias[0:64, 0:1]), sema, 1)
        wown("scalar")
        emit("scalar", lambda e: e.activation(loga8[:], loga8[:], AF.Copy,
                                              scale=C_GATE), sema, 1)
        wait("vector", semld, consts_ld)
        emit("vector", lambda e: e.tensor_scalar(oasel[:], asel[:], -1.0, 1.0,
                                                 ALU.mult, ALU.add), semv, 1)

        for wtile, outrow in ((wkq_b, negS_kq), (wav_b, negS_av)):
            ps = pe_bank(PROW, 1)
            wait("tensor", semw, wsmall_ld)
            wait("tensor", semg, memset_done)
            for i in range(NDCH):
                emit("tensor", lambda e, i=i, wtile=wtile, ps=ps: e.matmul(
                    ps[0:1, 0:128], onescol_b[:], wtile[:, i * 128:(i + 1) * 128],
                    start=(i == 0), stop=(i == NDCH - 1)), sempe, 1)
            wait("scalar", sempe)
            emit("scalar", lambda e, ps=ps, outrow=outrow: e.activation(
                outrow[:], ps[0:1, 0:128], AF.Copy, scale=-1.0 / D), PROW.sem, 1)

        xcast_pe = {}
        for h in range(2):
            t0 = h * 1024
            ps_kq = [pe_bank(PB0, 2), pe_bank(PB1, 2)]
            ps_va = [pe_bank(PS0, 2), pe_bank(PS1, 2)]
            ps_sx = [pe_bank(PROW, 2), pe_bank(PS2, 2)]
            ps_sq = [pe_bank(PC, 1), pe_bank(PTR, 1)]
            for i in range(NDCH):
                g = h * NDCH + i
                buf = xbf[g % 2]
                if g >= 2:
                    wait("gpsimd", sempe, xcast_pe[g - 2])
                emit("gpsimd", lambda e, i=i, buf=buf: e.dma_start(
                    out=buf[:], in_=xf[i * 128:(i + 1) * 128, :]), semx[g % 2], 16)
                xldv = semx[g % 2].n
                wait("scalar", semx[g % 2], xldv)
                emit("scalar", lambda e, i=i, t0=t0, buf=buf: e.activation(
                    xsq[i % 2][:], buf[:, t0:t0 + 1024], AF.Square), sema, 1)
                sq_v = sema.n
                wait("tensor", semx[g % 2], xldv)
                first = (i == 0)
                last = (i == NDCH - 1)
                for tt in range(2):
                    emit("tensor", lambda e, tt=tt, t0=t0, buf=buf, first=first,
                         last=last, ps=ps_sx[tt]: e.matmul(
                        ps[0:1, :], onescol_b[:],
                        buf[:, t0 + tt * 512: t0 + tt * 512 + 512],
                        start=first, stop=last), sempe, 1)
                wait("tensor", sema, sq_v)
                for tt in range(2):
                    emit("tensor", lambda e, i=i, tt=tt, first=first, last=last,
                         ps=ps_sq[tt]: e.matmul(
                        ps[0:1, 0:512], onescol_b[:],
                        xsq[i % 2][:, tt * 512: tt * 512 + 512],
                        start=first, stop=last), sempe, 1)
                for tt in range(2):
                    emit("tensor", lambda e, i=i, tt=tt, t0=t0, buf=buf, first=first,
                         ps=ps_kq[tt]: e.matmul(
                        ps[:, :], wkq_b[:, i * 128:(i + 1) * 128],
                        buf[:, t0 + tt * 512: t0 + tt * 512 + 512],
                        start=first, stop=False), sempe, 1)
                    emit("tensor", lambda e, i=i, tt=tt, t0=t0, buf=buf, first=first,
                         ps=ps_va[tt]: e.matmul(
                        ps[:, :], wav_b[:, i * 128:(i + 1) * 128],
                        buf[:, t0 + tt * 512: t0 + tt * 512 + 512],
                        start=first, stop=False), sempe, 1)
                xcast_pe[g] = sempe.n
            stats_pe = sempe.n
            wait("scalar", sempe, stats_pe)
            for tt in range(2):
                c0 = t0 + tt * 512
                sxbank = PROW if tt == 0 else PS2
                sqbank = PC if tt == 0 else PTR
                emit("scalar", lambda e, c0=c0, ps=ps_sx[tt]: e.activation(
                    rowA[0:1, c0: c0 + 512],
                    ps[0:1, :], AF.Copy), sxbank.sem, 1)
                emit("scalar", lambda e, c0=c0, ps=ps_sx[tt]: e.activation(
                    rowC[0:1, c0: c0 + 512],
                    ps[0:1, :], AF.Copy), sxbank.sem, 1)
                wown("scalar")
                emit("scalar", lambda e, c0=c0: e.activation(
                    rowA[0:1, 2048 + c0: 2048 + c0 + 512],
                    rowA[0:1, c0: c0 + 512],
                    AF.Square, scale=1.0 / D), sema, 1)
                m1sq_v = sema.n
                wait("vector", sema, m1sq_v)
                wait("vector", sempe, stats_pe)
                emit("vector", lambda e, c0=c0, ps=ps_sq[tt]:
                     e.scalar_tensor_tensor(
                    rowA[0:1, 2048 + c0: 2048 + c0 + 512],
                    ps[0:1, 0:512], 1.0 / D,
                    rowA[0:1, 2048 + c0: 2048 + c0 + 512],
                    ALU.mult, ALU.subtract), sqbank.sem, 1)
                var_v = semv.n
                wait("scalar", semv, var_v)
                emit("scalar", lambda e, c0=c0: e.activation(
                    rowA[64:65, c0: c0 + 512],
                    rowA[0:1, 2048 + c0: 2048 + c0 + 512],
                    AF.Ln, bias=cbias[64:65, 1:2]), sema, 1)
                wown("scalar")
                emit("scalar", lambda e, c0=c0: e.activation(
                    rowA[64:65, c0: c0 + 512],
                    rowA[64:65, c0: c0 + 512],
                    AF.Exp, scale=-0.5), sema, 1)
            rows_v = sema.n
            wait("tensor", sema, rows_v)
            for tt in range(2):
                c0 = t0 + tt * 512
                emit("tensor", lambda e, tt=tt, c0=c0, ps=ps_kq[tt]: e.matmul(
                    ps[:, :], negS_kq[:],
                    rowC[0:1, c0: c0 + 512], start=False, stop=True), sempe, 1)
                emit("tensor", lambda e, tt=tt, c0=c0, ps=ps_va[tt]: e.matmul(
                    ps[:, :], negS_av[:],
                    rowC[0:1, c0: c0 + 512], start=False, stop=True), sempe, 1)
            corr_pe = sempe.n
            for tt in range(2):
                c0 = t0 + tt * 512
                wown("scalar")
                emit("scalar", lambda e, c0=c0, t0=t0: e.activation(
                    rowC[0:1, 2048 + c0 - t0: 2048 + c0 - t0 + 512],
                    rowA[64:65, c0: c0 + 512], AF.Copy), sema, 1)
                rs1bf_a = sema.n
                psb = pe_bank(PC, 1)
                wait("tensor", sema, rs1bf_a)
                emit("tensor", lambda e, c0=c0, t0=t0, psb=psb: e.matmul(
                    psb[:, 0:512], onesrow_b[0:1, 0:64],
                    rowC[0:1, 2048 + c0 - t0: 2048 + c0 - t0 + 512],
                    start=True, stop=True), sempe, 1)
                wait("vector", sempe)
                emit("vector", lambda e, tt=tt, psb=psb: e.tensor_copy(
                    rs1b_h[:, tt * 512: tt * 512 + 512], psb[:, 0:512]), PC.sem, 1)
            rs1b_v = semv.n
            wait("scalar", sempe, corr_pe)
            wait("scalar", semv)
            for tt in range(2):
                c0 = t0 + tt * 512
                bk = PB0 if tt == 0 else PB1
                bv = PS0 if tt == 0 else PS1
                emit("scalar", lambda e, tt=tt, c0=c0, ps=ps_kq[tt]: e.activation(
                    kraw[:, c0: c0 + 512], ps[0:64, :], AF.Copy), bk.sem, 1)
                emit("scalar", lambda e, tt=tt, c0=c0, ps=ps_kq[tt]: e.activation(
                    qraw[:, c0: c0 + 512], ps[64:128, :], AF.Copy), bk.sem, 1)
                emit("scalar", lambda e, tt=tt, c0=c0, t0=t0, ps=ps_va[tt]:
                     e.activation(
                    prraw_h[:, c0 - t0: c0 - t0 + 512], ps[0:64, :], AF.Copy),
                    bv.sem, 1)
                emit("scalar", lambda e, tt=tt, c0=c0, t0=t0, ps=ps_va[tt]:
                     e.activation(
                    vraw_h[:, c0 - t0: c0 - t0 + 512], ps[64:128, :], AF.Copy),
                    bv.sem, 1)
            evac_a = sema.n
            wait("vector", sema, evac_a)
            wait("vector", semv, rs1b_v)
            emit("vector", lambda e, t0=t0: e.tensor_tensor(
                vs_f[:, t0:t0 + 1024], vraw_h[:, :], rs1b_h[:, :], ALU.mult), semv, 1)
            emit("vector", lambda e, t0=t0: e.tensor_tensor(
                la_t[:, t0:t0 + 1024], prraw_h[:, :], rs1b_h[:, :], ALU.mult), semv, 1)
        sweeps_v = semv.n
        sweeps_a = sema.n

        pcst = [AR[:, 23552:25600], AR[:, 25600:27648]]
        for t in range(NDCH):
            if t >= 2:
                wait("gpsimd", semw2b[t % 2], semw2b[t % 2].n)
            emit("gpsimd", lambda e, t=t: e.dma_start(
                out=pcst[t % 2], in_=w2[t * 128:(t + 1) * 128, :]),
                semw2[t % 2], 16)
            wait("gpsimd", semw2[t % 2], semw2[t % 2].n)
            emit("gpsimd", lambda e, t=t: e.dma_start(
                out=w2bf[t * 128:(t + 1) * 128, :], in_=pcst[t % 2]),
                semw2b[t % 2], 16)

        wait("scalar", semv, sweeps_v)
        wown("scalar")
        emit("scalar", lambda e: e.activation(ksq_b[:, :], kraw[:, :], AF.Square),
             sema, 1)
        ksq_a = sema.n
        rbank = (PROW, PS2, PB0, PB1)
        bbank = (PC, PTR, PS0, PS1)
        kmm_pe = 0
        for phase, (roff, raw, pb) in enumerate(
                ((0, kraw, 32), (2048, qraw, 64))):
            if phase == 1:
                wait("scalar", sempe, kmm_pe)
                wown("scalar")
                emit("scalar", lambda e: e.activation(
                    ksq_b[:, :], qraw[:, :], AF.Square), sema, 1)
            sq_a = sema.n
            pss = []
            wait("tensor", sema, sq_a)
            for tt in range(4):
                ps = pe_bank(rbank[tt], 1)
                pss.append(ps)
                emit("tensor", lambda e, tt=tt, ps=ps: e.matmul(
                    ps[0:1, :], onescol_b[0:64, :],
                    ksq_b[:, tt * 512: tt * 512 + 512],
                    start=True, stop=True), sempe, 1)
            kmm_pe = sempe.n
            wait("scalar", sempe, kmm_pe)
            for tt in range(4):
                emit("scalar", lambda e, tt=tt, roff=roff, ps=pss[tt]:
                     e.activation(
                    rowA[32:33, roff + tt * 512: roff + tt * 512 + 512],
                    ps[0:1, :], AF.Ln), rbank[tt].sem, 1)
            wown("scalar")
            for tt in range(4):
                emit("scalar", lambda e, tt=tt, roff=roff: e.activation(
                    rowA[32:33, roff + tt * 512: roff + tt * 512 + 512],
                    rowA[32:33, roff + tt * 512: roff + tt * 512 + 512],
                    AF.Exp, scale=-0.5), sema, 1)
            wown("scalar")
            for tt in range(4):
                emit("scalar", lambda e, tt=tt, roff=roff, pb=pb: e.activation(
                    rowC[pb:pb + 1, roff % 2048 + tt * 512:
                         roff % 2048 + tt * 512 + 512],
                    rowA[32:33, roff + tt * 512: roff + tt * 512 + 512],
                    AF.Copy), sema, 1)
        rn_a = sema.n
        wait("tensor", sema, rn_a)
        for pb, raw, dst in ((32, kraw, knn), (64, qraw, qnn)):
            pss2 = []
            for tt in range(4):
                psb = pe_bank(bbank[tt], 1)
                pss2.append(psb)
                emit("tensor", lambda e, tt=tt, pb=pb, psb=psb: e.matmul(
                    psb[0:64, 0:512], onesrow_b[pb:pb + 1, 0:64],
                    rowC[pb:pb + 1, tt * 512: tt * 512 + 512],
                    start=True, stop=True), sempe, 1)
            bc_pe = sempe.n
            wait("vector", sempe, bc_pe)
            for tt in range(4):
                emit("vector", lambda e, tt=tt, raw=raw, dst=dst, psb=pss2[tt]:
                     e.tensor_tensor(
                    dst[:, tt * 512: tt * 512 + 512], psb[0:64, 0:512],
                    raw[:, tt * 512: tt * 512 + 512], ALU.mult), bbank[tt].sem, 1)
        knn_v = semv.n

        wait("scalar", semv, sweeps_v)
        emit("scalar", lambda e: e.activation(alp[:, :], la_t[:, :], AF.Sigmoid,
                                              bias=wab_t[:]), sema, 1)
        wown("scalar")
        emit("scalar", lambda e: e.activation(la_t[:, :], alp[:, :], AF.Copy,
                                              scale=loga8[:]), sema, 1)
        wown("scalar")
        emit("scalar", lambda e: e.activation(alp[:, :], la_t[:, :], AF.Exp), sema, 1)
        alpha_a = sema.n
        wait("vector", sema, alpha_a)
        for c in range(NCH):
            emit("vector", lambda e, c=c: e.tensor_tensor_scan(
                cums[:, c * L:(c + 1) * L], la_t[:, c * L:(c + 1) * L],
                zeros64[:, :], 0.0, ALU.add, ALU.add), semv, 1)
        cums_v = semv.n
        wait("scalar", semv, cums_v)
        emit("scalar", lambda e: e.activation(lamI[:, :], cums[:, :], AF.Exp), sema, 1)
        lam_a = sema.n
        wait("vector", sema, alpha_a)
        wown("vector")
        emit("vector", lambda e: e.tensor_scalar(la_t[:, :], alp[:, :], -1.0, 1.0,
                                                 ALU.mult, ALU.add), semv, 1)
        wown("vector")
        emit("vector", lambda e: e.tensor_tensor(alp[:, :], la_t[:, :], knn[:, :],
                                                 ALU.mult), semv, 1)
        w1_v = semv.n

        rlam_f = SCR[0:64, 2048:2048 + T]
        wait("scalar", semv, w1_v)
        wait("scalar", sema, lam_a)
        emit("scalar", lambda e: e.activation(
            la_t[:, 1:T], lamI[:, 0:T - 1], AF.Copy), sema, 1)
        emit("scalar", lambda e: e.activation(
            rlam_f[:, :], cums[:, :], AF.Exp, scale=-1.0), sema, 1)
        lamE_a = sema.n
        wait("gpsimd", sema, lamE_a)
        for c in range(NCH):
            emit("gpsimd", lambda e, c=c: e.memset(
                la_t[:, c * L: c * L + 1], 1.0), semg, 1)
        lamE_g = semg.n
        til_piece = []
        wait("vector", sema, lamE_a)
        wait("vector", semg, lamE_g)
        wown("vector")
        for p4 in range(4):
            s4 = slice(p4 * 512, p4 * 512 + 512)
            emit("vector", lambda e, s4=s4: e.tensor_tensor(
                Ktil_s[:, s4], knn[:, s4], la_t[:, s4], ALU.mult), semv, 1)
            emit("vector", lambda e, s4=s4: e.tensor_tensor(
                Qtil_s[:, s4], qnn[:, s4], la_t[:, s4], ALU.mult), semv, 1)
            emit("vector", lambda e, s4=s4: e.tensor_tensor(
                Qhat_s[:, s4], qnn[:, s4], lamI[:, s4], ALU.mult), semv, 1)
            til_piece.append(semv.n)

        for tname, src in (("knn", knn), ("qnn", qnn), ("vs", vs_f),
                           ("alp", alp), ("lamI", lamI)):
            if tname in taps:
                wait("sync", semv, semv.n)
                wait("sync", sema, sema.n)
                emit("sync", lambda e, t=tname, src=src: e.dma_start(
                    out=tap_aps[t], in_=src[:, :]), semst, 16)

        if phases < 1.2:
            _replay(nc, q, semst, semcc)
            return nc
        PP = [
            dict(M1=M1_t[:, :], wbfm=wbfm_b[:, :], vb=vb_t[:, :],
                 Bn=Bn_b, R0n=R0n_b, R0t=R0t_b, B2n=B2n_b, B2t=B2t_b,
                 B4n=B4n_b, R1n=R1n_b, R1t=R1t_b, Gt=Gt_b[:, :],
                 t64b=tmp64b[:, :], t64c=tmp64c[:, :], t64a=tmp64a[:, :],
                 bA=PS0, bR0=PS1, bB2=PS2, bB4=PS0, bT1=PS1, bG=PS2, bEF=PC,
                 ptlo=0),
            dict(M1=AR[0:64, 30080:30208], wbfm=AR[0:64, 30208:30336],
                 vb=AR[0:64, 30336:30464],
                 Bn=AR[:, 29056:29184], R0n=AR[:, 29184:29312],
                 R0t=AR[:, 29312:29440], B2n=AR[:, 29440:29568],
                 B2t=AR[:, 29568:29696], B4n=AR[:, 29696:29824],
                 R1n=AR[:, 29824:29952], R1t=AR[:, 29952:30080],
                 Gt=AR[:, 30464:30528],
                 t64b=ARF[0:64, 15264:15328], t64c=ARF[0:64, 15328:15392],
                 t64a=ARF[0:64, 15392:15456],
                 bA=PB0, bR0=PB1, bB2=PROW, bB4=PB0, bT1=PB1, bG=PROW,
                 bEF=PB0, ptlo=1),
        ]
        pv = [dict(tr=0, psA=0, b2=0, b4=0, t1=0, ef=0, hiloa=0)
              for _ in range(2)]
        wait("scalar", semv, w1_v)

        def prep_s1(c, P, V):
            if V["psA"]:
                wait("vector", sempe, V["psA"])
            emit("vector", lambda e, c=c, P=P: e.tensor_tensor(
                P["M1"], alp[:, c * L:(c + 1) * L],
                rlam_f[:, c * L:(c + 1) * L], ALU.mult), semv, 1)
            m1_v = semv.n
            if V["tr"]:
                wait("scalar", sempe, V["tr"])
            emit("scalar", lambda e, c=c: e.activation(
                diagG_s(c), identf_t[:, :], AF.Copy,
                scale=lamI[:, c * L + L - 1: c * L + L]), sema, 1)
            wait("scalar", semv, m1_v)
            emit("scalar", lambda e, c=c, P=P: e.activation(
                P["wbfm"], P["M1"], AF.Copy,
                scale=lamI[:, c * L + L - 1: c * L + L]), sema, 1)
            emit("scalar", lambda e, c=c, P=P: e.activation(
                P["vb"], vs_f[:, c * L:(c + 1) * L], AF.Copy), sema, 1)
            return m1_v, sema.n

        def prep_s2pair(cs, m1s, scas):
            ptf = pe_bank(PTR, 8)
            pt = ptf[:].bitcast(BF16)
            for (c, P), sc_a in zip(cs, scas):
                o = P["ptlo"] * 192
                wait("tensor", sema, sc_a)
                emit("tensor", lambda e, c=c, pt=pt, o=o: e.transpose(
                    pt[:, o: o + 64], csl(Ktil_s, c), identb[0:64, 0:64]),
                    sempe, 1)
                emit("tensor", lambda e, P=P, pt=pt, o=o: e.transpose(
                    pt[:, o + 64: o + 128], P["wbfm"], identb[0:64, 0:64]),
                    sempe, 1)
                emit("tensor", lambda e, P=P, pt=pt, o=o: e.transpose(
                    pt[:, o + 128: o + 192], P["vb"], identb[0:64, 0:64]),
                    sempe, 1)
                pv[P["ptlo"]]["tr"] = sempe.n
            wait("scalar", sempe)
            for (c, P), sc_a in zip(cs, scas):
                o = P["ptlo"] * 192
                emit("scalar", lambda e, c=c, pt=pt, o=o: e.activation(
                    Ktm_s(c), pt[:, o: o + 64], AF.Copy), PTR.sem, 1)
                emit("scalar", lambda e, c=c, pt=pt, o=o: e.activation(
                    Wbtm_s(c), pt[:, o + 64: o + 128], AF.Copy), PTR.sem, 1)
                emit("scalar", lambda e, c=c, pt=pt, o=o: e.activation(
                    Vtmb_s(c), pt[:, o + 128: o + 192], AF.Copy), PTR.sem, 1)
                emit("scalar", lambda e, c=c, pt=pt, o=o: e.activation(
                    Vtmf_s(c), pt[:, o + 128: o + 192], AF.Copy), PTR.sem, 1)

        def prep_s3(c, P, V, m1_v):
            psA = pe_bank(P["bA"], 3)
            wait("tensor", semv, m1_v)
            emit("tensor", lambda e, c=c, P=P, psA=psA: e.matmul(
                psA[:, 0:128], P["M1"], csl(Ktil_s, c), start=True, stop=False),
                sempe, 1)
            emit("tensor", lambda e, c=c, P=P, psA=psA: e.matmul(
                psA[:, 128:256], P["M1"], csl(Qtil_s, c), start=False,
                stop=False), sempe, 1)
            emit("tensor", lambda e, c=c, P=P, psA=psA: e.matmul(
                psA[:, 256:384], P["M1"], csl(Qhat_s, c), start=False,
                stop=True), sempe, 1)
            V["psA"] = sempe.n
            wait("vector", sempe, V["psA"])
            emit("vector", lambda e, c=c, psA=psA: e.scalar_tensor_tensor(
                Bt_s(c), psA[:, 0:128], -1.0, maskS[:, :], ALU.mult, ALU.mult),
                P["bA"].sem, 1)
            emit("vector", lambda e, c=c, psA=psA: e.tensor_tensor(
                P1m_s(c), psA[:, 128:256], maskS[:, :], ALU.mult),
                P["bA"].sem, 1)
            emit("vector", lambda e, c=c, psA=psA: e.tensor_tensor(
                Ph1m_s(c), psA[:, 256:384], maskI[:, :], ALU.mult),
                P["bA"].sem, 1)
            return semv.n

        def prep_s4pair(cs, bvs):
            pt2 = pe_bank(PTR, 2)[:].bitcast(BF16)
            bn_pes = []
            for (c, P), b_v in zip(cs, bvs):
                o = P["ptlo"] * 128
                wait("tensor", semv, b_v)
                emit("tensor", lambda e, c=c, pt2=pt2, o=o: e.transpose(
                    pt2[:, o: o + 128], Bt_s(c), identb[:, :]), sempe, 1)
                bn_pes.append(sempe.n)
            wait("scalar", sempe)
            for (c, P), _ in zip(cs, bvs):
                o = P["ptlo"] * 128
                emit("scalar", lambda e, P=P, pt2=pt2, o=o: e.activation(
                    P["Bn"], pt2[:, o: o + 128], AF.Copy), PTR.sem, 1)
            return sema.n

        def prep_s4b(c, P, bn_a):
            ps0r = pe_bank(P["bR0"], 2)
            wait("tensor", sema, bn_a)
            emit("tensor", lambda e, P=P, ps0r=ps0r: e.matmul(
                ps0r[:, 0:128], identb[:, :], P["Bn"], start=True, stop=False),
                sempe, 1)
            emit("tensor", lambda e, ps0r=ps0r: e.matmul(
                ps0r[:, 0:128], identb[:, :], identb[:, :], start=False,
                stop=False), sempe, 1)
            emit("tensor", lambda e, c=c, ps0r=ps0r: e.matmul(
                ps0r[:, 128:256], identb[:, :], Bt_s(c), start=False, stop=False),
                sempe, 1)
            emit("tensor", lambda e, ps0r=ps0r: e.matmul(
                ps0r[:, 128:256], identb[:, :], identb[:, :], start=False,
                stop=True), sempe, 1)
            r0_pe = sempe.n
            wait("scalar", sempe, r0_pe)
            emit("scalar", lambda e, P=P, ps0r=ps0r: e.activation(
                P["R0n"], ps0r[:, 0:128], AF.Copy), P["bR0"].sem, 1)
            emit("scalar", lambda e, P=P, ps0r=ps0r: e.activation(
                P["R0t"], ps0r[:, 128:256], AF.Copy), P["bR0"].sem, 1)

        def prep_s5(c, P):
            ps2r = pe_bank(P["bB2"], 2)
            emit("tensor", lambda e, c=c, P=P, ps2r=ps2r: e.matmul(
                ps2r[:, 0:128], Bt_s(c), P["Bn"], start=True, stop=False),
                sempe, 1)
            emit("tensor", lambda e, c=c, P=P, ps2r=ps2r: e.matmul(
                ps2r[:, 128:256], P["Bn"], Bt_s(c), start=False, stop=True),
                sempe, 1)
            V = pv[P["ptlo"]]
            V["b2"] = sempe.n
            wait("scalar", sempe, V["b2"])
            emit("scalar", lambda e, P=P, ps2r=ps2r: e.activation(
                P["B2n"], ps2r[:, 0:128], AF.Copy), P["bB2"].sem, 1)
            emit("scalar", lambda e, P=P, ps2r=ps2r: e.activation(
                P["B2t"], ps2r[:, 128:256], AF.Copy), P["bB2"].sem, 1)
            return sema.n

        def prep_s6(c, P, b2_a):
            ps3r = pe_bank(P["bB4"], 3)
            wait("tensor", sema, b2_a)
            emit("tensor", lambda e, P=P, ps3r=ps3r: e.matmul(
                ps3r[:, 0:128], P["B2t"], P["B2n"], start=True, stop=False),
                sempe, 1)
            emit("tensor", lambda e, P=P, ps3r=ps3r: e.matmul(
                ps3r[:, 128:256], P["R0t"], P["B2n"], start=False, stop=False),
                sempe, 1)
            emit("tensor", lambda e, P=P, ps3r=ps3r: e.matmul(
                ps3r[:, 128:256], identb[:, :], P["R0n"], start=False,
                stop=False), sempe, 1)
            emit("tensor", lambda e, P=P, ps3r=ps3r: e.matmul(
                ps3r[:, 256:384], P["B2n"], P["R0t"], start=False, stop=False),
                sempe, 1)
            emit("tensor", lambda e, P=P, ps3r=ps3r: e.matmul(
                ps3r[:, 256:384], identb[:, :], P["R0t"], start=False,
                stop=True), sempe, 1)
            pv[P["ptlo"]]["b4"] = sempe.n
            wait("scalar", sempe, pv[P["ptlo"]]["b4"])
            emit("scalar", lambda e, P=P, ps3r=ps3r: e.activation(
                P["B4n"], ps3r[:, 0:128], AF.Copy), P["bB4"].sem, 1)
            emit("scalar", lambda e, P=P, ps3r=ps3r: e.activation(
                P["R1n"], ps3r[:, 128:256], AF.Copy), P["bB4"].sem, 1)
            emit("scalar", lambda e, P=P, ps3r=ps3r: e.activation(
                P["R1t"], ps3r[:, 256:384], AF.Copy), P["bB4"].sem, 1)
            return sema.n

        def prep_s7(c, P, r1_a):
            ps4r = pe_bank(P["bT1"], 2)
            wait("tensor", sema, r1_a)
            emit("tensor", lambda e, P=P, ps4r=ps4r: e.matmul(
                ps4r[:, 0:128], P["R1t"], P["B4n"], start=True, stop=False),
                sempe, 1)
            emit("tensor", lambda e, P=P, ps4r=ps4r: e.matmul(
                ps4r[:, 0:128], identb[:, :], P["R1n"], start=False, stop=False),
                sempe, 1)
            emit("tensor", lambda e, P=P, ps4r=ps4r: e.matmul(
                ps4r[:, 128:256], P["B4n"], P["R1t"], start=False, stop=False),
                sempe, 1)
            emit("tensor", lambda e, P=P, ps4r=ps4r: e.matmul(
                ps4r[:, 128:256], identb[:, :], P["R1t"], start=False,
                stop=True), sempe, 1)
            pv[P["ptlo"]]["t1"] = sempe.n
            wait("scalar", sempe, pv[P["ptlo"]]["t1"])
            emit("scalar", lambda e, c=c, ps4r=ps4r: e.activation(
                T1n_s(c), ps4r[:, 0:128], AF.Copy), P["bT1"].sem, 1)
            emit("scalar", lambda e, c=c, ps4r=ps4r: e.activation(
                T1t_s(c), ps4r[:, 128:256], AF.Copy), P["bT1"].sem, 1)
            return sema.n

        def prep_s8(c, P, t1_a):
            V = pv[P["ptlo"]]
            ps5r = pe_bank(P["bG"], 1)
            wait("tensor", sema, t1_a)
            emit("tensor", lambda e, c=c, P=P, ps5r=ps5r: e.matmul(
                ps5r[:, 0:64], T1n_s(c), Wbtm_s(c), start=True, stop=True),
                sempe, 1)
            wait("scalar", sempe)
            if V["ef"]:
                wait("scalar", sempe, V["ef"])
            emit("scalar", lambda e, P=P, ps5r=ps5r: e.activation(
                P["Gt"], ps5r[:, 0:64], AF.Copy), P["bG"].sem, 1)
            g1_a = sema.n
            psc = pe_bank(P["bEF"], 2)
            wait("tensor", sema, g1_a)
            emit("tensor", lambda e, c=c, P=P, psc=psc: e.matmul(
                psc[0:64, 0:64], Ktm_s(c), P["Gt"], start=True, stop=False),
                sempe, 1)
            emit("tensor", lambda e, c=c, P=P, psc=psc: e.matmul(
                psc[0:64, 64:128], P["Gt"], Vtmb_s(c), start=False, stop=True),
                sempe, 1)
            V["ef"] = sempe.n
            wait("vector", sempe, V["ef"])
            if V["hiloa"]:
                wait("vector", sema, V["hiloa"])
            emit("vector", lambda e, c=c, P=P, psc=psc: e.scalar_tensor_tensor(
                P["t64b"], psc[0:64, 0:64], -1.0, diagG_s(c), ALU.mult, ALU.add),
                P["bEF"].sem, 1)
            ef_v = semv.n
            wait("scalar", semv, ef_v)
            emit("scalar", lambda e, P=P, psc=psc: e.activation(
                P["t64c"], psc[0:64, 64:128], AF.Copy), P["bEF"].sem, 1)
            hilo(P["t64b"], ET1h_s(c), ET1l_s(c), P["t64a"])
            hilo(P["t64c"], F1h_s(c), F1l_s(c), P["t64a"])
            V["hiloa"] = sema.n
            hiloa1_of[c] = (sema.n, semv.n)

        cs_a = [0]
        cs_v = [0]

        def compose_init(prev_v, prev_a):
            wait("gpsimd", semv, prev_v)
            wait("gpsimd", sema, prev_a)
            emit("gpsimd", lambda e: e.memset(HS[:, :], 0.0), semg, 1)
            emit("gpsimd", lambda e: e.memset(HSh[:, :], 0.0), semg, 1)
            emit("gpsimd", lambda e: e.memset(HSl[:, :], 0.0), semg, 1)
            hs_g = semg.n
            wait("vector", semg, hs_g)
            emit("vector", lambda e: e.tensor_tensor(
                HS[:, 64:128], zeros64[:, 0:64], identf_t[:, :], ALU.add), semv, 1)
            wait("scalar", semg, hs_g)
            emit("scalar", lambda e: e.activation(
                HSh[:, 64:128], identf_t[:, :], AF.Copy), sema, 1)
            cs_v[0] = semv.n
            cs_a[0] = sema.n

        def compose_step(c, ST, bank, ef_a):
            ETh, ETl, Fh, Fl = ST
            psc = pe_bank(bank)
            wait("tensor", semv, cs_v[0])
            wait("tensor", sema, cs_a[0])
            if ef_a:
                wait("tensor", sema, ef_a[0])
                wait("tensor", semv, ef_a[1])
            emit("tensor", lambda e, c=c, psc=psc: e.matmul(
                psc[0:64, 0:128], ETh(c), HSh[:, :], start=True, stop=False),
                sempe, 1)
            emit("tensor", lambda e, c=c, psc=psc: e.matmul(
                psc[0:64, 0:128], ETh(c), HSl[:, :], start=False, stop=False),
                sempe, 1)
            emit("tensor", lambda e, c=c, psc=psc: e.matmul(
                psc[0:64, 0:128], ETl(c), HSh[:, :], start=False, stop=False),
                sempe, 1)
            emit("tensor", lambda e, c=c, psc=psc: e.matmul(
                psc[0:64, 0:64], identb[0:64, 0:64], Fh(c), start=False,
                stop=False), sempe, 1)
            emit("tensor", lambda e, c=c, psc=psc: e.matmul(
                psc[0:64, 0:64], identb[0:64, 0:64], Fl(c), start=False,
                stop=True), sempe, 1)
            cp_pe = sempe.n
            wait("scalar", sempe, cp_pe)
            wait("scalar", semv, cs_v[0])
            emit("scalar", lambda e, psc=psc: e.activation(
                HS[:, :], psc[0:64, 0:128], AF.Copy), sema, 1)
            wown("scalar")
            emit("scalar", lambda e: e.activation(
                HSh[:, :], HS[:, :], AF.Copy), sema, 1)
            wown("scalar")
            emit("scalar", lambda e: e.activation(
                segT[:, :], HSh[:, :], AF.Copy), sema, 1)
            cs_a[0] = sema.n
            wait("vector", sema, cs_a[0])
            emit("vector", lambda e: e.tensor_tensor(
                HSl[:, :], HS[:, :], segT[:, :], ALU.subtract), bank.sem, 1)
            cs_v[0] = semv.n

        def exchange_finish(passno):
            wait("scalar", semv, cs_v[0])
            wown("scalar")
            emit("scalar", lambda e: e.activation(segT[:, 0:64], HS[:, 0:64],
                                                  AF.Copy), sema, 1)
            seg_a = sema.n
            ptt = pe_bank(PTR)[:].bitcast(BF16)
            wait("tensor", sema, seg_a)
            emit("tensor", lambda e, ptt=ptt: e.transpose(
                ptt[0:64, 0:64], HSh[:, 64:128], identb[0:64, 0:64]), sempe, 1)
            emit("tensor", lambda e, ptt=ptt: e.transpose(
                ptt[0:64, 64:128], HSl[:, 64:128], identb[0:64, 0:64]), sempe, 1)
            wait("scalar", sempe)
            emit("scalar", lambda e, ptt=ptt: e.activation(
                tmp64b[:, :], ptt[0:64, 0:64], AF.Copy), sema, 1)
            emit("scalar", lambda e, ptt=ptt: e.activation(
                tmp64c[:, :], ptt[0:64, 64:128], AF.Copy), PTR.sem, 1)
            wait("vector", sema)
            emit("vector", lambda e: e.tensor_tensor(
                segT[:, 64:128], tmp64b[:, :], tmp64c[:, :], ALU.add), semv, 1)
            seg_v = semv.n
            wait("gpsimd", semv, seg_v)
            wait("gpsimd", sema)
            emit("gpsimd", lambda e, p=passno: e.dma_start(
                out=cc_in[p].ap(), in_=segT[:, :]), semcc, 16)
            wait("gpsimd", semcc, semcc.n)
            emit("gpsimd", lambda e, p=passno: e.collective_compute(
                "AllGather", ALU.bypass,
                replica_groups=[list(range(NCORES))],
                ins=[cc_in[p].ap().opt()],
                outs=[cc_out[p].ap().opt()]), semcl, 1)
            wait("gpsimd", semcl, semcl.n)
            emit("gpsimd", lambda e, p=passno: e.dma_start(
                out=gath[:, :].rearrange("p (r f) -> p r f", r=NCORES),
                in_=cc_out[p].ap().rearrange("(r p) f -> p r f", p=DS)), semcc, 16)
            gath_g = semcc.n
            emit("gpsimd", lambda e: e.memset(Hacc[:, :], 0.0), semg, 1)
            emit("gpsimd", lambda e: e.memset(Hah[:, :], 0.0), semg, 1)
            emit("gpsimd", lambda e: e.memset(Hal[:, :], 0.0), semg, 1)
            hz_g = semg.n
            for eng in ("tensor", "vector", "scalar"):
                wait(eng, semg, hz_g)
                wait(eng, semcc, gath_g)
            for r in range(NCORES):
                wait("scalar", semv)
                wown("scalar")
                emit("scalar", lambda e, r=r: e.activation(
                    HSh[:, :], gath[:, r * 128:(r + 1) * 128], AF.Copy), sema, 1)
                wown("scalar")
                emit("scalar", lambda e: e.activation(
                    segT[:, :], HSh[:, :], AF.Copy), sema, 1)
                wait("vector", sema)
                emit("vector", lambda e, r=r: e.tensor_tensor(
                    HSl[:, :], gath[:, r * 128:(r + 1) * 128], segT[:, :],
                    ALU.subtract), semv, 1)
                sp_v = semv.n
                psc = pe_bank(PC)
                wait("tensor", semv, sp_v)
                emit("tensor", lambda e, psc=psc: e.matmul(
                    psc[:, 0:64], HSh[:, 64:128], Hah[:, :], start=True,
                    stop=False), sempe, 1)
                emit("tensor", lambda e, psc=psc: e.matmul(
                    psc[:, 0:64], HSh[:, 64:128], Hal[:, :], start=False,
                    stop=False), sempe, 1)
                emit("tensor", lambda e, psc=psc: e.matmul(
                    psc[:, 0:64], HSl[:, 64:128], Hah[:, :], start=False,
                    stop=False), sempe, 1)
                emit("tensor", lambda e, psc=psc: e.matmul(
                    psc[:, 0:64], identb[0:64, 0:64], HSh[:, 0:64], start=False,
                    stop=False), sempe, 1)
                emit("tensor", lambda e, psc=psc: e.matmul(
                    psc[:, 0:64], identb[0:64, 0:64], HSl[:, 0:64], start=False,
                    stop=True), sempe, 1)
                wown("vector")
                emit("vector", lambda e, r=r: e.tensor_scalar(
                    tmp64a[:, :], Hacc[:, :], oasel[:, r:r + 1], None, ALU.mult),
                    semv, 1)
                wait("vector", sempe)
                wown("vector")
                emit("vector", lambda e, r=r, psc=psc: e.scalar_tensor_tensor(
                    Hacc[:, :], psc[:, 0:64], asel[:, r:r + 1], tmp64a[:, :],
                    ALU.mult, ALU.add), PC.sem, 1)
                wait("scalar", semv)
                wown("scalar")
                emit("scalar", lambda e: e.activation(Hah[:, :], Hacc[:, :],
                                                      AF.Copy), sema, 1)
                wown("scalar")
                emit("scalar", lambda e: e.activation(tmp64b[:, :], Hah[:, :],
                                                      AF.Copy), sema, 1)
                wait("vector", sema)
                wown("vector")
                emit("vector", lambda e: e.tensor_tensor(
                    Hal[:, :], Hacc[:, :], tmp64b[:, :], ALU.subtract), semv, 1)
                wait("tensor", semv, semv.n)
                wait("tensor", sema, sema.n)
            return semv.n

        hiloa1_of = {}
        hiloa2_of = {}
        compose_init(semv.n, sema.n)
        for k in range(NCH // 2):
            cs = [(2 * k, PP[0]), (2 * k + 1, PP[1])]
            r1 = [prep_s1(c, P, pv[P["ptlo"]]) for c, P in cs]
            prep_s2pair(cs, [r[0] for r in r1], [r[1] for r in r1])
            bvs = [prep_s3(c, P, pv[P["ptlo"]], m1v)
                   for (c, P), (m1v, _) in zip(cs, r1)]
            bn_a = prep_s4pair(cs, bvs)
            for c, P in cs:
                prep_s4b(c, P, bn_a)
            b2a = [prep_s5(c, P) for c, P in cs]
            r1a = [prep_s6(c, P, a) for (c, P), a in zip(cs, b2a)]
            t1a = [prep_s7(c, P, a) for (c, P), a in zip(cs, r1a)]
            for (c, P), a in zip(cs, t1a):
                prep_s8(c, P, a)
            if k >= 1:
                compose_step(2 * (k - 1), ET1_s, PC, hiloa1_of[2 * (k - 1)])
                compose_step(2 * (k - 1) + 1, ET1_s, PC,
                             hiloa1_of[2 * (k - 1) + 1])
        compose_step(NCH - 2, ET1_s, PC, hiloa1_of[NCH - 2])
        compose_step(NCH - 1, ET1_s, PC, hiloa1_of[NCH - 1])

        for tname, src_ap in (("ktil", Ktil_s[:, :]), ("bt", AR[:, 0:2048]),
                              ("t1n", AR[:, 2048:4096]),
                              ("et1h", AR[0:64, 19456:20480]),
                              ("f1h", AR[0:64, 21504:22528]),
                              ("vtmf", ARF[:, 7680:8704]), ("m1", M1_t[:, :])):
            if tname in taps:
                wait("sync", semv, semv.n)
                wait("sync", sema, sema.n)
                wait("sync", sempe, sempe.n)
                emit("sync", lambda e, t=tname, s=src_ap: e.dma_start(
                    out=tap_aps[t], in_=s), semst, 16)
        if phases < 1.6:
            _replay(nc, q, semst, semcc)
            return nc
        hs1_v = exchange_finish(0)
        if "seg1" in taps:
            wait("sync", sema, sema.n)
            emit("sync", lambda e: e.dma_start(out=tap_aps["seg1"], in_=segT[:]),
                 semst, 16)
        if "gath1" in taps:
            wait("sync", semcc, semcc.n)
            emit("sync", lambda e: e.dma_start(out=tap_aps["gath1"], in_=gath[:]),
                 semst, 16)
        if "hstart1" in taps:
            wait("sync", semv, hs1_v)
            emit("sync", lambda e: e.dma_start(out=tap_aps["hstart1"], in_=Hacc[:]),
                 semst, 16)

        slot_v = [0] * NCH
        last_u_rd = [0, 0]
        last_db_rd = [0, 0]
        last_hilo_a = [0, 0]
        last_dlt_rd = [0, 0]
        last_xn_rd = [0, 0]
        last_t2_rd = [0, 0]
        last_wb2_rd = [0, 0]
        last_gt_rd = [0, 0]
        sig_a = [0] * NCH
        pred_pe_of = [0] * NCH
        ttr_v2 = [0] * NCH
        rstd_a2 = [0] * NCH
        last_ynb_rd = [0, 0]
        last_yn1_rd = [0, 0]

        def spine_seed(hs_v):
            wait("scalar", semv, hs_v)
            emit("scalar", lambda e: e.activation(H0b_s(0), Hacc[:, :], AF.Copy),
                 sema, 1)
            wown("scalar")
            emit("scalar", lambda e: e.activation(stmp, H0b_s(0), AF.Copy), sema, 1)
            h_a = sema.n
            wait("vector", sema, h_a)
            emit("vector", lambda e: e.tensor_tensor(
                H0l_s(0), Hacc[:, :], stmp, ALU.subtract), semv, 1)
            slot_v[0] = semv.n

        def spine_step(c, ST):
            ETh, ETl, Fh, Fl = ST
            psc = pe_bank(PC)
            wait("tensor", semv, slot_v[c - 1])
            emit("tensor", lambda e, c=c, psc=psc: e.matmul(
                psc[:, 0:64], ETh(c - 1), H0b_s(c - 1), start=True, stop=False),
                sempe, 1)
            emit("tensor", lambda e, c=c, psc=psc: e.matmul(
                psc[:, 0:64], ETh(c - 1), H0l_s(c - 1), start=False, stop=False),
                sempe, 1)
            emit("tensor", lambda e, c=c, psc=psc: e.matmul(
                psc[:, 0:64], ETl(c - 1), H0b_s(c - 1), start=False, stop=False),
                sempe, 1)
            emit("tensor", lambda e, c=c, psc=psc: e.matmul(
                psc[:, 0:64], identb[0:64, 0:64], Fh(c - 1), start=False,
                stop=False), sempe, 1)
            emit("tensor", lambda e, c=c, psc=psc: e.matmul(
                psc[:, 0:64], identb[0:64, 0:64], Fl(c - 1), start=False,
                stop=True), sempe, 1)
            adv_pe = sempe.n
            wait("scalar", sempe, adv_pe)
            emit("scalar", lambda e, c=c, psc=psc: e.activation(
                H0b_s(c), psc[:, 0:64], AF.Copy), sema, 1)
            wown("scalar")
            emit("scalar", lambda e, c=c: e.activation(stmp, H0b_s(c), AF.Copy),
                 sema, 1)
            h_a = sema.n
            wait("vector", sema, h_a)
            emit("vector", lambda e, c=c, psc=psc: e.tensor_tensor(
                H0l_s(c), psc[:, 0:64], stmp, ALU.subtract), PC.sem, 1)
            slot_v[c] = semv.n

        def tail_DU(c, Tt_s):
            b = c % 2
            psd = pe_bank(PS0)
            wait("tensor", semv, slot_v[c])
            emit("tensor", lambda e, c=c, psd=psd: e.matmul(
                psd[:, 0:64], csl(Ktil_s, c), H0b_s(c), start=True, stop=False),
                sempe, 1)
            emit("tensor", lambda e, c=c, psd=psd: e.matmul(
                psd[:, 0:64], csl(Ktil_s, c), H0l_s(c), start=False, stop=True),
                sempe, 1)
            d_pe = sempe.n
            wait("vector", sempe, d_pe)
            if last_db_rd[b]:
                wait("vector", sempe, last_db_rd[b])
            emit("vector", lambda e, c=c, b=b, psd=psd: e.scalar_tensor_tensor(
                Db_x[b], psd[:, 0:64], -1.0, Vtmf_s(c), ALU.mult, ALU.add),
                PS0.sem, 1)
            d_v = semv.n
            psu = pe_bank(PS1)
            wait("tensor", semv, d_v)
            emit("tensor", lambda e, c=c, b=b, psu=psu: e.matmul(
                psu[:, 0:64], Tt_s(c), Db_x[b], start=True, stop=True), sempe, 1)
            u_pe = sempe.n
            last_db_rd[b] = u_pe
            wait("scalar", sempe, u_pe)
            if last_u_rd[b]:
                wait("scalar", sempe, last_u_rd[b])
            emit("scalar", lambda e, b=b, psu=psu: e.activation(
                Ub_x[b], psu[:, 0:64], AF.Copy), PS1.sem, 1)
            return sema.n

        def tail_pred(c):
            b = c % 2
            u_a = tail_DU(c, T1t_s)
            psp = pe_bank(PS2, 1)
            wait("tensor", sema, u_a)
            emit("tensor", lambda e, c=c, psp=psp: e.matmul(
                psp[:, 0:64], csl(Qtil_s, c), H0b_s(c), start=True, stop=False),
                sempe, 1)
            emit("tensor", lambda e, c=c, psp=psp: e.matmul(
                psp[:, 0:64], csl(Qtil_s, c), H0l_s(c), start=False, stop=False),
                sempe, 1)
            emit("tensor", lambda e, c=c, b=b, psp=psp: e.matmul(
                psp[:, 0:64], P1m_s(c), Ub_x[b], start=False, stop=True), sempe, 1)
            p_pe = sempe.n
            last_u_rd[b] = p_pe
            pred_pe_of[c] = p_pe
            wait("vector", sempe, p_pe)
            emit("vector", lambda e, c=c, b=b, psp=psp: e.scalar_tensor_tensor(
                ef_x[b], psp[:, 0:64], -1.0, Vtmf_s(c), ALU.mult, ALU.add),
                PS2.sem, 1)
            wown("vector")
            emit("vector", lambda e, b=b: e.tensor_tensor(
                ysq[:, :], ef_x[b], ef_x[b], ALU.mult), semv, 1)
            if c >= 2 and sig_a[c - 2]:
                wait("vector", sema, sig_a[c - 2])
            wown("vector")
            emit("vector", lambda e, b=b: e.tensor_reduce(
                err_x[b], ysq[:, :], mybir.AxisListType.X, ALU.add), semv, 1)
            err_v = semv.n
            wait("scalar", semv, err_v)
            emit("scalar", lambda e, c=c, b=b: e.activation(
                scol[:, c:c + 1], err_x[b], AF.Sigmoid, scale=1.0 / STEMP),
                sema, 1)
            sig_a[c] = sema.n

        def prep2(c):
            b = c % 2
            wait("vector", sema, sig_a[c])
            emit("vector", lambda e, c=c, b=b: e.tensor_scalar(
                sm1_x[b], scol[:, c:c + 1], 1.0, None, ALU.subtract), semv, 1)
            if last_dlt_rd[b]:
                wait("vector", sempe, last_dlt_rd[b])
            wown("vector")
            emit("vector", lambda e, c=c, b=b: e.tensor_scalar(
                Dlt_x[b], Bt_s(c), sm1_x[b], None, ALU.mult), semv, 1)
            if last_wb2_rd[b]:
                wait("vector", sempe, last_wb2_rd[b])
            emit("vector", lambda e, c=c, b=b: e.tensor_scalar(
                Wb2_x[b], Wbtm_s(c), scol[:, c:c + 1], None, ALU.mult), semv, 1)
            wait("vector", sempe, pred_pe_of[c])
            emit("vector", lambda e, c=c: e.tensor_scalar(
                P1m_s(c), Ph1m_s(c), scol[:, c:c + 1], None, ALU.mult), semv, 1)
            dl_v = semv.n
            psx = pe_bank(PS0, 1)
            wait("tensor", semv, dl_v)
            emit("tensor", lambda e, c=c, b=b, psx=psx: e.matmul(
                psx[:, 0:128], Dlt_x[b], T1n_s(c), start=True, stop=True),
                sempe, 1)
            x_pe = sempe.n
            last_dlt_rd[b] = x_pe
            wait("scalar", sempe, x_pe)
            if last_xn_rd[b]:
                wait("scalar", sempe, last_xn_rd[b])
            emit("scalar", lambda e, b=b, psx=psx: e.activation(
                Xn_x[b], psx[:, 0:128], AF.Copy), PS0.sem, 1)
            x_a = sema.n
            pst = pe_bank(PS1, 1)
            wait("tensor", sema, x_a)
            emit("tensor", lambda e, c=c, b=b, pst=pst: e.matmul(
                pst[:, 0:128], T1t_s(c), Xn_x[b], start=True, stop=False), sempe, 1)
            emit("tensor", lambda e, c=c, pst=pst: e.matmul(
                pst[:, 0:128], identb[:, :], T1n_s(c), start=False, stop=True),
                sempe, 1)
            t2_pe = sempe.n
            last_xn_rd[b] = t2_pe
            wait("scalar", sempe, t2_pe)
            if last_t2_rd[b]:
                wait("scalar", sempe, last_t2_rd[b])
            emit("scalar", lambda e, b=b, pst=pst: e.activation(
                T2n_x[b], pst[:, 0:128], AF.Copy), PS1.sem, 1)
            t2n_a = sema.n
            pt2 = pe_bank(PTR, 1)[:].bitcast(BF16)
            wait("tensor", sema, t2n_a)
            emit("tensor", lambda e, b=b, pt2=pt2: e.transpose(
                pt2[:, 0:128], T2n_x[b], identb[:, :]), sempe, 1)
            wait("scalar", sempe)
            emit("scalar", lambda e, c=c, pt2=pt2: e.activation(
                T2t_s(c), pt2[:, 0:128], AF.Copy), PTR.sem, 1)
            psg = pe_bank(PS2, 1)
            emit("tensor", lambda e, b=b, psg=psg: e.matmul(
                psg[:, 0:64], T2n_x[b], Wb2_x[b], start=True, stop=True), sempe, 1)
            g_pe = sempe.n
            last_t2_rd[b] = g_pe
            last_wb2_rd[b] = g_pe
            wait("scalar", sempe, g_pe)
            if last_gt_rd[b]:
                wait("scalar", sempe, last_gt_rd[b])
            emit("scalar", lambda e, b=b, psg=psg: e.activation(
                Gt_x[b], psg[:, 0:64], AF.Copy), PS2.sem, 1)
            g_a = sema.n
            psc2 = pe_bank(PB0, 2)
            wait("tensor", sema, g_a)
            emit("tensor", lambda e, c=c, b=b, psc2=psc2: e.matmul(
                psc2[0:64, 0:64], Ktm_s(c), Gt_x[b], start=True, stop=False),
                sempe, 1)
            emit("tensor", lambda e, c=c, b=b, psc2=psc2: e.matmul(
                psc2[0:64, 64:128], Gt_x[b], Vtmb_s(c), start=False, stop=True),
                sempe, 1)
            ef_pe = sempe.n
            last_gt_rd[b] = ef_pe
            wait("vector", sempe, ef_pe)
            if last_hilo_a[b]:
                wait("vector", sema, last_hilo_a[b])
            emit("vector", lambda e, c=c, b=b, psc2=psc2: e.scalar_tensor_tensor(
                t64b_x[b], psc2[0:64, 0:64], -1.0, diagG_s(c), ALU.mult, ALU.add),
                PB0.sem, 1)
            emit("vector", lambda e, b=b, psc2=psc2: e.tensor_copy(
                t64c_x[b], psc2[0:64, 64:128]), PB0.sem, 1)
            hilo(t64b_x[b], ET2h_s(c), ET2l_s(c), t64a_x[b])
            hilo(t64c_x[b], F2h_s(c), F2l_s(c), t64a_x[b])
            last_hilo_a[b] = sema.n
            hiloa2_of[c] = (sema.n, semv.n)

        def tail_y(c):
            b = c % 2
            u_a = tail_DU(c, T2t_s)
            psy = pe_bank(PS2, 1)
            wait("tensor", sema, u_a)
            emit("tensor", lambda e, c=c, psy=psy: e.matmul(
                psy[:, 0:64], csl(Qhat_s, c), H0b_s(c), start=True, stop=False),
                sempe, 1)
            emit("tensor", lambda e, c=c, psy=psy: e.matmul(
                psy[:, 0:64], csl(Qhat_s, c), H0l_s(c), start=False, stop=False),
                sempe, 1)
            emit("tensor", lambda e, c=c, b=b, psy=psy: e.matmul(
                psy[:, 0:64], P1m_s(c), Ub_x[b], start=False, stop=True), sempe, 1)
            y_pe = sempe.n
            last_u_rd[b] = y_pe
            wait("scalar", sempe, y_pe)
            if c >= 2 and ttr_v2[c - 2]:
                wait("scalar", semv, ttr_v2[c - 2])
            if last_yn1_rd[b]:
                wait("scalar", semv, last_yn1_rd[b])
            emit("scalar", lambda e, b=b, psy=psy: e.activation(
                ytm_x[b], psy[:, 0:64], AF.Copy), PS2.sem, 1)
            ytm_a = sema.n
            if "ytm" in taps:
                wait("sync", sema, ytm_a)
                emit("sync", lambda e, c=c, b=b: e.dma_start(
                    out=tap_aps["ytm"][:, c * DS:(c + 1) * DS], in_=ytm_x[b]),
                    semst, 16)
            wait("vector", sema, ytm_a)
            wown("vector")
            emit("vector", lambda e, b=b: e.tensor_tensor(
                ysq[:, :], ytm_x[b], ytm_x[b], ALU.mult), semv, 1)
            wown("vector")
            emit("vector", lambda e, b=b: e.tensor_reduce(
                ms_x[b], ysq[:, :], mybir.AxisListType.X, ALU.add), semv, 1)
            ttr_v2[c] = semv.n
            wait("scalar", semv, ttr_v2[c])
            emit("scalar", lambda e, b=b: e.activation(
                rs_x[b], ms_x[b], AF.Ln, scale=1.0 / DS, bias=cbias[:, 2:3]),
                sema, 1)
            wown("scalar")
            emit("scalar", lambda e, b=b: e.activation(
                rs_x[b], rs_x[b], AF.Exp, scale=-0.5), sema, 1)
            wown("scalar")
            emit("scalar", lambda e, b=b: e.activation(
                yn1_x[b], ytm_x[b], AF.Copy, scale=rs_x[b]), sema, 1)
            rstd_a2[c] = sema.n
            wait("vector", sema, rstd_a2[c])
            if last_ynb_rd[b]:
                wait("vector", sempe, last_ynb_rd[b])
            emit("vector", lambda e, b=b: e.tensor_tensor(
                ynb_x[b], yn1_x[b], normw_t[:, :], ALU.mult), semv, 1)
            ynb_v = semv.n
            last_yn1_rd[b] = ynb_v
            pty = pe_bank(PTR, 1)[:].bitcast(BF16)
            wait("tensor", semv, ynb_v)
            emit("tensor", lambda e, b=b, pty=pty: e.transpose(
                pty[0:64, 0:128], ynb_x[b], identb[:, :]), sempe, 1)
            ytr = sempe.n
            last_ynb_rd[b] = ytr
            wait("scalar", sempe, ytr)
            emit("scalar", lambda e, c=c, pty=pty: e.activation(
                ynfm[:, c * L:(c + 1) * L], pty[0:64, 0:128], AF.Copy), PTR.sem, 1)

        if phases < 1.9:
            _replay(nc, q, semst, semcc)
            return nc
        wait("scalar", semw2b[0], semw2b[0].n)
        wait("scalar", semw2b[1], semw2b[1].n)
        wait("vector", semw2b[0], semw2b[0].n)
        wait("vector", semw2b[1], semw2b[1].n)
        d_v1 = [0] * NCH
        u_a1 = [0] * NCH
        dl_v1 = [0] * NCH
        x_a1 = [0] * NCH
        t2n_a1 = [0] * NCH
        g_a1 = [0] * NCH

        def s1_D(c, Tt_s):
            b = c % 2
            psd = pe_bank(PS0)
            wait("tensor", semv, slot_v[c])
            emit("tensor", lambda e, c=c, psd=psd: e.matmul(
                psd[:, 0:64], csl(Ktil_s, c), H0b_s(c), start=True, stop=False),
                sempe, 1)
            emit("tensor", lambda e, c=c, psd=psd: e.matmul(
                psd[:, 0:64], csl(Ktil_s, c), H0l_s(c), start=False, stop=True),
                sempe, 1)
            d_pe = sempe.n
            wait("vector", sempe, d_pe)
            if last_db_rd[b]:
                wait("vector", sempe, last_db_rd[b])
            emit("vector", lambda e, c=c, b=b, psd=psd: e.scalar_tensor_tensor(
                Db_x[b], psd[:, 0:64], -1.0, Vtmf_s(c), ALU.mult, ALU.add),
                PS0.sem, 1)
            d_v1[c] = semv.n

        def s2_U(c, Tt_s):
            b = c % 2
            psu = pe_bank(PS1)
            wait("tensor", semv, d_v1[c])
            emit("tensor", lambda e, c=c, b=b, psu=psu: e.matmul(
                psu[:, 0:64], Tt_s(c), Db_x[b], start=True, stop=True), sempe, 1)
            u_pe = sempe.n
            last_db_rd[b] = u_pe
            wait("scalar", sempe, u_pe)
            if last_u_rd[b]:
                wait("scalar", sempe, last_u_rd[b])
            emit("scalar", lambda e, b=b, psu=psu: e.activation(
                Ub_x[b], psu[:, 0:64], AF.Copy), PS1.sem, 1)
            u_a1[c] = sema.n

        def s3_pred(c):
            b = c % 2
            psp = pe_bank((PS2, PB1)[b], 1)
            wait("tensor", sema, u_a1[c])
            emit("tensor", lambda e, c=c, psp=psp: e.matmul(
                psp[:, 0:64], csl(Qtil_s, c), H0b_s(c), start=True, stop=False),
                sempe, 1)
            emit("tensor", lambda e, c=c, psp=psp: e.matmul(
                psp[:, 0:64], csl(Qtil_s, c), H0l_s(c), start=False, stop=False),
                sempe, 1)
            emit("tensor", lambda e, c=c, b=b, psp=psp: e.matmul(
                psp[:, 0:64], P1m_s(c), Ub_x[b], start=False, stop=True), sempe, 1)
            p_pe = sempe.n
            last_u_rd[b] = p_pe
            pred_pe_of[c] = p_pe

        def s4_err(c):
            b = c % 2
            psp = (PS2, PB1)[b].t
            wait("vector", sempe, pred_pe_of[c])
            emit("vector", lambda e, c=c, b=b, psp=psp: e.scalar_tensor_tensor(
                ef_x[b], psp[:, 0:64], -1.0, Vtmf_s(c), ALU.mult, ALU.add),
                (PS2, PB1)[b].sem, 1)
            wown("vector")
            emit("vector", lambda e, b=b: e.tensor_tensor(
                ysq[:, :], ef_x[b], ef_x[b], ALU.mult), semv, 1)
            if c >= 2 and sig_a[c - 2]:
                wait("vector", sema, sig_a[c - 2])
            wown("vector")
            emit("vector", lambda e, b=b: e.tensor_reduce(
                err_x[b], ysq[:, :], mybir.AxisListType.X, ALU.add), semv, 1)
            err_v = semv.n
            wait("scalar", semv, err_v)
            emit("scalar", lambda e, c=c, b=b: e.activation(
                scol[:, c:c + 1], err_x[b], AF.Sigmoid, scale=1.0 / STEMP),
                sema, 1)
            sig_a[c] = sema.n

        def s5_dl(c):
            b = c % 2
            wait("vector", sema, sig_a[c])
            emit("vector", lambda e, c=c, b=b: e.tensor_scalar(
                sm1_x[b], scol[:, c:c + 1], 1.0, None, ALU.subtract), semv, 1)
            if last_dlt_rd[b]:
                wait("vector", sempe, last_dlt_rd[b])
            wown("vector")
            emit("vector", lambda e, c=c, b=b: e.tensor_scalar(
                Dlt_x[b], Bt_s(c), sm1_x[b], None, ALU.mult), semv, 1)
            wait("vector", sempe, pred_pe_of[c])
            emit("vector", lambda e, c=c: e.tensor_scalar(
                P1m_s(c), Ph1m_s(c), scol[:, c:c + 1], None, ALU.mult), semv, 1)
            dl_v1[c] = semv.n

        def s6_X(c):
            b = c % 2
            psx = pe_bank(PB0, 1)
            wait("tensor", semv, dl_v1[c])
            emit("tensor", lambda e, c=c, b=b, psx=psx: e.matmul(
                psx[:, 0:128], Dlt_x[b], T1n_s(c), start=True, stop=True),
                sempe, 1)
            x_pe = sempe.n
            last_dlt_rd[b] = x_pe
            wait("scalar", sempe, x_pe)
            if last_xn_rd[b]:
                wait("scalar", sempe, last_xn_rd[b])
            emit("scalar", lambda e, b=b, psx=psx: e.activation(
                Xn_x[b], psx[:, 0:128], AF.Copy), PB0.sem, 1)
            x_a1[c] = sema.n

        def s7_T2(c):
            b = c % 2
            pst = pe_bank(PS0, 1)
            wait("tensor", sema, x_a1[c])
            emit("tensor", lambda e, c=c, b=b, pst=pst: e.matmul(
                pst[:, 0:128], T1t_s(c), Xn_x[b], start=True, stop=False),
                sempe, 1)
            emit("tensor", lambda e, c=c, pst=pst: e.matmul(
                pst[:, 0:128], identb[:, :], T1n_s(c), start=False, stop=True),
                sempe, 1)
            t2_pe = sempe.n
            last_xn_rd[b] = t2_pe
            wait("scalar", sempe, t2_pe)
            if last_t2_rd[b]:
                wait("scalar", sempe, last_t2_rd[b])
            emit("scalar", lambda e, b=b, pst=pst: e.activation(
                T2n_x[b], pst[:, 0:128], AF.Copy), PS0.sem, 1)
            if last_wb2_rd[b]:
                wait("scalar", sempe, last_wb2_rd[b])
            wait("scalar", sema, sig_a[c])
            emit("scalar", lambda e, c=c, b=b: e.activation(
                Wb2_x[b], Wbtm_s(c), AF.Copy, scale=scol[:, c:c + 1]), sema, 1)
            t2n_a1[c] = sema.n

        def s8_TRG(c):
            b = c % 2
            pt2 = pe_bank(PTR, 1)[:].bitcast(BF16)
            wait("tensor", sema, t2n_a1[c])
            emit("tensor", lambda e, b=b, pt2=pt2: e.transpose(
                pt2[:, 0:128], T2n_x[b], identb[:, :]), sempe, 1)
            wait("scalar", sempe)
            emit("scalar", lambda e, c=c, pt2=pt2: e.activation(
                T2t_s(c), pt2[:, 0:128], AF.Copy), PTR.sem, 1)
            psg = pe_bank(PROW, 1)
            emit("tensor", lambda e, b=b, psg=psg: e.matmul(
                psg[:, 0:64], T2n_x[b], Wb2_x[b], start=True, stop=True), sempe, 1)
            g_pe = sempe.n
            last_t2_rd[b] = g_pe
            last_wb2_rd[b] = g_pe
            wait("scalar", sempe, g_pe)
            if last_gt_rd[b]:
                wait("scalar", sempe, last_gt_rd[b])
            emit("scalar", lambda e, b=b, psg=psg: e.activation(
                Gt_x[b], psg[:, 0:64], AF.Copy), PROW.sem, 1)
            g_a1[c] = sema.n

        def s9_EF(c):
            b = c % 2
            psc2 = pe_bank(PS1, 2)
            wait("tensor", sema, g_a1[c])
            emit("tensor", lambda e, c=c, b=b, psc2=psc2: e.matmul(
                psc2[0:64, 0:64], Ktm_s(c), Gt_x[b], start=True, stop=False),
                sempe, 1)
            emit("tensor", lambda e, c=c, b=b, psc2=psc2: e.matmul(
                psc2[0:64, 64:128], Gt_x[b], Vtmb_s(c), start=False, stop=True),
                sempe, 1)
            ef_pe = sempe.n
            last_gt_rd[b] = ef_pe
            wait("vector", sempe, ef_pe)
            if last_hilo_a[b]:
                wait("vector", sema, last_hilo_a[b])
            emit("vector", lambda e, c=c, b=b, psc2=psc2: e.scalar_tensor_tensor(
                t64b_x[b], psc2[0:64, 0:64], -1.0, diagG_s(c), ALU.mult, ALU.add),
                PS1.sem, 1)
            ef_v = semv.n
            wait("scalar", semv, ef_v)
            emit("scalar", lambda e, b=b, psc2=psc2: e.activation(
                t64c_x[b], psc2[0:64, 64:128], AF.Copy), PS1.sem, 1)
            hilo(t64b_x[b], ET2h_s(c), ET2l_s(c), t64a_x[b])
            hilo(t64c_x[b], F2h_s(c), F2l_s(c), t64a_x[b])
            last_hilo_a[b] = sema.n
            hiloa2_of[c] = (sema.n, semv.n)

        compose_init(semv.n, sema.n)
        spine_seed(hs1_v)
        for i in range(NCH + 9):
            if i + 1 < NCH:
                spine_step(i + 1, ET1_s)
            if i < NCH:
                s1_D(i, T1t_s)
            if 1 <= i < NCH + 1:
                s2_U(i - 1, T1t_s)
            if 2 <= i < NCH + 2:
                s3_pred(i - 2)
            if 3 <= i < NCH + 3:
                s4_err(i - 3)
            if 4 <= i < NCH + 4:
                s5_dl(i - 4)
            if 5 <= i < NCH + 5:
                s6_X(i - 5)
            if 6 <= i < NCH + 6:
                s7_T2(i - 6)
            if 7 <= i < NCH + 7:
                s8_TRG(i - 7)
            if 8 <= i < NCH + 8:
                s9_EF(i - 8)
            if 9 <= i < NCH + 9:
                compose_step(i - 9, ET2_s, PROW, hiloa2_of[i - 9])
        s_a = sema.n
        if "scol" in taps:
            wait("sync", sema, s_a)
            emit("sync", lambda e: e.dma_start(out=tap_aps["scol"], in_=scol[:]),
                 semst, 16)

        if phases < 3:
            _replay(nc, q, semst, semcc)
            return nc
        hs2_v = exchange_finish(1)
        u_a2 = [0] * NCH
        y_pe2 = [0] * NCH
        ytm_a2 = [0] * NCH

        def s2_U2(c):
            b = c % 2
            psu = pe_bank(PS1)
            wait("tensor", semv, d_v1[c])
            emit("tensor", lambda e, c=c, b=b, psu=psu: e.matmul(
                psu[:, 0:64], T2t_s(c), Db_x[b], start=True, stop=True), sempe, 1)
            u_pe = sempe.n
            last_db_rd[b] = u_pe
            wait("scalar", sempe, u_pe)
            if last_u_rd[b]:
                wait("scalar", sempe, last_u_rd[b])
            emit("scalar", lambda e, b=b, psu=psu: e.activation(
                Ub_x[b], psu[:, 0:64], AF.Copy), PS1.sem, 1)
            u_a2[c] = sema.n

        def s3_y(c):
            b = c % 2
            psy = pe_bank((PS2, PB1)[b], 1)
            wait("tensor", sema, u_a2[c])
            emit("tensor", lambda e, c=c, psy=psy: e.matmul(
                psy[:, 0:64], csl(Qhat_s, c), H0b_s(c), start=True, stop=False),
                sempe, 1)
            emit("tensor", lambda e, c=c, psy=psy: e.matmul(
                psy[:, 0:64], csl(Qhat_s, c), H0l_s(c), start=False, stop=False),
                sempe, 1)
            emit("tensor", lambda e, c=c, b=b, psy=psy: e.matmul(
                psy[:, 0:64], P1m_s(c), Ub_x[b], start=False, stop=True), sempe, 1)
            y_pe2[c] = sempe.n
            last_u_rd[b] = y_pe2[c]

        def s4_ms(c):
            b = c % 2
            psy = (PS2, PB1)[b].t
            wait("scalar", sempe, y_pe2[c])
            if c >= 2 and ttr_v2[c - 2]:
                wait("scalar", semv, ttr_v2[c - 2])
            if last_yn1_rd[b]:
                wait("scalar", semv, last_yn1_rd[b])
            if c >= 2 and rstd_a2[c - 2]:
                wait("scalar", sema, rstd_a2[c - 2])
            emit("scalar", lambda e, b=b, psy=psy: e.activation(
                ytm_x[b], psy[:, 0:64], AF.Copy), (PS2, PB1)[b].sem, 1)
            ytm_a2[c] = sema.n
            if "ytm" in taps:
                wait("sync", sema, ytm_a2[c])
                emit("sync", lambda e, c=c, b=b: e.dma_start(
                    out=tap_aps["ytm"][:, c * DS:(c + 1) * DS], in_=ytm_x[b]),
                    semst, 16)
            wait("vector", sema, ytm_a2[c])
            wown("vector")
            emit("vector", lambda e, b=b: e.tensor_tensor(
                ysq[:, :], ytm_x[b], ytm_x[b], ALU.mult), semv, 1)
            wown("vector")
            emit("vector", lambda e, b=b: e.tensor_reduce(
                ms_x[b], ysq[:, :], mybir.AxisListType.X, ALU.add), semv, 1)
            ttr_v2[c] = semv.n

        def s5_rs(c):
            b = c % 2
            wait("scalar", semv, ttr_v2[c])
            emit("scalar", lambda e, b=b: e.activation(
                rs_x[b], ms_x[b], AF.Ln, scale=1.0 / DS, bias=cbias[:, 2:3]),
                sema, 1)
            wown("scalar")
            emit("scalar", lambda e, b=b: e.activation(
                rs_x[b], rs_x[b], AF.Exp, scale=-0.5), sema, 1)
            wown("scalar")
            emit("scalar", lambda e, b=b: e.activation(
                yn1_x[b], ytm_x[b], AF.Copy, scale=rs_x[b]), sema, 1)
            rstd_a2[c] = sema.n

        def s6_out(c):
            b = c % 2
            wait("vector", sema, rstd_a2[c])
            if last_ynb_rd[b]:
                wait("vector", sempe, last_ynb_rd[b])
            emit("vector", lambda e, b=b: e.tensor_tensor(
                ynb_x[b], yn1_x[b], normw_t[:, :], ALU.mult), semv, 1)
            ynb_v = semv.n
            last_yn1_rd[b] = ynb_v
            pty = pe_bank(PTR, 1)[:].bitcast(BF16)
            wait("tensor", semv, ynb_v)
            emit("tensor", lambda e, b=b, pty=pty: e.transpose(
                pty[0:64, 0:128], ynb_x[b], identb[:, :]), sempe, 1)
            ytr = sempe.n
            last_ynb_rd[b] = ytr
            wait("scalar", sempe, ytr)
            emit("scalar", lambda e, c=c, pty=pty: e.activation(
                ynfm[:, c * L:(c + 1) * L], pty[0:64, 0:128], AF.Copy), PTR.sem, 1)

        spine_seed(hs2_v)
        for i in range(NCH + 6):
            if i + 1 < NCH:
                spine_step(i + 1, ET2_s)
            if i < NCH:
                s1_D(i, T2t_s)
            if 1 <= i < NCH + 1:
                s2_U2(i - 1)
            if 2 <= i < NCH + 2:
                s3_y(i - 2)
            if 3 <= i < NCH + 3:
                s4_ms(i - 3)
            if 4 <= i < NCH + 4:
                s5_rs(i - 4)
            if 5 <= i < NCH + 5:
                s6_out(i - 5)
        scan_a = sema.n
        scan_v = semv.n
        scan_pe = sempe.n
        if "ynfm" in taps:
            wait("sync", sema, scan_a)
            emit("sync", lambda e: e.dma_start(out=tap_aps["ynfm"], in_=ynfm[:]),
                 semst, 16)

        if phases < 4:
            _replay(nc, q, semst, semcc)
            return nc
        for eng in ("tensor", "vector", "scalar", "gpsimd", "sync"):
            wait(eng, sema, scan_a)
            wait(eng, semv, scan_v)
            wait(eng, sempe, scan_pe)

        w2_pe = {}
        xr_tt = {}
        ob_last = [0, 0, 0]
        cast_a_q = [0] * 4
        x2n_rd_pe = [0] * 4
        mrs2_rd_pe = [0] * 4
        bcast_pe_q = [0] * 4
        stats_fin_v = [0] * 4
        rs2hl_q = [0] * 4
        FQ = 4
        x2q = lambda qq, o: ARF[:, (qq % 2) * 8192 + o * 512: (qq % 2) * 8192
                                + o * 512 + 512]
        x2nq = lambda qq, o: SCRB[:, (qq % 2) * 8192 + o * 512: (qq % 2) * 8192
                                  + o * 512 + 512]

        def attn_stats_tile(qq, o, ps_sx, ps_sq):
            t0 = qq * 512
            idx = qq * NDCH + o
            bank = (PB0, PB1, PTR)[idx % 3]
            ps = pe_bank(bank, 1)
            emit("tensor", lambda e, o=o, t0=t0, ps=ps: e.matmul(
                ps[:, :], wo_b[:, o * 128:(o + 1) * 128],
                ynfm[:, t0: t0 + 512], start=True, stop=True), sempe, 1)
            mm_pe = sempe.n
            rot = idx % 4
            xq_eng = "sync"
            if idx >= 4:
                wait(xq_eng, semv, xr_tt[idx - 4])
            emit(xq_eng, lambda e, o=o, t0=t0, rot=rot: e.dma_start(
                out=xrl[rot],
                in_=xf[o * 128:(o + 1) * 128, t0: t0 + 512]), semxr[rot], 16)
            xr_ld = semxr[rot].n
            wait("vector", sempe, mm_pe)
            wait("vector", semxr[rot], xr_ld)
            if qq >= 2 and o == 0:
                wait("vector", sema, cast_a_q[qq - 2])
                wown("vector")
            emit("vector", lambda e, qq=qq, o=o, rot=rot, ps=ps:
                 e.scalar_tensor_tensor(
                x2q(qq, o), ps[:, :], 1.0, xrl[rot], ALU.mult, ALU.add),
                bank.sem, 1)
            xr_tt[idx] = semv.n
            rot2 = idx % 2
            wait("scalar", semv, xr_tt[idx])
            if idx >= 2:
                wait("scalar", sempe, w2_pe.get(("cast", idx - 2), 0))
            emit("scalar", lambda e, qq=qq, o=o, rot2=rot2: e.activation(
                x2b_t[rot2], x2q(qq, o), AF.Copy), sema, 1)
            emit("scalar", lambda e, qq=qq, o=o, rot2=rot2: e.activation(
                x2sq_t[rot2], x2q(qq, o), AF.Square), sema, 1)
            w2_pe[("casta", idx)] = sema.n
            if o == NDCH - 1:
                cast_a_q[qq] = sema.n

        def stats_mms(qq, o, ps_sx, ps_sq):
            idx = qq * NDCH + o
            rot2 = idx % 2
            wait("tensor", sema, w2_pe[("casta", idx)])
            emit("tensor", lambda e, o=o, rot2=rot2, ps_sx=ps_sx: e.matmul(
                ps_sx[0:1, 0:512], onescol_b[:], x2b_t[rot2],
                start=(o == 0), stop=(o == NDCH - 1)), sempe, 1)
            emit("tensor", lambda e, o=o, rot2=rot2, ps_sq=ps_sq: e.matmul(
                ps_sq[0:1, 0:512], onescol_b[:], x2sq_t[rot2],
                start=(o == 0), stop=(o == NDCH - 1)), sempe, 1)
            w2_pe[("cast", idx)] = sempe.n

        def stats_finish(qq, ps_sx, ps_sq):
            st_pe = w2_pe[("cast", qq * NDCH + NDCH - 1)]
            wait("scalar", sempe, st_pe)
            if qq >= 1:
                wait("scalar", semv, rs2hl_q[qq - 1])
            emit("scalar", lambda e, ps_sx=ps_sx: e.activation(
                rowA[0:1, 0:512], ps_sx[0:1, 0:512], AF.Copy), PROW.sem, 1)
            wown("scalar")
            emit("scalar", lambda e: e.activation(
                rowA[0:1, 2048:2560], rowA[0:1, 0:512], AF.Square,
                scale=1.0 / D), sema, 1)
            m2_a = sema.n
            wait("vector", sema, m2_a)
            wait("vector", sempe, st_pe)
            emit("vector", lambda e, ps_sq=ps_sq: e.scalar_tensor_tensor(
                rowA[0:1, 2048:2560], ps_sq[0:1, 0:512], 1.0 / D,
                rowA[0:1, 2048:2560], ALU.mult, ALU.subtract), PC.sem, 1)
            var_v = semv.n
            wait("scalar", semv, var_v)
            emit("scalar", lambda e: e.activation(
                rowA[0:1, 2560:3072], rowA[0:1, 2048:2560], AF.Ln,
                bias=cbias[0:1, 1:2]), sema, 1)
            wown("scalar")
            emit("scalar", lambda e: e.activation(
                rowA[0:1, 2560:3072], rowA[0:1, 2560:3072], AF.Exp,
                scale=-0.5), sema, 1)
            rs2_a = sema.n
            wait("vector", sema, rs2_a)
            if qq >= 1 and mrs2_rd_pe[qq - 1]:
                wait("vector", sempe, mrs2_rd_pe[qq - 1])
            emit("vector", lambda e: e.tensor_tensor(
                rowC[0:1, 2048:2560], rowA[0:1, 0:512],
                rowA[0:1, 2560:3072], ALU.mult), semv, 1)
            if qq >= 1 and bcast_pe_q[qq - 1]:
                wait("scalar", sempe, bcast_pe_q[qq - 1])
            wown("scalar")
            emit("scalar", lambda e: e.activation(
                rowC[0:1, 3072:3584], rowA[0:1, 2560:3072], AF.Copy), sema, 1)
            wown("scalar")
            emit("scalar", lambda e: e.activation(
                rowA[0:1, 512:1024], rowC[0:1, 3072:3584], AF.Copy), sema, 1)
            wait("vector", sema)
            emit("vector", lambda e: e.tensor_tensor(
                rowC[0:1, 3584:4096], rowA[0:1, 2560:3072],
                rowA[0:1, 512:1024], ALU.subtract), semv, 1)
            rs2hl_q[qq] = semv.n
            psb = pe_bank(PTR, 1)
            wait("tensor", semv, rs2hl_q[qq])
            wait("tensor", sema)
            emit("tensor", lambda e, psb=psb: e.matmul(
                psb[:, 0:512], onesrow_b[0:1, :], rowC[0:1, 3072:3584],
                start=True, stop=False), sempe, 1)
            emit("tensor", lambda e, psb=psb: e.matmul(
                psb[:, 0:512], onesrow_b[0:1, :], rowC[0:1, 3584:4096],
                start=False, stop=True), sempe, 1)
            bcast_pe_q[qq] = sempe.n
            wait("vector", sempe, bcast_pe_q[qq])
            emit("vector", lambda e, psb=psb: e.tensor_copy(
                rs2b[:, :], psb[:, 0:512]), PTR.sem, 1)
            if qq >= 2 and x2n_rd_pe[qq - 2]:
                wait("vector", sempe, x2n_rd_pe[qq - 2])
            wown("vector")
            for o in range(NDCH):
                emit("vector", lambda e, qq=qq, o=o: e.tensor_tensor(
                    x2nq(qq, o), x2q(qq, o), rs2b[:, :], ALU.mult), semv, 1)
            stats_fin_v[qq] = semv.n

        def flush_store():
            if pending_store:
                o_v, sq, so, sj, sov = pending_store.pop(0)
                wait("sync", semv, o_v)
                emit("sync", lambda e, sq=sq, so=so, sov=sov: e.dma_start(
                    out=outfm[so * 128:(so + 1) * 128,
                              sq * 512:(sq + 1) * 512],
                    in_=sov[:, :]), semo[sj], 16)
                ob_last[sj] = semo[sj].n

        def ffn_tile(qq, o):
            flush_store()
            t = qq * NDCH + o
            wt = w2t[t % 2]
            if t == 0:
                wait("gpsimd", semw2b[0], semw2b[0].n)
                wait("gpsimd", semw2b[1], semw2b[1].n)
            if t >= 1:
                prev = w2_pe.get(("ffn", t - 2))
                if prev:
                    wait("gpsimd", sempe, prev)
            emit("gpsimd", lambda e, o=o, wt=wt: e.dma_start(
                out=wt[:], in_=w2bf[o * 128:(o + 1) * 128, :]), semw2[t % 2], 16)
            wld = semw2[t % 2].n
            wsem = semw2[t % 2]
            nsw_a = sema.n
            if qq == 0:
                psn = pe_bank(PTR, 1)
                wait("tensor", wsem, wld)
                for i in range(NDCH):
                    emit("tensor", lambda e, i=i, wt=wt, psn=psn: e.matmul(
                        psn[0:1, 0:128], onescol_b[:],
                        wt[:, i * 128:(i + 1) * 128],
                        start=(i == 0), stop=(i == NDCH - 1)), sempe, 1)
                wait("scalar", sempe)
                emit("scalar", lambda e, o=o, psn=psn: e.activation(
                    rowC[0:1, o * 128:(o + 1) * 128], psn[0:1, 0:128],
                    AF.Copy, scale=-1.0 / D), PTR.sem, 1)
                nsw_a = sema.n
            bank3 = (PS0, PS1, PS2)[t % 3]
            ps = pe_bank(bank3, 1)
            wait("tensor", wsem, wld)
            wait("tensor", semv, stats_fin_v[qq])
            if qq == 0:
                wait("tensor", sema, nsw_a)
            for d in range(NDCH):
                emit("tensor", lambda e, qq=qq, d=d, wt=wt, ps=ps: e.matmul(
                    ps[:, :], wt[:, d * 128:(d + 1) * 128], x2nq(qq, d),
                    start=(d == 0), stop=False), sempe, 1)
            emit("tensor", lambda e, o=o, ps=ps: e.matmul(
                ps[:, :], rowC[0:1, o * 128:(o + 1) * 128],
                rowC[0:1, 2048:2560], start=False, stop=True), sempe, 1)
            ffn_pe = sempe.n
            w2_pe[("ffn", t)] = ffn_pe
            x2n_rd_pe[qq] = ffn_pe
            mrs2_rd_pe[qq] = ffn_pe
            j = t % 3
            ov = out_bufs[j]
            wait("scalar", sempe, ffn_pe)
            if ob_last[j]:
                wait("scalar", semo[j], ob_last[j])
            emit("scalar", lambda e, o=o, ps=ps, ov=ov: e.activation(
                ov[:, :], ps[:, :], AF.Copy, scale=sw2_t[:, o:o + 1]),
                bank3.sem, 1)
            f1_a = sema.n
            wait("vector", sema, f1_a)
            emit("vector", lambda e, qq=qq, o=o, ov=ov: e.scalar_tensor_tensor(
                ov[:, :], ov[:, :], b2_t[:, o:o + 1], x2q(qq, o),
                ALU.add, ALU.add), semv, 1)
            o_v = semv.n
            pending_store.append((o_v, qq, o, j, ov))

        pending_store = []
        for qq in range(FQ + 1):
            ps_sx = ps_sq = None
            if qq < FQ:
                ps_sx = pe_bank(PROW, 1)
                ps_sq = pe_bank(PC, 1)
            for o in range(NDCH):
                if qq < FQ:
                    attn_stats_tile(qq, o, ps_sx, ps_sq)
                if qq >= 1:
                    ffn_tile(qq - 1, o)
                if qq < FQ and o >= 1:
                    stats_mms(qq, o - 1, ps_sx, ps_sq)
            if qq < FQ:
                stats_mms(qq, NDCH - 1, ps_sx, ps_sq)
                stats_finish(qq, ps_sx, ps_sq)
        flush_store()

        for j in range(3):
            wait("gpsimd", semo[j], semo[j].n)
        _replay(nc, q, semst, semcc)

    return nc



def _prep_shared(inputs):
    f32 = np.float32
    Wk = np.asarray(inputs["L0_attn_Wk"], f32)
    Wv = np.asarray(inputs["L0_attn_Wv"], f32)
    Wq = np.asarray(inputs["L0_attn_Wq"], f32)
    Wa = np.asarray(inputs["L0_attn_Wa_w"], f32)
    def tile_lhs(wt):
        F = wt.shape[1]
        return np.ascontiguousarray(
            wt.reshape(NDCH, 128, F).transpose(1, 0, 2).reshape(128, NDCH * F))

    w2w = np.asarray(inputs["L0_ffn_weight"], f32)
    w2h = np.ascontiguousarray(
        w2w.reshape(NDCH, 128, NDCH, 128).transpose(0, 3, 2, 1).reshape(
            NDCH * 128, NDCH * 128))
    shared = {
        "wkq": tile_lhs(np.concatenate([Wk, Wq], 0).T),
        "wav": tile_lhs(np.concatenate([Wa, Wv], 0).T),
        "wo": np.ascontiguousarray(np.asarray(inputs["L0_attn_Wo"], f32).T),
        "w2": w2h,
        "wab": np.ascontiguousarray(
            np.asarray(inputs["L0_attn_Wa_b"], f32).reshape(DS, 1)),
        "lam": np.ascontiguousarray(
            np.asarray(inputs["L0_attn_lam"], f32).reshape(DS, 1)),
        "normw": np.ascontiguousarray(np.broadcast_to(
            np.asarray(inputs["L0_attn_norm_w"], f32), (128, DS))),
        "sw2c": np.ascontiguousarray(
            np.asarray(inputs["L0_ffn_scale_weight"], f32).reshape(NDCH, 128).T),
        "b2c": np.ascontiguousarray(
            np.asarray(inputs["L0_ffn_bias"], f32).reshape(NDCH, 128).T),
        "identf": np.eye(128, dtype=f32),
        "mask_s": np.ascontiguousarray(np.triu(np.ones((128, 128), f32), 1)),
        "mask_i": np.ascontiguousarray(np.triu(np.ones((128, 128), f32), 0)),
    }
    return shared


def kernel(**inputs):
    x = np.asarray(inputs["L0_attn_x"], np.float32)[0]
    shared = _prep_shared(inputs)
    nc = build_program(taps=DEBUG_TAPS)
    in_maps = []
    for c in range(NCORES):
        sel = np.zeros((DS, NCORES), np.float32)
        sel[:, :c] = 1.0
        m = dict(shared)
        m["xf"] = np.ascontiguousarray(x[c * T:(c + 1) * T, :].T)
        m["alphasel"] = sel
        in_maps.append(m)
    res = run_bass_kernel_spmd(nc, in_maps, core_ids=list(range(NCORES)))
    out = np.concatenate(
        [np.asarray(res.results[c]["outfm"]).T for c in range(NCORES)], 0)
    return out.reshape(1, NCORES * T, D)


# version 17 (final)
# speedup vs baseline: 1.1394x; 1.1394x over previous
import contextlib
import numpy as np
import ml_dtypes
import concourse.bass as bass
import concourse.mybir as mybir
from concourse.bass_utils import run_bass_kernel_spmd

F32 = mybir.dt.float32
BF16 = mybir.dt.bfloat16
AF = mybir.ActivationFunctionType
ALU = mybir.AluOpType
BF = ml_dtypes.bfloat16

NCORES = 8
D = 2048
DS = 64
T = 2048
L = 128
NCH = T // L
NDCH = D // 128
C_GATE = 8.0
LN_EPS = 1e-5
RMS_EPS = 1e-6
STEMP = 1.0 + 1e-6

ENGINES = ("sync", "vector", "scalar", "tensor", "gpsimd")

DEBUG_TAPS = ()
OWN_WAIT = False
PREP_CUT = 99
PH4_CUT = 99


class Sem:
    __slots__ = ("h", "n")

    def __init__(self, h):
        self.h = h
        self.n = 0


class Bank:

    __slots__ = ("t", "sem", "pending")

    def __init__(self, t, sem=None):
        self.t = t
        self.sem = sem
        self.pending = {}


def _replay(nc, q, semst, semcc):
    if semst.n > 0:
        q["gpsimd"].append(lambda e, v=semst.n, s=semst: e.wait_ge(s.h, v))
    with nc.Block() as block:
        @block.sync
        def _(e):
            for f in q["sync"]:
                f(e)

        @block.vector
        def _(e):
            for f in q["vector"]:
                f(e)

        @block.scalar
        def _(e):
            for f in q["scalar"]:
                f(e)

        @block.tensor
        def _(e):
            for f in q["tensor"]:
                f(e)

        @block.gpsimd
        def _(e):
            for f in q["gpsimd"]:
                f(e)
    return None


def build_program(taps=(), phases=4):
    nc = bass.Bass("TRN2", target_bir_lowering=False, debug=False)

    def din(name, shape, dtype=F32):
        return nc.dram_tensor(name, list(shape), dtype, kind="ExternalInput").ap()

    def dout(name, shape, dtype=F32):
        return nc.dram_tensor(name, list(shape), dtype, kind="ExternalOutput").ap()

    xf = din("xf", [D, T])
    wkq = din("wkq", [128, NDCH * 128])
    wav = din("wav", [128, NDCH * 128])
    wo = din("wo", [DS, D])
    w2 = din("w2", [NDCH * 128, NDCH * 128])
    wab = din("wab", [DS, 1])
    lam = din("lam", [DS, 1])
    normw = din("normw", [128, DS])
    sw2c = din("sw2c", [128, NDCH])
    b2c = din("b2c", [128, NDCH])
    identf = din("identf", [128, 128])
    mask_s = din("mask_s", [128, 128])
    mask_i = din("mask_i", [128, 128])
    alphasel = din("alphasel", [DS, NCORES])
    outfm = dout("outfm", [D, T])

    tapdefs = {
        "knn": ([DS, T], F32), "qnn": ([DS, T], F32), "vs": ([DS, T], F32),
        "alp": ([DS, T], F32), "lamI": ([DS, T], F32),
        "seg1": ([DS, 128], F32), "gath1": ([DS, NCORES * 128], F32),
        "bt": ([128, 2048], BF16), "t1n": ([128, 2048], BF16),
        "et1h": ([64, NCH * DS], BF16), "f1h": ([64, NCH * DS], BF16),
        "vtmf": ([128, NCH * DS], F32), "m1": ([DS, L], BF16),
        "hstart1": ([DS, DS], F32), "scol": ([128, NCH], F32),
        "ytm": ([128, NCH * DS], F32), "ynfm": ([DS, T], BF16),
        "ktil": ([DS, T], BF16),
        "x2h0": ([128, NDCH * 1024], F32),
    }
    tap_aps = {t: dout("tap_" + t, *tapdefs[t]) for t in taps}

    w2bf = nc.dram_tensor("w2bf", [NDCH * 128, NDCH * 128], BF16,
                          kind="Internal").ap()
    cc_in = [nc.dram_tensor(f"cc_in{p}", [DS, 128], F32, kind="Internal")
             for p in range(2)]
    cc_out = [nc.dram_tensor(f"cc_out{p}", [NCORES * DS, 128], F32, kind="Internal")
              for p in range(2)]

    with contextlib.ExitStack() as ST:
        E = ST.enter_context
        sb = lambda name, shape, dtype: E(nc.sbuf_tensor(name, list(shape), dtype))

        identb = sb("identb", [128, 128], BF16)
        identf_t = sb("identf_t", [64, 64], F32)
        maskS = sb("maskS", [128, 128], F32)
        maskI = sb("maskI", [128, 128], F32)
        asel = sb("asel", [DS, NCORES], F32)
        oasel = sb("oasel", [DS, NCORES], F32)
        onescol_b = sb("onescol_b", [128, 1], BF16)
        onesrow_f = sb("onesrow_f", [128, 128], F32)
        normw_t = sb("normw_t", [128, DS], F32)
        sw2_t = sb("sw2_t", [128, NDCH], F32)
        b2_t = sb("b2_t", [128, NDCH], F32)
        wab_t = sb("wab_t", [DS, 1], F32)
        lam_t = sb("lam_t", [DS, 1], F32)
        loga8 = sb("loga8", [DS, 1], F32)
        zeros64 = sb("zeros64", [DS, L], F32)
        cbias = sb("cbias", [128, 3], F32)
        wkq_b = sb("wkq_b", [128, NDCH * 128], BF16)
        wav_b = sb("wav_b", [128, NDCH * 128], BF16)
        wo_b = sb("wo_b", [DS, D], BF16)
        negS_kq = sb("negS_kq", [1, 128], BF16)
        negS_av = sb("negS_av", [1, 128], BF16)
        rowA = sb("rowA", [128, 4096], F32)
        rowC = sb("rowC", [128, 4096], BF16)
        xbf = [sb(f"xbf{i}", [128, T], BF16) for i in range(2)]
        Ktil_s = sb("Ktil_s", [DS, T], BF16)
        Qtil_s = sb("Qtil_s", [DS, T], BF16)
        Qhat_s = sb("Qhat_s", [DS, T], BF16)
        M1_t = sb("M1_t", [DS, L], BF16)
        scol = sb("scol", [128, NCH], F32)
        ynfm = sb("ynfm", [DS, T], BF16)
        HS = sb("HS", [DS, 128], F32)
        Hacc = sb("Hacc", [DS, DS], F32)
        segT = sb("segT", [DS, 128], F32)
        tmp64a = sb("tmp64a", [DS, DS], F32)
        tmp64b = sb("tmp64b", [DS, DS], F32)
        tmp64c = sb("tmp64c", [DS, DS], F32)
        HSh = sb("HSh", [DS, 128], BF16)
        HSl = sb("HSl", [DS, 128], BF16)
        Hah = sb("Hah", [DS, DS], BF16)
        Hal = sb("Hal", [DS, DS], BF16)
        H0lb = sb("H0lb", [DS, DS], BF16)
        onesrow_b = sb("onesrow_b", [128, 128], BF16)
        H0b = sb("H0b", [DS, DS], BF16)
        D_b = sb("D_b", [128, DS], BF16)
        U_b = sb("U_b", [128, DS], BF16)
        e_f = sb("e_f", [128, DS], F32)
        errcol = sb("errcol", [128, 1], F32)
        sm1c = sb("sm1c", [128, 1], F32)
        ytm_t = sb("ytm_t", [128, DS], F32)
        ysq = sb("ysq", [128, DS], F32)
        mscol = sb("mscol", [128, 1], F32)
        rstdc = sb("rstdc", [128, 1], F32)
        yn1 = sb("yn1", [128, DS], F32)
        ynb = sb("ynb", [128, DS], BF16)
        wbfm_b = sb("wbfm_b", [DS, L], BF16)
        vb_t = sb("vb_t", [DS, L], BF16)
        Gt_b = sb("Gt_b", [128, DS], BF16)
        Wb2tm_b = sb("Wb2tm_b", [128, DS], BF16)
        la_t = sb("la_t", [DS, T], F32)
        alp = sb("alp", [DS, T], F32)
        cums = sb("cums", [DS, T], F32)
        lamI = sb("lamI", [DS, T], F32)
        w2t = [wkq_b, wav_b]
        mrs2_b = rowC[0:1, 2048:3072]
        negS_w2 = rowC[0:1, 0:2048]

        AR = sb("AR", [128, 32768], BF16)
        ARF = AR[:].bitcast(F32)
        Bt_s = lambda c: AR[:, c * 128: c * 128 + 128]
        T1n_s = lambda c: AR[:, 2048 + c * 128: 2048 + c * 128 + 128]
        T1t_s = lambda c: AR[:, 4096 + c * 128: 4096 + c * 128 + 128]
        T2t_s = lambda c: AR[:, 6144 + c * 128: 6144 + c * 128 + 128]
        P1m_s = lambda c: AR[:, 8192 + c * 128: 8192 + c * 128 + 128]
        Ph1m_s = lambda c: AR[:, 10240 + c * 128: 10240 + c * 128 + 128]
        Ktm_s = lambda c: AR[:, 12288 + c * DS: 12288 + c * DS + DS]
        Wbtm_s = lambda c: AR[:, 13312 + c * DS: 13312 + c * DS + DS]
        Vtmb_s = lambda c: AR[:, 14336 + c * DS: 14336 + c * DS + DS]
        Vtmf_s = lambda c: ARF[:, 7680 + c * DS: 7680 + c * DS + DS]
        diagG_s = lambda c: ARF[0:64, 8704 + c * DS: 8704 + c * DS + DS]
        ET1h_s = lambda c: AR[0:64, 19456 + c * DS: 19456 + c * DS + DS]
        ET1l_s = lambda c: AR[0:64, 20480 + c * DS: 20480 + c * DS + DS]
        F1h_s = lambda c: AR[0:64, 21504 + c * DS: 21504 + c * DS + DS]
        F1l_s = lambda c: AR[0:64, 22528 + c * DS: 22528 + c * DS + DS]
        ET2h_s = lambda c: AR[0:64, 23552 + c * DS: 23552 + c * DS + DS]
        ET2l_s = lambda c: AR[0:64, 24576 + c * DS: 24576 + c * DS + DS]
        F2h_s = lambda c: AR[0:64, 25600 + c * DS: 25600 + c * DS + DS]
        F2l_s = lambda c: AR[0:64, 26624 + c * DS: 26624 + c * DS + DS]
        ET1_s = (ET1h_s, ET1l_s, F1h_s, F1l_s)
        ET2_s = (ET2h_s, ET2l_s, F2h_s, F2l_s)
        x2_s = lambda o: ARF[:, o * 1024: o * 1024 + 1024]
        Bn_b = AR[:, 27648:27776]
        R0n_b = AR[:, 27776:27904]
        R0t_b = AR[:, 27904:28032]
        B2n_b = AR[:, 28032:28160]
        B2t_b = AR[:, 28160:28288]
        B4n_b = AR[:, 28288:28416]
        R1n_b = AR[:, 28416:28544]
        R1t_b = AR[:, 28544:28672]
        Xn_b = AR[:, 28672:28800]
        T2n_b = AR[:, 28800:28928]
        Dlt_b = AR[:, 28928:29056]
        H0b_s = lambda c: AR[0:64, 29056 + c * DS: 29056 + c * DS + DS]
        H0l_s = lambda c: AR[0:64, 30080 + c * DS: 30080 + c * DS + DS]
        stmp = ARF[0:64, 15552:15616]
        Db_x = [D_b[:, :], AR[:, 31232:31296]]
        Ub_x = [U_b[:, :], AR[:, 31296:31360]]
        ef_x = [e_f[:, :], ARF[:, 15680:15744]]
        err_x = [errcol[:, :], ARF[:, 15744:15745]]
        Dlt_x = [Dlt_b, AR[:, 31490:31618]]
        Xn_x = [Xn_b, AR[:, 31618:31746]]
        T2n_x = [T2n_b, AR[:, 31746:31874]]
        Wb2_x = [Wb2tm_b[:, :], AR[:, 31874:31938]]
        Gt_x = [Gt_b[:, :], AR[:, 31938:32002]]
        sm1_x = [sm1c[:, :], ARF[:, 16001:16002]]
        t64b_x = [tmp64b[:, :], ARF[0:64, 16002:16066]]
        t64c_x = [tmp64c[:, :], ARF[0:64, 16066:16130]]
        t64a_x = [tmp64a[:, :], ARF[0:64, 16130:16194]]
        ytm_x = [ytm_t[:, :], ARF[:, 16194:16258]]
        yn1_x = [yn1[:, :], ARF[:, 16258:16322]]
        ynb_x = [ynb[:, :], AR[:, 32644:32708]]
        ms_x = [mscol[:, :], ARF[:, 16354:16355]]
        rs_x = [rstdc[:, :], ARF[:, 16355:16356]]
        xsq = [AR[:, 2048:3072], AR[:, 3072:4096]]
        ksq_b = AR[0:64, 0:2048]
        gath = rowA[0:64, 2048:3072]

        SCR = sb("SCR", [128, 10240], F32)
        SCRB = SCR[:].bitcast(BF16)
        kraw = SCR[0:64, 0:T]
        qraw = SCR[0:64, 2048:2048 + T]
        vs_f = SCR[0:64, 4096:4096 + T]
        knn = SCR[0:64, 6144:6144 + T]
        qnn = SCR[0:64, 8192:8192 + T]
        vraw_h = SCR[0:64, 6144:6144 + 1024]
        prraw_h = SCR[0:64, 7168:7168 + 1024]
        rs1b_h = SCR[0:64, 8192:8192 + 1024]
        x2n_v = lambda o: SCRB[:, o * 1024: o * 1024 + 1024]
        x2b_t = [SCRB[:, 16384 + i * 512: 16384 + (i + 1) * 512] for i in range(2)]
        x2sq_t = [SCRB[:, 17408 + i * 512: 17408 + (i + 1) * 512] for i in range(2)]
        rs2b = SCR[:, 9216:9216 + 512]
        xrl = [xbf[i][:].bitcast(F32)[:, j * 512:(j + 1) * 512]
               for i in range(2) for j in range(2)]
        out_bufs = [SCR[:, 9728:9728 + 512], rowA[:, 3072:3584],
                    rowA[:, 3584:4096]]

        semld = Sem(E(nc.semaphore("semld")))
        semw = Sem(E(nc.semaphore("semw")))
        semx = [Sem(E(nc.semaphore(f"semx{i}"))) for i in range(2)]
        semw2 = [Sem(E(nc.semaphore(f"semw2{i}"))) for i in range(2)]
        semxr = [Sem(E(nc.semaphore(f"semxr{i}"))) for i in range(4)]
        semo = [Sem(E(nc.semaphore(f"semo{i}"))) for i in range(3)]
        semw2b = [Sem(E(nc.semaphore(f"semw2b{i}"))) for i in range(2)]
        semst = Sem(E(nc.semaphore("semst")))
        semcc = Sem(E(nc.semaphore("semcc")))
        semcl = Sem(E(nc.semaphore("semcl")))
        sempe = Sem(E(nc.semaphore("sempe")))
        semv = Sem(E(nc.semaphore("semv")))
        sema = Sem(E(nc.semaphore("sema")))
        semg = Sem(E(nc.semaphore("semg")))
        bsem = [Sem(E(nc.semaphore(f"bk{i}"))) for i in range(8)]
        PB0 = Bank(E(nc.psum_tensor("PB0", [128, 512], F32)), bsem[0])
        PB1 = Bank(E(nc.psum_tensor("PB1", [128, 512], F32)), bsem[1])
        PS0 = Bank(E(nc.psum_tensor("PS0", [128, 512], F32)), bsem[2])
        PS1 = Bank(E(nc.psum_tensor("PS1", [128, 512], F32)), bsem[3])
        PS2 = Bank(E(nc.psum_tensor("PS2", [128, 512], F32)), bsem[4])
        PROW = Bank(E(nc.psum_tensor("PROW", [128, 512], F32)), bsem[5])
        PTR = Bank(E(nc.psum_tensor("PTR", [128, 512], F32)), bsem[6])
        PC = Bank(E(nc.psum_tensor("PC", [64, 512], F32)), bsem[7])

        q = {e: [] for e in ENGINES}
        OWN = {"vector": semv, "scalar": sema, "gpsimd": semg, "tensor": sempe}
        bank_by_sem = {id(b.sem): b
                       for b in (PB0, PB1, PS0, PS1, PS2, PROW, PTR, PC)}

        def wown(eng):
            wait(eng, OWN[eng])

        def wait(eng, sem, val=None):
            v = sem.n if val is None else val
            if v > 0:
                q[eng].append(lambda e, s=sem, v=v: e.wait_ge(s.h, v))

        def emit(eng, make, inc=None, inck=1):
            own = OWN.get(eng)
            dma_like = inck == 16 or (inc is not None and inc in
                                      (semcc, semcl, semst, semld))
            if eng in ("vector", "scalar") or (eng == "gpsimd" and not dma_like):
                if OWN_WAIT:
                    wait(eng, own)
                own.n += 1
                tgt, k = own, 1
                if inc is not None and id(inc) in bank_by_sem:
                    bank_by_sem[id(inc)].pending[id(own)] = (own, own.n)
            elif eng == "tensor":
                own.n += 1
                tgt, k = own, 1
            else:
                if inc is not None:
                    inc.n += inck
                tgt, k = inc, inck

            def f(e, make=make, tgt=tgt, k=k):
                ins = make(e)
                if tgt is not None:
                    ins.then_inc(tgt.h, k)
            q[eng].append(f)
            return tgt.n if tgt is not None else None

        def pe_bank(bank, ncons=1):
            for s, v in bank.pending.values():
                wait("tensor", s, v)
            bank.pending = {}
            return bank.t

        def csl(t, c, w=L):
            return t[:, c * w:(c + 1) * w]

        def hilo(src_ap, hi_ap, lo_ap, tmpf_ap):
            wait("scalar", semv)
            emit("scalar", lambda e, s=src_ap, h=hi_ap: e.activation(
                h, s, AF.Copy), sema, 1)
            wown("scalar")
            emit("scalar", lambda e, h=hi_ap, t=tmpf_ap: e.activation(
                t, h, AF.Copy), sema, 1)
            wait("vector", sema)
            wown("vector")
            emit("vector", lambda e, s=src_ap, t=tmpf_ap, l=lo_ap: e.tensor_tensor(
                l, s, t, ALU.subtract), semv, 1)
            return semv.n

        emit("sync", lambda e: e.dma_start(out=identf_t[:], in_=identf[0:64, 0:64]),
             semld, 16)
        emit("sync", lambda e: e.dma_start(out=maskS[:], in_=mask_s), semld, 16)
        emit("sync", lambda e: e.dma_start(out=maskI[:], in_=mask_i), semld, 16)
        emit("sync", lambda e: e.dma_start(out=asel[:], in_=alphasel), semld, 16)
        emit("sync", lambda e: e.dma_start(out=normw_t[:], in_=normw), semld, 16)
        emit("sync", lambda e: e.dma_start(out=sw2_t[:], in_=sw2c), semld, 16)
        emit("sync", lambda e: e.dma_start(out=b2_t[:], in_=b2c), semld, 16)
        emit("sync", lambda e: e.dma_start(out=wab_t[:], in_=wab), semld, 16)
        emit("sync", lambda e: e.dma_start(out=lam_t[:], in_=lam), semld, 16)
        consts_ld = semld.n
        emit("gpsimd", lambda e: e.dma_start(out=identb[:], in_=identf), semw, 16)
        emit("gpsimd", lambda e: e.dma_start(out=wkq_b[:], in_=wkq), semw, 16)
        emit("gpsimd", lambda e: e.dma_start(out=wav_b[:], in_=wav), semw, 16)
        emit("gpsimd", lambda e: e.dma_start(out=wo_b[:], in_=wo), semw, 16)
        wsmall_ld = semw.n
        emit("gpsimd", lambda e: e.memset(onescol_b[:], 1.0), semg, 1)
        emit("gpsimd", lambda e: e.memset(onesrow_f[:], 1.0), semg, 1)
        emit("gpsimd", lambda e: e.memset(zeros64[:], 0.0), semg, 1)
        emit("gpsimd", lambda e: e.memset(cbias[:, 0:1], 1e-8), semg, 1)
        emit("gpsimd", lambda e: e.memset(cbias[:, 1:2], LN_EPS), semg, 1)
        emit("gpsimd", lambda e: e.memset(cbias[:, 2:3], RMS_EPS), semg, 1)
        emit("gpsimd", lambda e: e.memset(onesrow_b[:], 1.0), semg, 1)
        memset_done = semg.n

        wait("scalar", semld, consts_ld)
        wait("scalar", semg, memset_done)
        emit("scalar", lambda e: e.activation(loga8[:], lam_t[:], AF.Sigmoid), sema, 1)
        wown("scalar")
        emit("scalar", lambda e: e.activation(loga8[:], loga8[:], AF.Ln,
                                              bias=cbias[0:64, 0:1]), sema, 1)
        wown("scalar")
        emit("scalar", lambda e: e.activation(loga8[:], loga8[:], AF.Copy,
                                              scale=C_GATE), sema, 1)
        wait("vector", semld, consts_ld)
        emit("vector", lambda e: e.tensor_scalar(oasel[:], asel[:], -1.0, 1.0,
                                                 ALU.mult, ALU.add), semv, 1)

        for wtile, outrow in ((wkq_b, negS_kq), (wav_b, negS_av)):
            ps = pe_bank(PROW, 1)
            wait("tensor", semw, wsmall_ld)
            wait("tensor", semg, memset_done)
            for i in range(NDCH):
                emit("tensor", lambda e, i=i, wtile=wtile, ps=ps: e.matmul(
                    ps[0:1, 0:128], onescol_b[:], wtile[:, i * 128:(i + 1) * 128],
                    start=(i == 0), stop=(i == NDCH - 1)), sempe, 1)
            wait("scalar", sempe)
            emit("scalar", lambda e, ps=ps, outrow=outrow: e.activation(
                outrow[:], ps[0:1, 0:128], AF.Copy, scale=-1.0 / D), PROW.sem, 1)

        xcast_pe = {}
        for h in range(2):
            t0 = h * 1024
            ps_kq = [pe_bank(PB0, 2), pe_bank(PB1, 2)]
            ps_va = [pe_bank(PS0, 2), pe_bank(PS1, 2)]
            ps_sx = [pe_bank(PROW, 2), pe_bank(PS2, 2)]
            ps_sq = [pe_bank(PC, 1), pe_bank(PTR, 1)]
            for i in range(NDCH):
                g = h * NDCH + i
                buf = xbf[g % 2]
                if g >= 2:
                    wait("gpsimd", sempe, xcast_pe[g - 2])
                emit("gpsimd", lambda e, i=i, t0=t0, buf=buf: e.dma_start(
                    out=buf[:, t0:t0 + 1024],
                    in_=xf[i * 128:(i + 1) * 128, t0:t0 + 1024]), semx[g % 2], 16)
                xldv = semx[g % 2].n
                wait("scalar", semx[g % 2], xldv)
                emit("scalar", lambda e, i=i, t0=t0, buf=buf: e.activation(
                    xsq[i % 2][:], buf[:, t0:t0 + 1024], AF.Square), sema, 1)
                sq_v = sema.n
                wait("tensor", semx[g % 2], xldv)
                first = (i == 0)
                last = (i == NDCH - 1)
                for tt in range(2):
                    emit("tensor", lambda e, tt=tt, t0=t0, buf=buf, first=first,
                         last=last, ps=ps_sx[tt]: e.matmul(
                        ps[0:1, :], onescol_b[:],
                        buf[:, t0 + tt * 512: t0 + tt * 512 + 512],
                        start=first, stop=last), sempe, 1)
                wait("tensor", sema, sq_v)
                for tt in range(2):
                    emit("tensor", lambda e, i=i, tt=tt, first=first, last=last,
                         ps=ps_sq[tt]: e.matmul(
                        ps[0:1, 0:512], onescol_b[:],
                        xsq[i % 2][:, tt * 512: tt * 512 + 512],
                        start=first, stop=last), sempe, 1)
                for tt in range(2):
                    emit("tensor", lambda e, i=i, tt=tt, t0=t0, buf=buf, first=first,
                         ps=ps_kq[tt]: e.matmul(
                        ps[:, :], wkq_b[:, i * 128:(i + 1) * 128],
                        buf[:, t0 + tt * 512: t0 + tt * 512 + 512],
                        start=first, stop=False), sempe, 1)
                    emit("tensor", lambda e, i=i, tt=tt, t0=t0, buf=buf, first=first,
                         ps=ps_va[tt]: e.matmul(
                        ps[:, :], wav_b[:, i * 128:(i + 1) * 128],
                        buf[:, t0 + tt * 512: t0 + tt * 512 + 512],
                        start=first, stop=False), sempe, 1)
                xcast_pe[g] = sempe.n
            stats_pe = sempe.n
            wait("scalar", sempe, stats_pe)
            for tt in range(2):
                c0 = t0 + tt * 512
                sxbank = PROW if tt == 0 else PS2
                sqbank = PC if tt == 0 else PTR
                emit("scalar", lambda e, c0=c0, ps=ps_sx[tt]: e.activation(
                    rowA[0:1, c0: c0 + 512],
                    ps[0:1, :], AF.Copy), sxbank.sem, 1)
                emit("scalar", lambda e, c0=c0, ps=ps_sx[tt]: e.activation(
                    rowC[0:1, c0: c0 + 512],
                    ps[0:1, :], AF.Copy), sxbank.sem, 1)
                wown("scalar")
                emit("scalar", lambda e, c0=c0: e.activation(
                    rowA[0:1, 2048 + c0: 2048 + c0 + 512],
                    rowA[0:1, c0: c0 + 512],
                    AF.Square, scale=1.0 / D), sema, 1)
                m1sq_v = sema.n
                wait("vector", sema, m1sq_v)
                wait("vector", sempe, stats_pe)
                emit("vector", lambda e, c0=c0, ps=ps_sq[tt]:
                     e.scalar_tensor_tensor(
                    rowA[0:1, 2048 + c0: 2048 + c0 + 512],
                    ps[0:1, 0:512], 1.0 / D,
                    rowA[0:1, 2048 + c0: 2048 + c0 + 512],
                    ALU.mult, ALU.subtract), sqbank.sem, 1)
                var_v = semv.n
                wait("scalar", semv, var_v)
                emit("scalar", lambda e, c0=c0: e.activation(
                    rowA[64:65, c0: c0 + 512],
                    rowA[0:1, 2048 + c0: 2048 + c0 + 512],
                    AF.Ln, bias=cbias[64:65, 1:2]), sema, 1)
                wown("scalar")
                emit("scalar", lambda e, c0=c0: e.activation(
                    rowA[64:65, c0: c0 + 512],
                    rowA[64:65, c0: c0 + 512],
                    AF.Exp, scale=-0.5), sema, 1)
            rows_v = sema.n
            wait("tensor", sema, rows_v)
            for tt in range(2):
                c0 = t0 + tt * 512
                emit("tensor", lambda e, tt=tt, c0=c0, ps=ps_kq[tt]: e.matmul(
                    ps[:, :], negS_kq[:],
                    rowC[0:1, c0: c0 + 512], start=False, stop=True), sempe, 1)
                emit("tensor", lambda e, tt=tt, c0=c0, ps=ps_va[tt]: e.matmul(
                    ps[:, :], negS_av[:],
                    rowC[0:1, c0: c0 + 512], start=False, stop=True), sempe, 1)
            corr_pe = sempe.n
            for tt in range(2):
                c0 = t0 + tt * 512
                wown("scalar")
                emit("scalar", lambda e, c0=c0, t0=t0: e.activation(
                    rowC[0:1, 2048 + c0 - t0: 2048 + c0 - t0 + 512],
                    rowA[64:65, c0: c0 + 512], AF.Copy), sema, 1)
                rs1bf_a = sema.n
                psb = pe_bank(PC, 1)
                wait("tensor", sema, rs1bf_a)
                emit("tensor", lambda e, c0=c0, t0=t0, psb=psb: e.matmul(
                    psb[:, 0:512], onesrow_b[0:1, 0:64],
                    rowC[0:1, 2048 + c0 - t0: 2048 + c0 - t0 + 512],
                    start=True, stop=True), sempe, 1)
                wait("vector", sempe)
                emit("vector", lambda e, tt=tt, psb=psb: e.tensor_copy(
                    rs1b_h[:, tt * 512: tt * 512 + 512], psb[:, 0:512]), PC.sem, 1)
            rs1b_v = semv.n
            wait("scalar", sempe, corr_pe)
            wait("scalar", semv)
            for tt in range(2):
                c0 = t0 + tt * 512
                bk = PB0 if tt == 0 else PB1
                bv = PS0 if tt == 0 else PS1
                emit("scalar", lambda e, tt=tt, c0=c0, ps=ps_kq[tt]: e.activation(
                    kraw[:, c0: c0 + 512], ps[0:64, :], AF.Copy), bk.sem, 1)
                emit("scalar", lambda e, tt=tt, c0=c0, ps=ps_kq[tt]: e.activation(
                    qraw[:, c0: c0 + 512], ps[64:128, :], AF.Copy), bk.sem, 1)
                emit("scalar", lambda e, tt=tt, c0=c0, t0=t0, ps=ps_va[tt]:
                     e.activation(
                    prraw_h[:, c0 - t0: c0 - t0 + 512], ps[0:64, :], AF.Copy),
                    bv.sem, 1)
                emit("scalar", lambda e, tt=tt, c0=c0, t0=t0, ps=ps_va[tt]:
                     e.activation(
                    vraw_h[:, c0 - t0: c0 - t0 + 512], ps[64:128, :], AF.Copy),
                    bv.sem, 1)
            evac_a = sema.n
            wait("vector", sema, evac_a)
            wait("vector", semv, rs1b_v)
            emit("vector", lambda e, t0=t0: e.tensor_tensor(
                vs_f[:, t0:t0 + 1024], vraw_h[:, :], rs1b_h[:, :], ALU.mult), semv, 1)
            emit("vector", lambda e, t0=t0: e.tensor_tensor(
                la_t[:, t0:t0 + 1024], prraw_h[:, :], rs1b_h[:, :], ALU.mult), semv, 1)
        sweeps_v = semv.n
        sweeps_a = sema.n

        pcst = [AR[:, 23552:25600], AR[:, 25600:27648]]
        for t in range(NDCH):
            if t >= 2:
                wait("gpsimd", semw2b[t % 2], semw2b[t % 2].n)
            emit("gpsimd", lambda e, t=t: e.dma_start(
                out=pcst[t % 2], in_=w2[t * 128:(t + 1) * 128, :]),
                semw2[t % 2], 16)
            wait("gpsimd", semw2[t % 2], semw2[t % 2].n)
            emit("gpsimd", lambda e, t=t: e.dma_start(
                out=w2bf[t * 128:(t + 1) * 128, :], in_=pcst[t % 2]),
                semw2b[t % 2], 16)

        wait("scalar", semv, sweeps_v)
        wown("scalar")
        emit("scalar", lambda e: e.activation(ksq_b[:, :], kraw[:, :], AF.Square),
             sema, 1)
        ksq_a = sema.n
        rbank = (PROW, PS2, PB0, PB1)
        bbank = (PC, PTR, PS0, PS1)
        kmm_pe = 0
        for phase, (roff, raw, pb) in enumerate(
                ((0, kraw, 32), (2048, qraw, 64))):
            if phase == 1:
                wait("scalar", sempe, kmm_pe)
                wown("scalar")
                emit("scalar", lambda e: e.activation(
                    ksq_b[:, :], qraw[:, :], AF.Square), sema, 1)
            sq_a = sema.n
            pss = []
            wait("tensor", sema, sq_a)
            for tt in range(4):
                ps = pe_bank(rbank[tt], 1)
                pss.append(ps)
                emit("tensor", lambda e, tt=tt, ps=ps: e.matmul(
                    ps[0:1, :], onescol_b[0:64, :],
                    ksq_b[:, tt * 512: tt * 512 + 512],
                    start=True, stop=True), sempe, 1)
            kmm_pe = sempe.n
            wait("scalar", sempe, kmm_pe)
            for tt in range(4):
                emit("scalar", lambda e, tt=tt, roff=roff, ps=pss[tt]:
                     e.activation(
                    rowA[32:33, roff + tt * 512: roff + tt * 512 + 512],
                    ps[0:1, :], AF.Ln), rbank[tt].sem, 1)
            wown("scalar")
            for tt in range(4):
                emit("scalar", lambda e, tt=tt, roff=roff: e.activation(
                    rowA[32:33, roff + tt * 512: roff + tt * 512 + 512],
                    rowA[32:33, roff + tt * 512: roff + tt * 512 + 512],
                    AF.Exp, scale=-0.5), sema, 1)
            wown("scalar")
            for tt in range(4):
                emit("scalar", lambda e, tt=tt, roff=roff, pb=pb: e.activation(
                    rowC[pb:pb + 1, roff % 2048 + tt * 512:
                         roff % 2048 + tt * 512 + 512],
                    rowA[32:33, roff + tt * 512: roff + tt * 512 + 512],
                    AF.Copy), sema, 1)
        rn_a = sema.n
        wait("tensor", sema, rn_a)
        for pb, raw, dst in ((32, kraw, knn), (64, qraw, qnn)):
            pss2 = []
            for tt in range(4):
                psb = pe_bank(bbank[tt], 1)
                pss2.append(psb)
                emit("tensor", lambda e, tt=tt, pb=pb, psb=psb: e.matmul(
                    psb[0:64, 0:512], onesrow_b[pb:pb + 1, 0:64],
                    rowC[pb:pb + 1, tt * 512: tt * 512 + 512],
                    start=True, stop=True), sempe, 1)
            bc_pe = sempe.n
            wait("vector", sempe, bc_pe)
            for tt in range(4):
                emit("vector", lambda e, tt=tt, raw=raw, dst=dst, psb=pss2[tt]:
                     e.tensor_tensor(
                    dst[:, tt * 512: tt * 512 + 512], psb[0:64, 0:512],
                    raw[:, tt * 512: tt * 512 + 512], ALU.mult), bbank[tt].sem, 1)
        knn_v = semv.n

        wait("scalar", semv, sweeps_v)
        emit("scalar", lambda e: e.activation(alp[:, :], la_t[:, :], AF.Sigmoid,
                                              bias=wab_t[:]), sema, 1)
        wown("scalar")
        emit("scalar", lambda e: e.activation(la_t[:, :], alp[:, :], AF.Copy,
                                              scale=loga8[:]), sema, 1)
        wown("scalar")
        emit("scalar", lambda e: e.activation(alp[:, :], la_t[:, :], AF.Exp), sema, 1)
        alpha_a = sema.n
        wait("vector", sema, alpha_a)
        for c in range(NCH):
            emit("vector", lambda e, c=c: e.tensor_tensor_scan(
                cums[:, c * L:(c + 1) * L], la_t[:, c * L:(c + 1) * L],
                zeros64[:, :], 0.0, ALU.add, ALU.add), semv, 1)
        cums_v = semv.n
        wait("scalar", semv, cums_v)
        emit("scalar", lambda e: e.activation(lamI[:, :], cums[:, :], AF.Exp), sema, 1)
        lam_a = sema.n
        wait("vector", sema, alpha_a)
        wown("vector")
        emit("vector", lambda e: e.tensor_scalar(la_t[:, :], alp[:, :], -1.0, 1.0,
                                                 ALU.mult, ALU.add), semv, 1)
        wown("vector")
        emit("vector", lambda e: e.tensor_tensor(alp[:, :], la_t[:, :], knn[:, :],
                                                 ALU.mult), semv, 1)
        w1_v = semv.n

        rlam_f = SCR[0:64, 2048:2048 + T]
        wait("scalar", semv, w1_v)
        wait("scalar", sema, lam_a)
        emit("scalar", lambda e: e.activation(
            la_t[:, 1:T], lamI[:, 0:T - 1], AF.Copy), sema, 1)
        emit("scalar", lambda e: e.activation(
            rlam_f[:, :], cums[:, :], AF.Exp, scale=-1.0), sema, 1)
        lamE_a = sema.n
        wait("gpsimd", sema, lamE_a)
        for c in range(NCH):
            emit("gpsimd", lambda e, c=c: e.memset(
                la_t[:, c * L: c * L + 1], 1.0), semg, 1)
        lamE_g = semg.n
        til_piece = []
        wait("vector", sema, lamE_a)
        wait("vector", semg, lamE_g)
        wown("vector")
        for p4 in range(4):
            s4 = slice(p4 * 512, p4 * 512 + 512)
            emit("vector", lambda e, s4=s4: e.tensor_tensor(
                Ktil_s[:, s4], knn[:, s4], la_t[:, s4], ALU.mult), semv, 1)
            emit("vector", lambda e, s4=s4: e.tensor_tensor(
                Qtil_s[:, s4], qnn[:, s4], la_t[:, s4], ALU.mult), semv, 1)
            emit("vector", lambda e, s4=s4: e.tensor_tensor(
                Qhat_s[:, s4], qnn[:, s4], lamI[:, s4], ALU.mult), semv, 1)
            til_piece.append(semv.n)

        for tname, src in (("knn", knn), ("qnn", qnn), ("vs", vs_f),
                           ("alp", alp), ("lamI", lamI)):
            if tname in taps:
                wait("sync", semv, semv.n)
                wait("sync", sema, sema.n)
                emit("sync", lambda e, t=tname, src=src: e.dma_start(
                    out=tap_aps[t], in_=src[:, :]), semst, 16)

        if phases < 1.2:
            _replay(nc, q, semst, semcc)
            return nc
        PP = [
            dict(M1=M1_t[:, :], wbfm=wbfm_b[:, :], vb=vb_t[:, :],
                 Bn=Bn_b, R0n=R0n_b, R0t=R0t_b, B2n=B2n_b, B2t=B2t_b,
                 B4n=B4n_b, R1n=R1n_b, R1t=R1t_b, Gt=Gt_b[:, :],
                 t64b=tmp64b[:, :], t64c=tmp64c[:, :], t64a=tmp64a[:, :],
                 bA=PS0, bR0=PS1, bB2=PS2, bB4=PS0, bT1=PS1, bG=PS2, bEF=PC,
                 ptlo=0),
            dict(M1=AR[0:64, 30080:30208], wbfm=AR[0:64, 30208:30336],
                 vb=AR[0:64, 30336:30464],
                 Bn=AR[:, 29056:29184], R0n=AR[:, 29184:29312],
                 R0t=AR[:, 29312:29440], B2n=AR[:, 29440:29568],
                 B2t=AR[:, 29568:29696], B4n=AR[:, 29696:29824],
                 R1n=AR[:, 29824:29952], R1t=AR[:, 29952:30080],
                 Gt=AR[:, 30464:30528],
                 t64b=ARF[0:64, 15264:15328], t64c=ARF[0:64, 15328:15392],
                 t64a=ARF[0:64, 15392:15456],
                 bA=PB0, bR0=PB1, bB2=PROW, bB4=PB0, bT1=PB1, bG=PROW,
                 bEF=PB0, ptlo=1),
        ]
        pv = [dict(tr=0, psA=0, b2=0, b4=0, t1=0, ef=0, hiloa=0)
              for _ in range(2)]
        wait("scalar", semv, w1_v)

        def prep_s1(c, P, V):
            if V["psA"]:
                wait("vector", sempe, V["psA"])
            emit("vector", lambda e, c=c, P=P: e.tensor_tensor(
                P["M1"], alp[:, c * L:(c + 1) * L],
                rlam_f[:, c * L:(c + 1) * L], ALU.mult), semv, 1)
            m1_v = semv.n
            if V["tr"]:
                wait("scalar", sempe, V["tr"])
            emit("scalar", lambda e, c=c: e.activation(
                diagG_s(c), identf_t[:, :], AF.Copy,
                scale=lamI[:, c * L + L - 1: c * L + L]), sema, 1)
            wait("scalar", semv, m1_v)
            emit("scalar", lambda e, c=c, P=P: e.activation(
                P["wbfm"], P["M1"], AF.Copy,
                scale=lamI[:, c * L + L - 1: c * L + L]), sema, 1)
            emit("scalar", lambda e, c=c, P=P: e.activation(
                P["vb"], vs_f[:, c * L:(c + 1) * L], AF.Copy), sema, 1)
            return m1_v, sema.n

        def prep_s2pair(cs, m1s, scas):
            ptf = pe_bank(PTR, 8)
            pt = ptf[:].bitcast(BF16)
            for (c, P), sc_a in zip(cs, scas):
                o = P["ptlo"] * 192
                wait("tensor", sema, sc_a)
                emit("tensor", lambda e, c=c, pt=pt, o=o: e.transpose(
                    pt[:, o: o + 64], csl(Ktil_s, c), identb[0:64, 0:64]),
                    sempe, 1)
                emit("tensor", lambda e, P=P, pt=pt, o=o: e.transpose(
                    pt[:, o + 64: o + 128], P["wbfm"], identb[0:64, 0:64]),
                    sempe, 1)
                emit("tensor", lambda e, P=P, pt=pt, o=o: e.transpose(
                    pt[:, o + 128: o + 192], P["vb"], identb[0:64, 0:64]),
                    sempe, 1)
                pv[P["ptlo"]]["tr"] = sempe.n
            wait("scalar", sempe)
            for (c, P), sc_a in zip(cs, scas):
                o = P["ptlo"] * 192
                emit("scalar", lambda e, c=c, pt=pt, o=o: e.activation(
                    Ktm_s(c), pt[:, o: o + 64], AF.Copy), PTR.sem, 1)
                emit("scalar", lambda e, c=c, pt=pt, o=o: e.activation(
                    Wbtm_s(c), pt[:, o + 64: o + 128], AF.Copy), PTR.sem, 1)
                emit("scalar", lambda e, c=c, pt=pt, o=o: e.activation(
                    Vtmb_s(c), pt[:, o + 128: o + 192], AF.Copy), PTR.sem, 1)
                emit("scalar", lambda e, c=c, pt=pt, o=o: e.activation(
                    Vtmf_s(c), pt[:, o + 128: o + 192], AF.Copy), PTR.sem, 1)

        def prep_s3(c, P, V, m1_v):
            psA = pe_bank(P["bA"], 3)
            wait("tensor", semv, m1_v)
            emit("tensor", lambda e, c=c, P=P, psA=psA: e.matmul(
                psA[:, 0:128], P["M1"], csl(Ktil_s, c), start=True, stop=False),
                sempe, 1)
            emit("tensor", lambda e, c=c, P=P, psA=psA: e.matmul(
                psA[:, 128:256], P["M1"], csl(Qtil_s, c), start=False,
                stop=False), sempe, 1)
            emit("tensor", lambda e, c=c, P=P, psA=psA: e.matmul(
                psA[:, 256:384], P["M1"], csl(Qhat_s, c), start=False,
                stop=True), sempe, 1)
            V["psA"] = sempe.n
            wait("vector", sempe, V["psA"])
            emit("vector", lambda e, c=c, psA=psA: e.scalar_tensor_tensor(
                Bt_s(c), psA[:, 0:128], -1.0, maskS[:, :], ALU.mult, ALU.mult),
                P["bA"].sem, 1)
            emit("vector", lambda e, c=c, psA=psA: e.tensor_tensor(
                P1m_s(c), psA[:, 128:256], maskS[:, :], ALU.mult),
                P["bA"].sem, 1)
            emit("vector", lambda e, c=c, psA=psA: e.tensor_tensor(
                Ph1m_s(c), psA[:, 256:384], maskI[:, :], ALU.mult),
                P["bA"].sem, 1)
            return semv.n

        def prep_s4pair(cs, bvs):
            pt2 = pe_bank(PTR, 2)[:].bitcast(BF16)
            bn_pes = []
            for (c, P), b_v in zip(cs, bvs):
                o = P["ptlo"] * 128
                wait("tensor", semv, b_v)
                emit("tensor", lambda e, c=c, pt2=pt2, o=o: e.transpose(
                    pt2[:, o: o + 128], Bt_s(c), identb[:, :]), sempe, 1)
                bn_pes.append(sempe.n)
            wait("scalar", sempe)
            for (c, P), _ in zip(cs, bvs):
                o = P["ptlo"] * 128
                emit("scalar", lambda e, P=P, pt2=pt2, o=o: e.activation(
                    P["Bn"], pt2[:, o: o + 128], AF.Copy), PTR.sem, 1)
            return sema.n

        def prep_s4b(c, P, bn_a):
            ps0r = pe_bank(P["bR0"], 2)
            wait("tensor", sema, bn_a)
            emit("tensor", lambda e, P=P, ps0r=ps0r: e.matmul(
                ps0r[:, 0:128], identb[:, :], P["Bn"], start=True, stop=False),
                sempe, 1)
            emit("tensor", lambda e, ps0r=ps0r: e.matmul(
                ps0r[:, 0:128], identb[:, :], identb[:, :], start=False,
                stop=False), sempe, 1)
            emit("tensor", lambda e, c=c, ps0r=ps0r: e.matmul(
                ps0r[:, 128:256], identb[:, :], Bt_s(c), start=False, stop=False),
                sempe, 1)
            emit("tensor", lambda e, ps0r=ps0r: e.matmul(
                ps0r[:, 128:256], identb[:, :], identb[:, :], start=False,
                stop=True), sempe, 1)
            r0_pe = sempe.n
            wait("scalar", sempe, r0_pe)
            emit("scalar", lambda e, P=P, ps0r=ps0r: e.activation(
                P["R0n"], ps0r[:, 0:128], AF.Copy), P["bR0"].sem, 1)
            emit("scalar", lambda e, P=P, ps0r=ps0r: e.activation(
                P["R0t"], ps0r[:, 128:256], AF.Copy), P["bR0"].sem, 1)

        def prep_s5(c, P):
            ps2r = pe_bank(P["bB2"], 2)
            emit("tensor", lambda e, c=c, P=P, ps2r=ps2r: e.matmul(
                ps2r[:, 0:128], Bt_s(c), P["Bn"], start=True, stop=False),
                sempe, 1)
            emit("tensor", lambda e, c=c, P=P, ps2r=ps2r: e.matmul(
                ps2r[:, 128:256], P["Bn"], Bt_s(c), start=False, stop=True),
                sempe, 1)
            V = pv[P["ptlo"]]
            V["b2"] = sempe.n
            wait("scalar", sempe, V["b2"])
            emit("scalar", lambda e, P=P, ps2r=ps2r: e.activation(
                P["B2n"], ps2r[:, 0:128], AF.Copy), P["bB2"].sem, 1)
            emit("scalar", lambda e, P=P, ps2r=ps2r: e.activation(
                P["B2t"], ps2r[:, 128:256], AF.Copy), P["bB2"].sem, 1)
            return sema.n

        def prep_s6(c, P, b2_a):
            ps3r = pe_bank(P["bB4"], 3)
            wait("tensor", sema, b2_a)
            emit("tensor", lambda e, P=P, ps3r=ps3r: e.matmul(
                ps3r[:, 0:128], P["B2t"], P["B2n"], start=True, stop=False),
                sempe, 1)
            emit("tensor", lambda e, P=P, ps3r=ps3r: e.matmul(
                ps3r[:, 128:256], P["R0t"], P["B2n"], start=False, stop=False),
                sempe, 1)
            emit("tensor", lambda e, P=P, ps3r=ps3r: e.matmul(
                ps3r[:, 128:256], identb[:, :], P["R0n"], start=False,
                stop=False), sempe, 1)
            emit("tensor", lambda e, P=P, ps3r=ps3r: e.matmul(
                ps3r[:, 256:384], P["B2n"], P["R0t"], start=False, stop=False),
                sempe, 1)
            emit("tensor", lambda e, P=P, ps3r=ps3r: e.matmul(
                ps3r[:, 256:384], identb[:, :], P["R0t"], start=False,
                stop=True), sempe, 1)
            pv[P["ptlo"]]["b4"] = sempe.n
            wait("scalar", sempe, pv[P["ptlo"]]["b4"])
            emit("scalar", lambda e, P=P, ps3r=ps3r: e.activation(
                P["B4n"], ps3r[:, 0:128], AF.Copy), P["bB4"].sem, 1)
            emit("scalar", lambda e, P=P, ps3r=ps3r: e.activation(
                P["R1n"], ps3r[:, 128:256], AF.Copy), P["bB4"].sem, 1)
            emit("scalar", lambda e, P=P, ps3r=ps3r: e.activation(
                P["R1t"], ps3r[:, 256:384], AF.Copy), P["bB4"].sem, 1)
            return sema.n

        def prep_s7(c, P, r1_a):
            ps4r = pe_bank(P["bT1"], 2)
            wait("tensor", sema, r1_a)
            emit("tensor", lambda e, P=P, ps4r=ps4r: e.matmul(
                ps4r[:, 0:128], P["R1t"], P["B4n"], start=True, stop=False),
                sempe, 1)
            emit("tensor", lambda e, P=P, ps4r=ps4r: e.matmul(
                ps4r[:, 0:128], identb[:, :], P["R1n"], start=False, stop=False),
                sempe, 1)
            emit("tensor", lambda e, P=P, ps4r=ps4r: e.matmul(
                ps4r[:, 128:256], P["B4n"], P["R1t"], start=False, stop=False),
                sempe, 1)
            emit("tensor", lambda e, P=P, ps4r=ps4r: e.matmul(
                ps4r[:, 128:256], identb[:, :], P["R1t"], start=False,
                stop=True), sempe, 1)
            pv[P["ptlo"]]["t1"] = sempe.n
            wait("scalar", sempe, pv[P["ptlo"]]["t1"])
            emit("scalar", lambda e, c=c, ps4r=ps4r: e.activation(
                T1n_s(c), ps4r[:, 0:128], AF.Copy), P["bT1"].sem, 1)
            emit("scalar", lambda e, c=c, ps4r=ps4r: e.activation(
                T1t_s(c), ps4r[:, 128:256], AF.Copy), P["bT1"].sem, 1)
            return sema.n

        def prep_s8(c, P, t1_a):
            V = pv[P["ptlo"]]
            ps5r = pe_bank(P["bG"], 1)
            wait("tensor", sema, t1_a)
            emit("tensor", lambda e, c=c, P=P, ps5r=ps5r: e.matmul(
                ps5r[:, 0:64], T1n_s(c), Wbtm_s(c), start=True, stop=True),
                sempe, 1)
            wait("scalar", sempe)
            if V["ef"]:
                wait("scalar", sempe, V["ef"])
            emit("scalar", lambda e, P=P, ps5r=ps5r: e.activation(
                P["Gt"], ps5r[:, 0:64], AF.Copy), P["bG"].sem, 1)
            g1_a = sema.n
            psc = pe_bank(P["bEF"], 2)
            wait("tensor", sema, g1_a)
            emit("tensor", lambda e, c=c, P=P, psc=psc: e.matmul(
                psc[0:64, 0:64], Ktm_s(c), P["Gt"], start=True, stop=False),
                sempe, 1)
            emit("tensor", lambda e, c=c, P=P, psc=psc: e.matmul(
                psc[0:64, 64:128], P["Gt"], Vtmb_s(c), start=False, stop=True),
                sempe, 1)
            V["ef"] = sempe.n
            wait("vector", sempe, V["ef"])
            if V["hiloa"]:
                wait("vector", sema, V["hiloa"])
            emit("vector", lambda e, c=c, P=P, psc=psc: e.scalar_tensor_tensor(
                P["t64b"], psc[0:64, 0:64], -1.0, diagG_s(c), ALU.mult, ALU.add),
                P["bEF"].sem, 1)
            ef_v = semv.n
            wait("scalar", semv, ef_v)
            emit("scalar", lambda e, P=P, psc=psc: e.activation(
                P["t64c"], psc[0:64, 64:128], AF.Copy), P["bEF"].sem, 1)
            hilo(P["t64b"], ET1h_s(c), ET1l_s(c), P["t64a"])
            hilo(P["t64c"], F1h_s(c), F1l_s(c), P["t64a"])
            V["hiloa"] = sema.n
            hiloa1_of[c] = (sema.n, semv.n)

        cs_a = [0]
        cs_v = [0]

        def compose_init(prev_v, prev_a):
            wait("gpsimd", semv, prev_v)
            wait("gpsimd", sema, prev_a)
            emit("gpsimd", lambda e: e.memset(HS[:, :], 0.0), semg, 1)
            emit("gpsimd", lambda e: e.memset(HSh[:, :], 0.0), semg, 1)
            emit("gpsimd", lambda e: e.memset(HSl[:, :], 0.0), semg, 1)
            hs_g = semg.n
            wait("vector", semg, hs_g)
            emit("vector", lambda e: e.tensor_tensor(
                HS[:, 64:128], zeros64[:, 0:64], identf_t[:, :], ALU.add), semv, 1)
            wait("scalar", semg, hs_g)
            emit("scalar", lambda e: e.activation(
                HSh[:, 64:128], identf_t[:, :], AF.Copy), sema, 1)
            cs_v[0] = semv.n
            cs_a[0] = sema.n

        def compose_step(c, ST, bank, ef_a):
            ETh, ETl, Fh, Fl = ST
            psc = pe_bank(bank)
            wait("tensor", semv, cs_v[0])
            wait("tensor", sema, cs_a[0])
            if ef_a:
                wait("tensor", sema, ef_a[0])
                wait("tensor", semv, ef_a[1])
            emit("tensor", lambda e, c=c, psc=psc: e.matmul(
                psc[0:64, 0:128], ETh(c), HSh[:, :], start=True, stop=False),
                sempe, 1)
            emit("tensor", lambda e, c=c, psc=psc: e.matmul(
                psc[0:64, 0:128], ETh(c), HSl[:, :], start=False, stop=False),
                sempe, 1)
            emit("tensor", lambda e, c=c, psc=psc: e.matmul(
                psc[0:64, 0:128], ETl(c), HSh[:, :], start=False, stop=False),
                sempe, 1)
            emit("tensor", lambda e, c=c, psc=psc: e.matmul(
                psc[0:64, 0:64], identb[0:64, 0:64], Fh(c), start=False,
                stop=False), sempe, 1)
            emit("tensor", lambda e, c=c, psc=psc: e.matmul(
                psc[0:64, 0:64], identb[0:64, 0:64], Fl(c), start=False,
                stop=True), sempe, 1)
            cp_pe = sempe.n
            wait("scalar", sempe, cp_pe)
            wait("scalar", semv, cs_v[0])
            emit("scalar", lambda e, psc=psc: e.activation(
                HS[:, :], psc[0:64, 0:128], AF.Copy), sema, 1)
            wown("scalar")
            emit("scalar", lambda e: e.activation(
                HSh[:, :], HS[:, :], AF.Copy), sema, 1)
            wown("scalar")
            emit("scalar", lambda e: e.activation(
                segT[:, :], HSh[:, :], AF.Copy), sema, 1)
            cs_a[0] = sema.n
            wait("vector", sema, cs_a[0])
            emit("vector", lambda e: e.tensor_tensor(
                HSl[:, :], HS[:, :], segT[:, :], ALU.subtract), bank.sem, 1)
            cs_v[0] = semv.n

        def exchange_finish(passno):
            wait("scalar", semv, cs_v[0])
            wown("scalar")
            emit("scalar", lambda e: e.activation(segT[:, 0:64], HS[:, 0:64],
                                                  AF.Copy), sema, 1)
            seg_a = sema.n
            ptt = pe_bank(PTR)[:].bitcast(BF16)
            wait("tensor", sema, seg_a)
            emit("tensor", lambda e, ptt=ptt: e.transpose(
                ptt[0:64, 0:64], HSh[:, 64:128], identb[0:64, 0:64]), sempe, 1)
            emit("tensor", lambda e, ptt=ptt: e.transpose(
                ptt[0:64, 64:128], HSl[:, 64:128], identb[0:64, 0:64]), sempe, 1)
            wait("scalar", sempe)
            emit("scalar", lambda e, ptt=ptt: e.activation(
                tmp64b[:, :], ptt[0:64, 0:64], AF.Copy), sema, 1)
            emit("scalar", lambda e, ptt=ptt: e.activation(
                tmp64c[:, :], ptt[0:64, 64:128], AF.Copy), PTR.sem, 1)
            wait("vector", sema)
            emit("vector", lambda e: e.tensor_tensor(
                segT[:, 64:128], tmp64b[:, :], tmp64c[:, :], ALU.add), semv, 1)
            seg_v = semv.n
            wait("gpsimd", semv, seg_v)
            wait("gpsimd", sema)
            emit("gpsimd", lambda e, p=passno: e.dma_start(
                out=cc_in[p].ap(), in_=segT[:, :]), semcc, 16)
            wait("gpsimd", semcc, semcc.n)
            emit("gpsimd", lambda e, p=passno: e.collective_compute(
                "AllGather", ALU.bypass,
                replica_groups=[list(range(NCORES))],
                ins=[cc_in[p].ap().opt()],
                outs=[cc_out[p].ap().opt()]), semcl, 1)
            wait("gpsimd", semcl, semcl.n)
            emit("gpsimd", lambda e, p=passno: e.dma_start(
                out=gath[:, :].rearrange("p (r f) -> p r f", r=NCORES),
                in_=cc_out[p].ap().rearrange("(r p) f -> p r f", p=DS)), semcc, 16)
            gath_g = semcc.n
            emit("gpsimd", lambda e: e.memset(Hacc[:, :], 0.0), semg, 1)
            emit("gpsimd", lambda e: e.memset(Hah[:, :], 0.0), semg, 1)
            emit("gpsimd", lambda e: e.memset(Hal[:, :], 0.0), semg, 1)
            hz_g = semg.n
            for eng in ("tensor", "vector", "scalar"):
                wait(eng, semg, hz_g)
                wait(eng, semcc, gath_g)
            for r in range(NCORES):
                wait("scalar", semv)
                wown("scalar")
                emit("scalar", lambda e, r=r: e.activation(
                    HSh[:, :], gath[:, r * 128:(r + 1) * 128], AF.Copy), sema, 1)
                wown("scalar")
                emit("scalar", lambda e: e.activation(
                    segT[:, :], HSh[:, :], AF.Copy), sema, 1)
                wait("vector", sema)
                emit("vector", lambda e, r=r: e.tensor_tensor(
                    HSl[:, :], gath[:, r * 128:(r + 1) * 128], segT[:, :],
                    ALU.subtract), semv, 1)
                sp_v = semv.n
                psc = pe_bank(PC)
                wait("tensor", semv, sp_v)
                emit("tensor", lambda e, psc=psc: e.matmul(
                    psc[:, 0:64], HSh[:, 64:128], Hah[:, :], start=True,
                    stop=False), sempe, 1)
                emit("tensor", lambda e, psc=psc: e.matmul(
                    psc[:, 0:64], HSh[:, 64:128], Hal[:, :], start=False,
                    stop=False), sempe, 1)
                emit("tensor", lambda e, psc=psc: e.matmul(
                    psc[:, 0:64], HSl[:, 64:128], Hah[:, :], start=False,
                    stop=False), sempe, 1)
                emit("tensor", lambda e, psc=psc: e.matmul(
                    psc[:, 0:64], identb[0:64, 0:64], HSh[:, 0:64], start=False,
                    stop=False), sempe, 1)
                emit("tensor", lambda e, psc=psc: e.matmul(
                    psc[:, 0:64], identb[0:64, 0:64], HSl[:, 0:64], start=False,
                    stop=True), sempe, 1)
                wown("vector")
                emit("vector", lambda e, r=r: e.tensor_scalar(
                    tmp64a[:, :], Hacc[:, :], oasel[:, r:r + 1], None, ALU.mult),
                    semv, 1)
                wait("vector", sempe)
                wown("vector")
                emit("vector", lambda e, r=r, psc=psc: e.scalar_tensor_tensor(
                    Hacc[:, :], psc[:, 0:64], asel[:, r:r + 1], tmp64a[:, :],
                    ALU.mult, ALU.add), PC.sem, 1)
                wait("scalar", semv)
                wown("scalar")
                emit("scalar", lambda e: e.activation(Hah[:, :], Hacc[:, :],
                                                      AF.Copy), sema, 1)
                wown("scalar")
                emit("scalar", lambda e: e.activation(tmp64b[:, :], Hah[:, :],
                                                      AF.Copy), sema, 1)
                wait("vector", sema)
                wown("vector")
                emit("vector", lambda e: e.tensor_tensor(
                    Hal[:, :], Hacc[:, :], tmp64b[:, :], ALU.subtract), semv, 1)
                wait("tensor", semv, semv.n)
                wait("tensor", sema, sema.n)
            return semv.n

        hiloa1_of = {}
        hiloa2_of = {}
        compose_init(semv.n, sema.n)
        for k in range(NCH // 2):
            cs = [(2 * k, PP[0]), (2 * k + 1, PP[1])]
            r1 = [prep_s1(c, P, pv[P["ptlo"]]) for c, P in cs]
            prep_s2pair(cs, [r[0] for r in r1], [r[1] for r in r1])
            bvs = [prep_s3(c, P, pv[P["ptlo"]], m1v)
                   for (c, P), (m1v, _) in zip(cs, r1)]
            bn_a = prep_s4pair(cs, bvs)
            for c, P in cs:
                prep_s4b(c, P, bn_a)
            b2a = [prep_s5(c, P) for c, P in cs]
            r1a = [prep_s6(c, P, a) for (c, P), a in zip(cs, b2a)]
            t1a = [prep_s7(c, P, a) for (c, P), a in zip(cs, r1a)]
            for (c, P), a in zip(cs, t1a):
                prep_s8(c, P, a)
            if k >= 1:
                compose_step(2 * (k - 1), ET1_s, PC, hiloa1_of[2 * (k - 1)])
                compose_step(2 * (k - 1) + 1, ET1_s, PC,
                             hiloa1_of[2 * (k - 1) + 1])
        compose_step(NCH - 2, ET1_s, PC, hiloa1_of[NCH - 2])
        compose_step(NCH - 1, ET1_s, PC, hiloa1_of[NCH - 1])

        for tname, src_ap in (("ktil", Ktil_s[:, :]), ("bt", AR[:, 0:2048]),
                              ("t1n", AR[:, 2048:4096]),
                              ("et1h", AR[0:64, 19456:20480]),
                              ("f1h", AR[0:64, 21504:22528]),
                              ("vtmf", ARF[:, 7680:8704]), ("m1", M1_t[:, :])):
            if tname in taps:
                wait("sync", semv, semv.n)
                wait("sync", sema, sema.n)
                wait("sync", sempe, sempe.n)
                emit("sync", lambda e, t=tname, s=src_ap: e.dma_start(
                    out=tap_aps[t], in_=s), semst, 16)
        if phases < 1.6:
            _replay(nc, q, semst, semcc)
            return nc
        hs1_v = exchange_finish(0)
        if "seg1" in taps:
            wait("sync", sema, sema.n)
            emit("sync", lambda e: e.dma_start(out=tap_aps["seg1"], in_=segT[:]),
                 semst, 16)
        if "gath1" in taps:
            wait("sync", semcc, semcc.n)
            emit("sync", lambda e: e.dma_start(out=tap_aps["gath1"], in_=gath[:]),
                 semst, 16)
        if "hstart1" in taps:
            wait("sync", semv, hs1_v)
            emit("sync", lambda e: e.dma_start(out=tap_aps["hstart1"], in_=Hacc[:]),
                 semst, 16)

        slot_v = [0] * NCH
        last_u_rd = [0, 0]
        last_db_rd = [0, 0]
        last_hilo_a = [0, 0]
        last_dlt_rd = [0, 0]
        last_xn_rd = [0, 0]
        last_t2_rd = [0, 0]
        last_wb2_rd = [0, 0]
        last_gt_rd = [0, 0]
        sig_a = [0] * NCH
        pred_pe_of = [0] * NCH
        ttr_v2 = [0] * NCH
        rstd_a2 = [0] * NCH
        last_ynb_rd = [0, 0]
        last_yn1_rd = [0, 0]

        def spine_seed(hs_v):
            wait("scalar", semv, hs_v)
            emit("scalar", lambda e: e.activation(H0b_s(0), Hacc[:, :], AF.Copy),
                 sema, 1)
            wown("scalar")
            emit("scalar", lambda e: e.activation(stmp, H0b_s(0), AF.Copy), sema, 1)
            h_a = sema.n
            wait("vector", sema, h_a)
            emit("vector", lambda e: e.tensor_tensor(
                H0l_s(0), Hacc[:, :], stmp, ALU.subtract), semv, 1)
            slot_v[0] = semv.n

        def spine_step(c, ST):
            ETh, ETl, Fh, Fl = ST
            psc = pe_bank(PC)
            wait("tensor", semv, slot_v[c - 1])
            emit("tensor", lambda e, c=c, psc=psc: e.matmul(
                psc[:, 0:64], ETh(c - 1), H0b_s(c - 1), start=True, stop=False),
                sempe, 1)
            emit("tensor", lambda e, c=c, psc=psc: e.matmul(
                psc[:, 0:64], ETh(c - 1), H0l_s(c - 1), start=False, stop=False),
                sempe, 1)
            emit("tensor", lambda e, c=c, psc=psc: e.matmul(
                psc[:, 0:64], ETl(c - 1), H0b_s(c - 1), start=False, stop=False),
                sempe, 1)
            emit("tensor", lambda e, c=c, psc=psc: e.matmul(
                psc[:, 0:64], identb[0:64, 0:64], Fh(c - 1), start=False,
                stop=False), sempe, 1)
            emit("tensor", lambda e, c=c, psc=psc: e.matmul(
                psc[:, 0:64], identb[0:64, 0:64], Fl(c - 1), start=False,
                stop=True), sempe, 1)
            adv_pe = sempe.n
            wait("scalar", sempe, adv_pe)
            emit("scalar", lambda e, c=c, psc=psc: e.activation(
                H0b_s(c), psc[:, 0:64], AF.Copy), sema, 1)
            wown("scalar")
            emit("scalar", lambda e, c=c: e.activation(stmp, H0b_s(c), AF.Copy),
                 sema, 1)
            h_a = sema.n
            wait("vector", sema, h_a)
            emit("vector", lambda e, c=c, psc=psc: e.tensor_tensor(
                H0l_s(c), psc[:, 0:64], stmp, ALU.subtract), PC.sem, 1)
            slot_v[c] = semv.n

        def tail_DU(c, Tt_s):
            b = c % 2
            psd = pe_bank(PS0)
            wait("tensor", semv, slot_v[c])
            emit("tensor", lambda e, c=c, psd=psd: e.matmul(
                psd[:, 0:64], csl(Ktil_s, c), H0b_s(c), start=True, stop=False),
                sempe, 1)
            emit("tensor", lambda e, c=c, psd=psd: e.matmul(
                psd[:, 0:64], csl(Ktil_s, c), H0l_s(c), start=False, stop=True),
                sempe, 1)
            d_pe = sempe.n
            wait("vector", sempe, d_pe)
            if last_db_rd[b]:
                wait("vector", sempe, last_db_rd[b])
            emit("vector", lambda e, c=c, b=b, psd=psd: e.scalar_tensor_tensor(
                Db_x[b], psd[:, 0:64], -1.0, Vtmf_s(c), ALU.mult, ALU.add),
                PS0.sem, 1)
            d_v = semv.n
            psu = pe_bank(PS1)
            wait("tensor", semv, d_v)
            emit("tensor", lambda e, c=c, b=b, psu=psu: e.matmul(
                psu[:, 0:64], Tt_s(c), Db_x[b], start=True, stop=True), sempe, 1)
            u_pe = sempe.n
            last_db_rd[b] = u_pe
            wait("scalar", sempe, u_pe)
            if last_u_rd[b]:
                wait("scalar", sempe, last_u_rd[b])
            emit("scalar", lambda e, b=b, psu=psu: e.activation(
                Ub_x[b], psu[:, 0:64], AF.Copy), PS1.sem, 1)
            return sema.n

        def tail_pred(c):
            b = c % 2
            u_a = tail_DU(c, T1t_s)
            psp = pe_bank(PS2, 1)
            wait("tensor", sema, u_a)
            emit("tensor", lambda e, c=c, psp=psp: e.matmul(
                psp[:, 0:64], csl(Qtil_s, c), H0b_s(c), start=True, stop=False),
                sempe, 1)
            emit("tensor", lambda e, c=c, psp=psp: e.matmul(
                psp[:, 0:64], csl(Qtil_s, c), H0l_s(c), start=False, stop=False),
                sempe, 1)
            emit("tensor", lambda e, c=c, b=b, psp=psp: e.matmul(
                psp[:, 0:64], P1m_s(c), Ub_x[b], start=False, stop=True), sempe, 1)
            p_pe = sempe.n
            last_u_rd[b] = p_pe
            pred_pe_of[c] = p_pe
            wait("vector", sempe, p_pe)
            emit("vector", lambda e, c=c, b=b, psp=psp: e.scalar_tensor_tensor(
                ef_x[b], psp[:, 0:64], -1.0, Vtmf_s(c), ALU.mult, ALU.add),
                PS2.sem, 1)
            wown("vector")
            emit("vector", lambda e, b=b: e.tensor_tensor(
                ysq[:, :], ef_x[b], ef_x[b], ALU.mult), semv, 1)
            if c >= 2 and sig_a[c - 2]:
                wait("vector", sema, sig_a[c - 2])
            wown("vector")
            emit("vector", lambda e, b=b: e.tensor_reduce(
                err_x[b], ysq[:, :], mybir.AxisListType.X, ALU.add), semv, 1)
            err_v = semv.n
            wait("scalar", semv, err_v)
            emit("scalar", lambda e, c=c, b=b: e.activation(
                scol[:, c:c + 1], err_x[b], AF.Sigmoid, scale=1.0 / STEMP),
                sema, 1)
            sig_a[c] = sema.n

        def prep2(c):
            b = c % 2
            wait("vector", sema, sig_a[c])
            emit("vector", lambda e, c=c, b=b: e.tensor_scalar(
                sm1_x[b], scol[:, c:c + 1], 1.0, None, ALU.subtract), semv, 1)
            if last_dlt_rd[b]:
                wait("vector", sempe, last_dlt_rd[b])
            wown("vector")
            emit("vector", lambda e, c=c, b=b: e.tensor_scalar(
                Dlt_x[b], Bt_s(c), sm1_x[b], None, ALU.mult), semv, 1)
            if last_wb2_rd[b]:
                wait("vector", sempe, last_wb2_rd[b])
            emit("vector", lambda e, c=c, b=b: e.tensor_scalar(
                Wb2_x[b], Wbtm_s(c), scol[:, c:c + 1], None, ALU.mult), semv, 1)
            wait("vector", sempe, pred_pe_of[c])
            emit("vector", lambda e, c=c: e.tensor_scalar(
                P1m_s(c), Ph1m_s(c), scol[:, c:c + 1], None, ALU.mult), semv, 1)
            dl_v = semv.n
            psx = pe_bank(PS0, 1)
            wait("tensor", semv, dl_v)
            emit("tensor", lambda e, c=c, b=b, psx=psx: e.matmul(
                psx[:, 0:128], Dlt_x[b], T1n_s(c), start=True, stop=True),
                sempe, 1)
            x_pe = sempe.n
            last_dlt_rd[b] = x_pe
            wait("scalar", sempe, x_pe)
            if last_xn_rd[b]:
                wait("scalar", sempe, last_xn_rd[b])
            emit("scalar", lambda e, b=b, psx=psx: e.activation(
                Xn_x[b], psx[:, 0:128], AF.Copy), PS0.sem, 1)
            x_a = sema.n
            pst = pe_bank(PS1, 1)
            wait("tensor", sema, x_a)
            emit("tensor", lambda e, c=c, b=b, pst=pst: e.matmul(
                pst[:, 0:128], T1t_s(c), Xn_x[b], start=True, stop=False), sempe, 1)
            emit("tensor", lambda e, c=c, pst=pst: e.matmul(
                pst[:, 0:128], identb[:, :], T1n_s(c), start=False, stop=True),
                sempe, 1)
            t2_pe = sempe.n
            last_xn_rd[b] = t2_pe
            wait("scalar", sempe, t2_pe)
            if last_t2_rd[b]:
                wait("scalar", sempe, last_t2_rd[b])
            emit("scalar", lambda e, b=b, pst=pst: e.activation(
                T2n_x[b], pst[:, 0:128], AF.Copy), PS1.sem, 1)
            t2n_a = sema.n
            pt2 = pe_bank(PTR, 1)[:].bitcast(BF16)
            wait("tensor", sema, t2n_a)
            emit("tensor", lambda e, b=b, pt2=pt2: e.transpose(
                pt2[:, 0:128], T2n_x[b], identb[:, :]), sempe, 1)
            wait("scalar", sempe)
            emit("scalar", lambda e, c=c, pt2=pt2: e.activation(
                T2t_s(c), pt2[:, 0:128], AF.Copy), PTR.sem, 1)
            psg = pe_bank(PS2, 1)
            emit("tensor", lambda e, b=b, psg=psg: e.matmul(
                psg[:, 0:64], T2n_x[b], Wb2_x[b], start=True, stop=True), sempe, 1)
            g_pe = sempe.n
            last_t2_rd[b] = g_pe
            last_wb2_rd[b] = g_pe
            wait("scalar", sempe, g_pe)
            if last_gt_rd[b]:
                wait("scalar", sempe, last_gt_rd[b])
            emit("scalar", lambda e, b=b, psg=psg: e.activation(
                Gt_x[b], psg[:, 0:64], AF.Copy), PS2.sem, 1)
            g_a = sema.n
            psc2 = pe_bank(PB0, 2)
            wait("tensor", sema, g_a)
            emit("tensor", lambda e, c=c, b=b, psc2=psc2: e.matmul(
                psc2[0:64, 0:64], Ktm_s(c), Gt_x[b], start=True, stop=False),
                sempe, 1)
            emit("tensor", lambda e, c=c, b=b, psc2=psc2: e.matmul(
                psc2[0:64, 64:128], Gt_x[b], Vtmb_s(c), start=False, stop=True),
                sempe, 1)
            ef_pe = sempe.n
            last_gt_rd[b] = ef_pe
            wait("vector", sempe, ef_pe)
            if last_hilo_a[b]:
                wait("vector", sema, last_hilo_a[b])
            emit("vector", lambda e, c=c, b=b, psc2=psc2: e.scalar_tensor_tensor(
                t64b_x[b], psc2[0:64, 0:64], -1.0, diagG_s(c), ALU.mult, ALU.add),
                PB0.sem, 1)
            emit("vector", lambda e, b=b, psc2=psc2: e.tensor_copy(
                t64c_x[b], psc2[0:64, 64:128]), PB0.sem, 1)
            hilo(t64b_x[b], ET2h_s(c), ET2l_s(c), t64a_x[b])
            hilo(t64c_x[b], F2h_s(c), F2l_s(c), t64a_x[b])
            last_hilo_a[b] = sema.n
            hiloa2_of[c] = (sema.n, semv.n)

        def tail_y(c):
            b = c % 2
            u_a = tail_DU(c, T2t_s)
            psy = pe_bank(PS2, 1)
            wait("tensor", sema, u_a)
            emit("tensor", lambda e, c=c, psy=psy: e.matmul(
                psy[:, 0:64], csl(Qhat_s, c), H0b_s(c), start=True, stop=False),
                sempe, 1)
            emit("tensor", lambda e, c=c, psy=psy: e.matmul(
                psy[:, 0:64], csl(Qhat_s, c), H0l_s(c), start=False, stop=False),
                sempe, 1)
            emit("tensor", lambda e, c=c, b=b, psy=psy: e.matmul(
                psy[:, 0:64], P1m_s(c), Ub_x[b], start=False, stop=True), sempe, 1)
            y_pe = sempe.n
            last_u_rd[b] = y_pe
            wait("scalar", sempe, y_pe)
            if c >= 2 and ttr_v2[c - 2]:
                wait("scalar", semv, ttr_v2[c - 2])
            if last_yn1_rd[b]:
                wait("scalar", semv, last_yn1_rd[b])
            emit("scalar", lambda e, b=b, psy=psy: e.activation(
                ytm_x[b], psy[:, 0:64], AF.Copy), PS2.sem, 1)
            ytm_a = sema.n
            if "ytm" in taps:
                wait("sync", sema, ytm_a)
                emit("sync", lambda e, c=c, b=b: e.dma_start(
                    out=tap_aps["ytm"][:, c * DS:(c + 1) * DS], in_=ytm_x[b]),
                    semst, 16)
            wait("vector", sema, ytm_a)
            wown("vector")
            emit("vector", lambda e, b=b: e.tensor_tensor(
                ysq[:, :], ytm_x[b], ytm_x[b], ALU.mult), semv, 1)
            wown("vector")
            emit("vector", lambda e, b=b: e.tensor_reduce(
                ms_x[b], ysq[:, :], mybir.AxisListType.X, ALU.add), semv, 1)
            ttr_v2[c] = semv.n
            wait("scalar", semv, ttr_v2[c])
            emit("scalar", lambda e, b=b: e.activation(
                rs_x[b], ms_x[b], AF.Ln, scale=1.0 / DS, bias=cbias[:, 2:3]),
                sema, 1)
            wown("scalar")
            emit("scalar", lambda e, b=b: e.activation(
                rs_x[b], rs_x[b], AF.Exp, scale=-0.5), sema, 1)
            wown("scalar")
            emit("scalar", lambda e, b=b: e.activation(
                yn1_x[b], ytm_x[b], AF.Copy, scale=rs_x[b]), sema, 1)
            rstd_a2[c] = sema.n
            wait("vector", sema, rstd_a2[c])
            if last_ynb_rd[b]:
                wait("vector", sempe, last_ynb_rd[b])
            emit("vector", lambda e, b=b: e.tensor_tensor(
                ynb_x[b], yn1_x[b], normw_t[:, :], ALU.mult), semv, 1)
            ynb_v = semv.n
            last_yn1_rd[b] = ynb_v
            pty = pe_bank(PTR, 1)[:].bitcast(BF16)
            wait("tensor", semv, ynb_v)
            emit("tensor", lambda e, b=b, pty=pty: e.transpose(
                pty[0:64, 0:128], ynb_x[b], identb[:, :]), sempe, 1)
            ytr = sempe.n
            last_ynb_rd[b] = ytr
            wait("scalar", sempe, ytr)
            emit("scalar", lambda e, c=c, pty=pty: e.activation(
                ynfm[:, c * L:(c + 1) * L], pty[0:64, 0:128], AF.Copy), PTR.sem, 1)

        if phases < 1.9:
            _replay(nc, q, semst, semcc)
            return nc
        wait("scalar", semw2b[0], semw2b[0].n)
        wait("scalar", semw2b[1], semw2b[1].n)
        wait("vector", semw2b[0], semw2b[0].n)
        wait("vector", semw2b[1], semw2b[1].n)
        d_v1 = [0] * NCH
        u_a1 = [0] * NCH
        dl_v1 = [0] * NCH
        x_a1 = [0] * NCH
        t2n_a1 = [0] * NCH
        g_a1 = [0] * NCH

        def s1_D(c, Tt_s):
            b = c % 2
            psd = pe_bank(PS0)
            wait("tensor", semv, slot_v[c])
            emit("tensor", lambda e, c=c, psd=psd: e.matmul(
                psd[:, 0:64], csl(Ktil_s, c), H0b_s(c), start=True, stop=False),
                sempe, 1)
            emit("tensor", lambda e, c=c, psd=psd: e.matmul(
                psd[:, 0:64], csl(Ktil_s, c), H0l_s(c), start=False, stop=True),
                sempe, 1)
            d_pe = sempe.n
            wait("vector", sempe, d_pe)
            if last_db_rd[b]:
                wait("vector", sempe, last_db_rd[b])
            emit("vector", lambda e, c=c, b=b, psd=psd: e.scalar_tensor_tensor(
                Db_x[b], psd[:, 0:64], -1.0, Vtmf_s(c), ALU.mult, ALU.add),
                PS0.sem, 1)
            d_v1[c] = semv.n

        def s2_U(c, Tt_s):
            b = c % 2
            psu = pe_bank(PS1)
            wait("tensor", semv, d_v1[c])
            emit("tensor", lambda e, c=c, b=b, psu=psu: e.matmul(
                psu[:, 0:64], Tt_s(c), Db_x[b], start=True, stop=True), sempe, 1)
            u_pe = sempe.n
            last_db_rd[b] = u_pe
            wait("scalar", sempe, u_pe)
            if last_u_rd[b]:
                wait("scalar", sempe, last_u_rd[b])
            emit("scalar", lambda e, b=b, psu=psu: e.activation(
                Ub_x[b], psu[:, 0:64], AF.Copy), PS1.sem, 1)
            u_a1[c] = sema.n

        def s3_pred(c):
            b = c % 2
            psp = pe_bank((PS2, PB1)[b], 1)
            wait("tensor", sema, u_a1[c])
            emit("tensor", lambda e, c=c, psp=psp: e.matmul(
                psp[:, 0:64], csl(Qtil_s, c), H0b_s(c), start=True, stop=False),
                sempe, 1)
            emit("tensor", lambda e, c=c, psp=psp: e.matmul(
                psp[:, 0:64], csl(Qtil_s, c), H0l_s(c), start=False, stop=False),
                sempe, 1)
            emit("tensor", lambda e, c=c, b=b, psp=psp: e.matmul(
                psp[:, 0:64], P1m_s(c), Ub_x[b], start=False, stop=True), sempe, 1)
            p_pe = sempe.n
            last_u_rd[b] = p_pe
            pred_pe_of[c] = p_pe

        def s4_err(c):
            b = c % 2
            psp = (PS2, PB1)[b].t
            wait("vector", sempe, pred_pe_of[c])
            emit("vector", lambda e, c=c, b=b, psp=psp: e.scalar_tensor_tensor(
                ef_x[b], psp[:, 0:64], -1.0, Vtmf_s(c), ALU.mult, ALU.add),
                (PS2, PB1)[b].sem, 1)
            wown("vector")
            emit("vector", lambda e, b=b: e.tensor_tensor(
                ysq[:, :], ef_x[b], ef_x[b], ALU.mult), semv, 1)
            if c >= 2 and sig_a[c - 2]:
                wait("vector", sema, sig_a[c - 2])
            wown("vector")
            emit("vector", lambda e, b=b: e.tensor_reduce(
                err_x[b], ysq[:, :], mybir.AxisListType.X, ALU.add), semv, 1)
            err_v = semv.n
            wait("scalar", semv, err_v)
            emit("scalar", lambda e, c=c, b=b: e.activation(
                scol[:, c:c + 1], err_x[b], AF.Sigmoid, scale=1.0 / STEMP),
                sema, 1)
            sig_a[c] = sema.n

        def s5_dl(c):
            b = c % 2
            wait("vector", sema, sig_a[c])
            emit("vector", lambda e, c=c, b=b: e.tensor_scalar(
                sm1_x[b], scol[:, c:c + 1], 1.0, None, ALU.subtract), semv, 1)
            if last_dlt_rd[b]:
                wait("vector", sempe, last_dlt_rd[b])
            wown("vector")
            emit("vector", lambda e, c=c, b=b: e.tensor_scalar(
                Dlt_x[b], Bt_s(c), sm1_x[b], None, ALU.mult), semv, 1)
            wait("vector", sempe, pred_pe_of[c])
            emit("vector", lambda e, c=c: e.tensor_scalar(
                P1m_s(c), Ph1m_s(c), scol[:, c:c + 1], None, ALU.mult), semv, 1)
            dl_v1[c] = semv.n

        def s6_X(c):
            b = c % 2
            psx = pe_bank(PB0, 1)
            wait("tensor", semv, dl_v1[c])
            emit("tensor", lambda e, c=c, b=b, psx=psx: e.matmul(
                psx[:, 0:128], Dlt_x[b], T1n_s(c), start=True, stop=True),
                sempe, 1)
            x_pe = sempe.n
            last_dlt_rd[b] = x_pe
            wait("scalar", sempe, x_pe)
            if last_xn_rd[b]:
                wait("scalar", sempe, last_xn_rd[b])
            emit("scalar", lambda e, b=b, psx=psx: e.activation(
                Xn_x[b], psx[:, 0:128], AF.Copy), PB0.sem, 1)
            x_a1[c] = sema.n

        def s7_T2(c):
            b = c % 2
            pst = pe_bank(PS0, 1)
            wait("tensor", sema, x_a1[c])
            emit("tensor", lambda e, c=c, b=b, pst=pst: e.matmul(
                pst[:, 0:128], T1t_s(c), Xn_x[b], start=True, stop=False),
                sempe, 1)
            emit("tensor", lambda e, c=c, pst=pst: e.matmul(
                pst[:, 0:128], identb[:, :], T1n_s(c), start=False, stop=True),
                sempe, 1)
            t2_pe = sempe.n
            last_xn_rd[b] = t2_pe
            wait("scalar", sempe, t2_pe)
            if last_t2_rd[b]:
                wait("scalar", sempe, last_t2_rd[b])
            emit("scalar", lambda e, b=b, pst=pst: e.activation(
                T2n_x[b], pst[:, 0:128], AF.Copy), PS0.sem, 1)
            if last_wb2_rd[b]:
                wait("scalar", sempe, last_wb2_rd[b])
            wait("scalar", sema, sig_a[c])
            emit("scalar", lambda e, c=c, b=b: e.activation(
                Wb2_x[b], Wbtm_s(c), AF.Copy, scale=scol[:, c:c + 1]), sema, 1)
            t2n_a1[c] = sema.n

        def s8_TRG(c):
            b = c % 2
            pt2 = pe_bank(PTR, 1)[:].bitcast(BF16)
            wait("tensor", sema, t2n_a1[c])
            emit("tensor", lambda e, b=b, pt2=pt2: e.transpose(
                pt2[:, 0:128], T2n_x[b], identb[:, :]), sempe, 1)
            wait("scalar", sempe)
            emit("scalar", lambda e, c=c, pt2=pt2: e.activation(
                T2t_s(c), pt2[:, 0:128], AF.Copy), PTR.sem, 1)
            psg = pe_bank(PROW, 1)
            emit("tensor", lambda e, b=b, psg=psg: e.matmul(
                psg[:, 0:64], T2n_x[b], Wb2_x[b], start=True, stop=True), sempe, 1)
            g_pe = sempe.n
            last_t2_rd[b] = g_pe
            last_wb2_rd[b] = g_pe
            wait("scalar", sempe, g_pe)
            if last_gt_rd[b]:
                wait("scalar", sempe, last_gt_rd[b])
            emit("scalar", lambda e, b=b, psg=psg: e.activation(
                Gt_x[b], psg[:, 0:64], AF.Copy), PROW.sem, 1)
            g_a1[c] = sema.n

        def s9_EF(c):
            b = c % 2
            psc2 = pe_bank(PS1, 2)
            wait("tensor", sema, g_a1[c])
            emit("tensor", lambda e, c=c, b=b, psc2=psc2: e.matmul(
                psc2[0:64, 0:64], Ktm_s(c), Gt_x[b], start=True, stop=False),
                sempe, 1)
            emit("tensor", lambda e, c=c, b=b, psc2=psc2: e.matmul(
                psc2[0:64, 64:128], Gt_x[b], Vtmb_s(c), start=False, stop=True),
                sempe, 1)
            ef_pe = sempe.n
            last_gt_rd[b] = ef_pe
            wait("vector", sempe, ef_pe)
            if last_hilo_a[b]:
                wait("vector", sema, last_hilo_a[b])
            emit("vector", lambda e, c=c, b=b, psc2=psc2: e.scalar_tensor_tensor(
                t64b_x[b], psc2[0:64, 0:64], -1.0, diagG_s(c), ALU.mult, ALU.add),
                PS1.sem, 1)
            ef_v = semv.n
            wait("scalar", semv, ef_v)
            emit("scalar", lambda e, b=b, psc2=psc2: e.activation(
                t64c_x[b], psc2[0:64, 64:128], AF.Copy), PS1.sem, 1)
            hilo(t64b_x[b], ET2h_s(c), ET2l_s(c), t64a_x[b])
            hilo(t64c_x[b], F2h_s(c), F2l_s(c), t64a_x[b])
            last_hilo_a[b] = sema.n
            hiloa2_of[c] = (sema.n, semv.n)

        compose_init(semv.n, sema.n)
        spine_seed(hs1_v)
        for i in range(NCH + 9):
            if i + 1 < NCH:
                spine_step(i + 1, ET1_s)
            if i < NCH:
                s1_D(i, T1t_s)
            if 1 <= i < NCH + 1:
                s2_U(i - 1, T1t_s)
            if 2 <= i < NCH + 2:
                s3_pred(i - 2)
            if 3 <= i < NCH + 3:
                s4_err(i - 3)
            if 4 <= i < NCH + 4:
                s5_dl(i - 4)
            if 5 <= i < NCH + 5:
                s6_X(i - 5)
            if 6 <= i < NCH + 6:
                s7_T2(i - 6)
            if 7 <= i < NCH + 7:
                s8_TRG(i - 7)
            if 8 <= i < NCH + 8:
                s9_EF(i - 8)
            if 9 <= i < NCH + 9:
                compose_step(i - 9, ET2_s, PROW, hiloa2_of[i - 9])
        s_a = sema.n
        if "scol" in taps:
            wait("sync", sema, s_a)
            emit("sync", lambda e: e.dma_start(out=tap_aps["scol"], in_=scol[:]),
                 semst, 16)

        if phases < 3:
            _replay(nc, q, semst, semcc)
            return nc
        hs2_v = exchange_finish(1)
        u_a2 = [0] * NCH
        y_pe2 = [0] * NCH
        ytm_a2 = [0] * NCH

        def s2_U2(c):
            b = c % 2
            psu = pe_bank(PS1)
            wait("tensor", semv, d_v1[c])
            emit("tensor", lambda e, c=c, b=b, psu=psu: e.matmul(
                psu[:, 0:64], T2t_s(c), Db_x[b], start=True, stop=True), sempe, 1)
            u_pe = sempe.n
            last_db_rd[b] = u_pe
            wait("scalar", sempe, u_pe)
            if last_u_rd[b]:
                wait("scalar", sempe, last_u_rd[b])
            emit("scalar", lambda e, b=b, psu=psu: e.activation(
                Ub_x[b], psu[:, 0:64], AF.Copy), PS1.sem, 1)
            u_a2[c] = sema.n

        def s3_y(c):
            b = c % 2
            psy = pe_bank((PS2, PB1)[b], 1)
            wait("tensor", sema, u_a2[c])
            emit("tensor", lambda e, c=c, psy=psy: e.matmul(
                psy[:, 0:64], csl(Qhat_s, c), H0b_s(c), start=True, stop=False),
                sempe, 1)
            emit("tensor", lambda e, c=c, psy=psy: e.matmul(
                psy[:, 0:64], csl(Qhat_s, c), H0l_s(c), start=False, stop=False),
                sempe, 1)
            emit("tensor", lambda e, c=c, b=b, psy=psy: e.matmul(
                psy[:, 0:64], P1m_s(c), Ub_x[b], start=False, stop=True), sempe, 1)
            y_pe2[c] = sempe.n
            last_u_rd[b] = y_pe2[c]

        def s4_ms(c):
            b = c % 2
            psy = (PS2, PB1)[b].t
            wait("scalar", sempe, y_pe2[c])
            if c >= 2 and ttr_v2[c - 2]:
                wait("scalar", semv, ttr_v2[c - 2])
            if last_yn1_rd[b]:
                wait("scalar", semv, last_yn1_rd[b])
            if c >= 2 and rstd_a2[c - 2]:
                wait("scalar", sema, rstd_a2[c - 2])
            emit("scalar", lambda e, b=b, psy=psy: e.activation(
                ytm_x[b], psy[:, 0:64], AF.Copy), (PS2, PB1)[b].sem, 1)
            ytm_a2[c] = sema.n
            if "ytm" in taps:
                wait("sync", sema, ytm_a2[c])
                emit("sync", lambda e, c=c, b=b: e.dma_start(
                    out=tap_aps["ytm"][:, c * DS:(c + 1) * DS], in_=ytm_x[b]),
                    semst, 16)
            wait("vector", sema, ytm_a2[c])
            wown("vector")
            emit("vector", lambda e, b=b: e.tensor_tensor(
                ysq[:, :], ytm_x[b], ytm_x[b], ALU.mult), semv, 1)
            wown("vector")
            emit("vector", lambda e, b=b: e.tensor_reduce(
                ms_x[b], ysq[:, :], mybir.AxisListType.X, ALU.add), semv, 1)
            ttr_v2[c] = semv.n

        def s5_rs(c):
            b = c % 2
            wait("scalar", semv, ttr_v2[c])
            emit("scalar", lambda e, b=b: e.activation(
                rs_x[b], ms_x[b], AF.Ln, scale=1.0 / DS, bias=cbias[:, 2:3]),
                sema, 1)
            wown("scalar")
            emit("scalar", lambda e, b=b: e.activation(
                rs_x[b], rs_x[b], AF.Exp, scale=-0.5), sema, 1)
            wown("scalar")
            emit("scalar", lambda e, b=b: e.activation(
                yn1_x[b], ytm_x[b], AF.Copy, scale=rs_x[b]), sema, 1)
            rstd_a2[c] = sema.n

        def s6_out(c):
            b = c % 2
            wait("vector", sema, rstd_a2[c])
            if last_ynb_rd[b]:
                wait("vector", sempe, last_ynb_rd[b])
            emit("vector", lambda e, b=b: e.tensor_tensor(
                ynb_x[b], yn1_x[b], normw_t[:, :], ALU.mult), semv, 1)
            ynb_v = semv.n
            last_yn1_rd[b] = ynb_v
            pty = pe_bank(PTR, 1)[:].bitcast(BF16)
            wait("tensor", semv, ynb_v)
            emit("tensor", lambda e, b=b, pty=pty: e.transpose(
                pty[0:64, 0:128], ynb_x[b], identb[:, :]), sempe, 1)
            ytr = sempe.n
            last_ynb_rd[b] = ytr
            wait("scalar", sempe, ytr)
            emit("scalar", lambda e, c=c, pty=pty: e.activation(
                ynfm[:, c * L:(c + 1) * L], pty[0:64, 0:128], AF.Copy), PTR.sem, 1)

        spine_seed(hs2_v)
        for i in range(NCH + 6):
            if i + 1 < NCH:
                spine_step(i + 1, ET2_s)
            if i < NCH:
                s1_D(i, T2t_s)
            if 1 <= i < NCH + 1:
                s2_U2(i - 1)
            if 2 <= i < NCH + 2:
                s3_y(i - 2)
            if 3 <= i < NCH + 3:
                s4_ms(i - 3)
            if 4 <= i < NCH + 4:
                s5_rs(i - 4)
            if 5 <= i < NCH + 5:
                s6_out(i - 5)
        scan_a = sema.n
        scan_v = semv.n
        scan_pe = sempe.n
        if "ynfm" in taps:
            wait("sync", sema, scan_a)
            emit("sync", lambda e: e.dma_start(out=tap_aps["ynfm"], in_=ynfm[:]),
                 semst, 16)

        if phases < 4:
            _replay(nc, q, semst, semcc)
            return nc
        for eng in ("tensor", "vector", "scalar", "gpsimd", "sync"):
            wait(eng, sema, scan_a)
            wait(eng, semv, scan_v)
            wait(eng, sempe, scan_pe)

        w2_pe = {}
        xr_tt = {}
        ob_last = [0, 0, 0]
        cast_a_q = [0] * 4
        x2n_rd_pe = [0] * 4
        mrs2_rd_pe = [0] * 4
        bcast_pe_q = [0] * 4
        stats_fin_v = [0] * 4
        rs2hl_q = [0] * 4
        FQ = 4
        x2q = lambda qq, o: ARF[:, (qq % 2) * 8192 + o * 512: (qq % 2) * 8192
                                + o * 512 + 512]
        x2nq = lambda qq, o: SCRB[:, (qq % 2) * 8192 + o * 512: (qq % 2) * 8192
                                  + o * 512 + 512]

        def attn_stats_tile(qq, o, ps_sx, ps_sq):
            t0 = qq * 512
            idx = qq * NDCH + o
            bank = (PB0, PB1, PTR)[idx % 3]
            ps = pe_bank(bank, 1)
            emit("tensor", lambda e, o=o, t0=t0, ps=ps: e.matmul(
                ps[:, :], wo_b[:, o * 128:(o + 1) * 128],
                ynfm[:, t0: t0 + 512], start=True, stop=True), sempe, 1)
            mm_pe = sempe.n
            rot = idx % 4
            xq_eng = "sync"
            if idx >= 4:
                wait(xq_eng, semv, xr_tt[idx - 4])
            emit(xq_eng, lambda e, o=o, t0=t0, rot=rot: e.dma_start(
                out=xrl[rot],
                in_=xf[o * 128:(o + 1) * 128, t0: t0 + 512]), semxr[rot], 16)
            xr_ld = semxr[rot].n
            wait("vector", sempe, mm_pe)
            wait("vector", semxr[rot], xr_ld)
            if qq >= 2 and o == 0:
                wait("vector", sema, cast_a_q[qq - 2])
                wown("vector")
            emit("vector", lambda e, qq=qq, o=o, rot=rot, ps=ps:
                 e.scalar_tensor_tensor(
                x2q(qq, o), ps[:, :], 1.0, xrl[rot], ALU.mult, ALU.add),
                bank.sem, 1)
            xr_tt[idx] = semv.n
            rot2 = idx % 2
            wait("scalar", semv, xr_tt[idx])
            if idx >= 2:
                wait("scalar", sempe, w2_pe.get(("cast", idx - 2), 0))
            emit("scalar", lambda e, qq=qq, o=o, rot2=rot2: e.activation(
                x2b_t[rot2], x2q(qq, o), AF.Copy), sema, 1)
            emit("scalar", lambda e, qq=qq, o=o, rot2=rot2: e.activation(
                x2sq_t[rot2], x2q(qq, o), AF.Square), sema, 1)
            w2_pe[("casta", idx)] = sema.n
            if o == NDCH - 1:
                cast_a_q[qq] = sema.n

        def stats_mms(qq, o, ps_sx, ps_sq):
            idx = qq * NDCH + o
            rot2 = idx % 2
            wait("tensor", sema, w2_pe[("casta", idx)])
            emit("tensor", lambda e, o=o, rot2=rot2, ps_sx=ps_sx: e.matmul(
                ps_sx[0:1, 0:512], onescol_b[:], x2b_t[rot2],
                start=(o == 0), stop=(o == NDCH - 1)), sempe, 1)
            emit("tensor", lambda e, o=o, rot2=rot2, ps_sq=ps_sq: e.matmul(
                ps_sq[0:1, 0:512], onescol_b[:], x2sq_t[rot2],
                start=(o == 0), stop=(o == NDCH - 1)), sempe, 1)
            w2_pe[("cast", idx)] = sempe.n

        def stats_finish(qq, ps_sx, ps_sq):
            st_pe = w2_pe[("cast", qq * NDCH + NDCH - 1)]
            wait("scalar", sempe, st_pe)
            if qq >= 1:
                wait("scalar", semv, rs2hl_q[qq - 1])
            emit("scalar", lambda e, ps_sx=ps_sx: e.activation(
                rowA[0:1, 0:512], ps_sx[0:1, 0:512], AF.Copy), PROW.sem, 1)
            wown("scalar")
            emit("scalar", lambda e: e.activation(
                rowA[0:1, 2048:2560], rowA[0:1, 0:512], AF.Square,
                scale=1.0 / D), sema, 1)
            m2_a = sema.n
            wait("vector", sema, m2_a)
            wait("vector", sempe, st_pe)
            emit("vector", lambda e, ps_sq=ps_sq: e.scalar_tensor_tensor(
                rowA[0:1, 2048:2560], ps_sq[0:1, 0:512], 1.0 / D,
                rowA[0:1, 2048:2560], ALU.mult, ALU.subtract), PC.sem, 1)
            var_v = semv.n
            wait("scalar", semv, var_v)
            emit("scalar", lambda e: e.activation(
                rowA[0:1, 2560:3072], rowA[0:1, 2048:2560], AF.Ln,
                bias=cbias[0:1, 1:2]), sema, 1)
            wown("scalar")
            emit("scalar", lambda e: e.activation(
                rowA[0:1, 2560:3072], rowA[0:1, 2560:3072], AF.Exp,
                scale=-0.5), sema, 1)
            rs2_a = sema.n
            wait("vector", sema, rs2_a)
            if qq >= 1 and mrs2_rd_pe[qq - 1]:
                wait("vector", sempe, mrs2_rd_pe[qq - 1])
            emit("vector", lambda e: e.tensor_tensor(
                rowC[0:1, 2048:2560], rowA[0:1, 0:512],
                rowA[0:1, 2560:3072], ALU.mult), semv, 1)
            if qq >= 1 and bcast_pe_q[qq - 1]:
                wait("scalar", sempe, bcast_pe_q[qq - 1])
            wown("scalar")
            emit("scalar", lambda e: e.activation(
                rowC[0:1, 3072:3584], rowA[0:1, 2560:3072], AF.Copy), sema, 1)
            wown("scalar")
            emit("scalar", lambda e: e.activation(
                rowA[0:1, 512:1024], rowC[0:1, 3072:3584], AF.Copy), sema, 1)
            wait("vector", sema)
            emit("vector", lambda e: e.tensor_tensor(
                rowC[0:1, 3584:4096], rowA[0:1, 2560:3072],
                rowA[0:1, 512:1024], ALU.subtract), semv, 1)
            rs2hl_q[qq] = semv.n
            psb = pe_bank(PTR, 1)
            wait("tensor", semv, rs2hl_q[qq])
            wait("tensor", sema)
            emit("tensor", lambda e, psb=psb: e.matmul(
                psb[:, 0:512], onesrow_b[0:1, :], rowC[0:1, 3072:3584],
                start=True, stop=False), sempe, 1)
            emit("tensor", lambda e, psb=psb: e.matmul(
                psb[:, 0:512], onesrow_b[0:1, :], rowC[0:1, 3584:4096],
                start=False, stop=True), sempe, 1)
            bcast_pe_q[qq] = sempe.n
            wait("vector", sempe, bcast_pe_q[qq])
            emit("vector", lambda e, psb=psb: e.tensor_copy(
                rs2b[:, :], psb[:, 0:512]), PTR.sem, 1)
            if qq >= 2 and x2n_rd_pe[qq - 2]:
                wait("vector", sempe, x2n_rd_pe[qq - 2])
            wown("vector")
            for o in range(NDCH):
                emit("vector", lambda e, qq=qq, o=o: e.tensor_tensor(
                    x2nq(qq, o), x2q(qq, o), rs2b[:, :], ALU.mult), semv, 1)
            stats_fin_v[qq] = semv.n

        def flush_store():
            if pending_store:
                o_v, sq, so, sj, sov = pending_store.pop(0)
                wait("sync", semv, o_v)
                emit("sync", lambda e, sq=sq, so=so, sov=sov: e.dma_start(
                    out=outfm[so * 128:(so + 1) * 128,
                              sq * 512:(sq + 1) * 512],
                    in_=sov[:, :]), semo[sj], 16)
                ob_last[sj] = semo[sj].n

        def ffn_tile(qq, o):
            flush_store()
            t = qq * NDCH + o
            wt = w2t[t % 2]
            if t == 0:
                wait("gpsimd", semw2b[0], semw2b[0].n)
                wait("gpsimd", semw2b[1], semw2b[1].n)
            if t >= 1:
                prev = w2_pe.get(("ffn", t - 2))
                if prev:
                    wait("gpsimd", sempe, prev)
            emit("gpsimd", lambda e, o=o, wt=wt: e.dma_start(
                out=wt[:], in_=w2bf[o * 128:(o + 1) * 128, :]), semw2[t % 2], 16)
            wld = semw2[t % 2].n
            wsem = semw2[t % 2]
            nsw_a = sema.n
            if qq == 0:
                psn = pe_bank(PTR, 1)
                wait("tensor", wsem, wld)
                for i in range(NDCH):
                    emit("tensor", lambda e, i=i, wt=wt, psn=psn: e.matmul(
                        psn[0:1, 0:128], onescol_b[:],
                        wt[:, i * 128:(i + 1) * 128],
                        start=(i == 0), stop=(i == NDCH - 1)), sempe, 1)
                wait("scalar", sempe)
                emit("scalar", lambda e, o=o, psn=psn: e.activation(
                    rowC[0:1, o * 128:(o + 1) * 128], psn[0:1, 0:128],
                    AF.Copy, scale=-1.0 / D), PTR.sem, 1)
                nsw_a = sema.n
            bank3 = (PS0, PS1, PS2)[t % 3]
            ps = pe_bank(bank3, 1)
            wait("tensor", wsem, wld)
            wait("tensor", semv, stats_fin_v[qq])
            if qq == 0:
                wait("tensor", sema, nsw_a)
            for d in range(NDCH):
                emit("tensor", lambda e, qq=qq, d=d, wt=wt, ps=ps: e.matmul(
                    ps[:, :], wt[:, d * 128:(d + 1) * 128], x2nq(qq, d),
                    start=(d == 0), stop=False), sempe, 1)
            emit("tensor", lambda e, o=o, ps=ps: e.matmul(
                ps[:, :], rowC[0:1, o * 128:(o + 1) * 128],
                rowC[0:1, 2048:2560], start=False, stop=True), sempe, 1)
            ffn_pe = sempe.n
            w2_pe[("ffn", t)] = ffn_pe
            x2n_rd_pe[qq] = ffn_pe
            mrs2_rd_pe[qq] = ffn_pe
            j = t % 3
            ov = out_bufs[j]
            wait("scalar", sempe, ffn_pe)
            if ob_last[j]:
                wait("scalar", semo[j], ob_last[j])
            emit("scalar", lambda e, o=o, ps=ps, ov=ov: e.activation(
                ov[:, :], ps[:, :], AF.Copy, scale=sw2_t[:, o:o + 1]),
                bank3.sem, 1)
            f1_a = sema.n
            wait("vector", sema, f1_a)
            emit("vector", lambda e, qq=qq, o=o, ov=ov: e.scalar_tensor_tensor(
                ov[:, :], ov[:, :], b2_t[:, o:o + 1], x2q(qq, o),
                ALU.add, ALU.add), semv, 1)
            o_v = semv.n
            pending_store.append((o_v, qq, o, j, ov))

        pending_store = []
        for qq in range(FQ + 1):
            ps_sx = ps_sq = None
            if qq < FQ:
                ps_sx = pe_bank(PROW, 1)
                ps_sq = pe_bank(PC, 1)
            for o in range(NDCH):
                if qq < FQ:
                    attn_stats_tile(qq, o, ps_sx, ps_sq)
                if qq >= 1:
                    ffn_tile(qq - 1, o)
                if qq < FQ and o >= 1:
                    stats_mms(qq, o - 1, ps_sx, ps_sq)
            if qq < FQ:
                stats_mms(qq, NDCH - 1, ps_sx, ps_sq)
                stats_finish(qq, ps_sx, ps_sq)
        flush_store()

        for j in range(3):
            wait("gpsimd", semo[j], semo[j].n)
        _replay(nc, q, semst, semcc)

    return nc



def _prep_shared(inputs):
    f32 = np.float32
    Wk = np.asarray(inputs["L0_attn_Wk"], f32)
    Wv = np.asarray(inputs["L0_attn_Wv"], f32)
    Wq = np.asarray(inputs["L0_attn_Wq"], f32)
    Wa = np.asarray(inputs["L0_attn_Wa_w"], f32)
    def tile_lhs(wt):
        F = wt.shape[1]
        return np.ascontiguousarray(
            wt.reshape(NDCH, 128, F).transpose(1, 0, 2).reshape(128, NDCH * F))

    w2w = np.asarray(inputs["L0_ffn_weight"], f32)
    w2h = np.ascontiguousarray(
        w2w.reshape(NDCH, 128, NDCH, 128).transpose(0, 3, 2, 1).reshape(
            NDCH * 128, NDCH * 128))
    shared = {
        "wkq": tile_lhs(np.concatenate([Wk, Wq], 0).T),
        "wav": tile_lhs(np.concatenate([Wa, Wv], 0).T),
        "wo": np.ascontiguousarray(np.asarray(inputs["L0_attn_Wo"], f32).T),
        "w2": w2h,
        "wab": np.ascontiguousarray(
            np.asarray(inputs["L0_attn_Wa_b"], f32).reshape(DS, 1)),
        "lam": np.ascontiguousarray(
            np.asarray(inputs["L0_attn_lam"], f32).reshape(DS, 1)),
        "normw": np.ascontiguousarray(np.broadcast_to(
            np.asarray(inputs["L0_attn_norm_w"], f32), (128, DS))),
        "sw2c": np.ascontiguousarray(
            np.asarray(inputs["L0_ffn_scale_weight"], f32).reshape(NDCH, 128).T),
        "b2c": np.ascontiguousarray(
            np.asarray(inputs["L0_ffn_bias"], f32).reshape(NDCH, 128).T),
        "identf": np.eye(128, dtype=f32),
        "mask_s": np.ascontiguousarray(np.triu(np.ones((128, 128), f32), 1)),
        "mask_i": np.ascontiguousarray(np.triu(np.ones((128, 128), f32), 0)),
    }
    return shared


def kernel(**inputs):
    x = np.asarray(inputs["L0_attn_x"], np.float32)[0]
    shared = _prep_shared(inputs)
    nc = build_program(taps=DEBUG_TAPS)
    in_maps = []
    for c in range(NCORES):
        sel = np.zeros((DS, NCORES), np.float32)
        sel[:, :c] = 1.0
        m = dict(shared)
        m["xf"] = np.ascontiguousarray(x[c * T:(c + 1) * T, :].T)
        m["alphasel"] = sel
        in_maps.append(m)
    res = run_bass_kernel_spmd(nc, in_maps, core_ids=list(range(NCORES)))
    out = np.concatenate(
        [np.asarray(res.results[c]["outfm"]).T for c in range(NCORES)], 0)
    return out.reshape(1, NCORES * T, D)


# version 18
# speedup vs baseline: 1.1779x; 1.0338x over previous
import contextlib
import numpy as np
import ml_dtypes
import concourse.bass as bass
import concourse.mybir as mybir
from concourse.bass_utils import run_bass_kernel_spmd

F32 = mybir.dt.float32
BF16 = mybir.dt.bfloat16
AF = mybir.ActivationFunctionType
ALU = mybir.AluOpType
BF = ml_dtypes.bfloat16

NCORES = 8
D = 2048
DS = 64
T = 2048
L = 128
NCH = T // L
NDCH = D // 128
C_GATE = 8.0
LN_EPS = 1e-5
RMS_EPS = 1e-6
STEMP = 1.0 + 1e-6

ENGINES = ("sync", "vector", "scalar", "tensor", "gpsimd")

DEBUG_TAPS = ()
OWN_WAIT = False
PREP_CUT = 99
PH4_CUT = 99


class Sem:
    __slots__ = ("h", "n")

    def __init__(self, h):
        self.h = h
        self.n = 0


class Bank:

    __slots__ = ("t", "sem", "pending")

    def __init__(self, t, sem=None):
        self.t = t
        self.sem = sem
        self.pending = {}


def _replay(nc, q, semst, semcc):
    if semst.n > 0:
        q["gpsimd"].append(lambda e, v=semst.n, s=semst: e.wait_ge(s.h, v))
    with nc.Block() as block:
        @block.sync
        def _(e):
            for f in q["sync"]:
                f(e)

        @block.vector
        def _(e):
            for f in q["vector"]:
                f(e)

        @block.scalar
        def _(e):
            for f in q["scalar"]:
                f(e)

        @block.tensor
        def _(e):
            for f in q["tensor"]:
                f(e)

        @block.gpsimd
        def _(e):
            for f in q["gpsimd"]:
                f(e)
    return None


def build_program(taps=(), phases=4):
    nc = bass.Bass("TRN2", target_bir_lowering=False, debug=False)

    def din(name, shape, dtype=F32):
        return nc.dram_tensor(name, list(shape), dtype, kind="ExternalInput").ap()

    def dout(name, shape, dtype=F32):
        return nc.dram_tensor(name, list(shape), dtype, kind="ExternalOutput").ap()

    xf = din("xf", [D, T])
    wkq = din("wkq", [128, NDCH * 128])
    wav = din("wav", [128, NDCH * 128])
    wo = din("wo", [DS, D])
    w2 = din("w2", [NDCH * 128, NDCH * 128])
    wab = din("wab", [DS, 1])
    lam = din("lam", [DS, 1])
    normw = din("normw", [128, DS])
    sw2c = din("sw2c", [128, NDCH])
    b2c = din("b2c", [128, NDCH])
    identf = din("identf", [128, 128])
    mask_s = din("mask_s", [128, 128])
    mask_i = din("mask_i", [128, 128])
    alphasel = din("alphasel", [DS, NCORES])
    outfm = dout("outfm", [D, T])

    tapdefs = {
        "knn": ([DS, T], F32), "qnn": ([DS, T], F32), "vs": ([DS, T], F32),
        "alp": ([DS, T], F32), "lamI": ([DS, T], F32),
        "seg1": ([DS, 128], F32), "gath1": ([DS, NCORES * 128], F32),
        "bt": ([128, 2048], BF16), "t1n": ([128, 2048], BF16),
        "et1h": ([64, NCH * DS], BF16), "f1h": ([64, NCH * DS], BF16),
        "vtmf": ([128, NCH * DS], F32), "m1": ([DS, L], BF16),
        "hstart1": ([DS, DS], F32), "scol": ([128, NCH], F32),
        "ytm": ([128, NCH * DS], F32), "ynfm": ([DS, T], BF16),
        "ktil": ([DS, T], BF16),
        "x2h0": ([128, NDCH * 1024], F32),
    }
    tap_aps = {t: dout("tap_" + t, *tapdefs[t]) for t in taps}

    w2bf = nc.dram_tensor("w2bf", [NDCH * 128, NDCH * 128], BF16,
                          kind="Internal").ap()
    cc_in = [nc.dram_tensor(f"cc_in{p}", [DS, 128], F32, kind="Internal")
             for p in range(2)]
    cc_out = [nc.dram_tensor(f"cc_out{p}", [NCORES * DS, 128], F32, kind="Internal")
              for p in range(2)]

    with contextlib.ExitStack() as ST:
        E = ST.enter_context
        sb = lambda name, shape, dtype: E(nc.sbuf_tensor(name, list(shape), dtype))

        identb = sb("identb", [128, 128], BF16)
        identf_t = sb("identf_t", [64, 64], F32)
        maskS = sb("maskS", [128, 128], F32)
        maskI = sb("maskI", [128, 128], F32)
        asel = sb("asel", [DS, NCORES], F32)
        oasel = sb("oasel", [DS, NCORES], F32)
        onescol_b = sb("onescol_b", [128, 1], BF16)
        onesrow_f = sb("onesrow_f", [128, 128], F32)
        normw_t = sb("normw_t", [128, DS], F32)
        sw2_t = sb("sw2_t", [128, NDCH], F32)
        b2_t = sb("b2_t", [128, NDCH], F32)
        wab_t = sb("wab_t", [DS, 1], F32)
        lam_t = sb("lam_t", [DS, 1], F32)
        loga8 = sb("loga8", [DS, 1], F32)
        zeros64 = sb("zeros64", [DS, L], F32)
        cbias = sb("cbias", [128, 3], F32)
        wkq_b = sb("wkq_b", [128, NDCH * 128], BF16)
        wav_b = sb("wav_b", [128, NDCH * 128], BF16)
        wo_b = sb("wo_b", [DS, D], BF16)
        negS_kq = sb("negS_kq", [1, 128], BF16)
        negS_av = sb("negS_av", [1, 128], BF16)
        rowA = sb("rowA", [128, 4096], F32)
        rowC = sb("rowC", [128, 4096], BF16)
        xbf = [sb(f"xbf{i}", [128, T], BF16) for i in range(2)]
        Ktil_s = sb("Ktil_s", [DS, T], BF16)
        Qtil_s = sb("Qtil_s", [DS, T], BF16)
        Qhat_s = sb("Qhat_s", [DS, T], BF16)
        M1_t = sb("M1_t", [DS, L], BF16)
        scol = sb("scol", [128, NCH], F32)
        ynfm = sb("ynfm", [DS, T], BF16)
        HS = sb("HS", [DS, 128], F32)
        Hacc = sb("Hacc", [DS, DS], F32)
        segT = sb("segT", [DS, 128], F32)
        tmp64a = sb("tmp64a", [DS, DS], F32)
        tmp64b = sb("tmp64b", [DS, DS], F32)
        tmp64c = sb("tmp64c", [DS, DS], F32)
        HSh = sb("HSh", [DS, 128], BF16)
        HSl = sb("HSl", [DS, 128], BF16)
        Hah = sb("Hah", [DS, DS], BF16)
        Hal = sb("Hal", [DS, DS], BF16)
        H0lb = sb("H0lb", [DS, DS], BF16)
        onesrow_b = sb("onesrow_b", [128, 128], BF16)
        H0b = sb("H0b", [DS, DS], BF16)
        D_b = sb("D_b", [128, DS], BF16)
        U_b = sb("U_b", [128, DS], BF16)
        e_f = sb("e_f", [128, DS], F32)
        errcol = sb("errcol", [128, 1], F32)
        sm1c = sb("sm1c", [128, 1], F32)
        ytm_t = sb("ytm_t", [128, DS], F32)
        ysq = sb("ysq", [128, DS], F32)
        mscol = sb("mscol", [128, 1], F32)
        rstdc = sb("rstdc", [128, 1], F32)
        yn1 = sb("yn1", [128, DS], F32)
        ynb = sb("ynb", [128, DS], BF16)
        wbfm_b = sb("wbfm_b", [DS, L], BF16)
        vb_t = sb("vb_t", [DS, L], BF16)
        Gt_b = sb("Gt_b", [128, DS], BF16)
        Wb2tm_b = sb("Wb2tm_b", [128, DS], BF16)
        la_t = sb("la_t", [DS, T], F32)
        alp = sb("alp", [DS, T], F32)
        cums = sb("cums", [DS, T], F32)
        lamI = sb("lamI", [DS, T], F32)
        w2t = [wkq_b, wav_b]
        mrs2_b = rowC[0:1, 2048:3072]
        negS_w2 = rowC[0:1, 0:2048]

        AR = sb("AR", [128, 32768], BF16)
        ARF = AR[:].bitcast(F32)
        Bt_s = lambda c: AR[:, c * 128: c * 128 + 128]
        T1n_s = lambda c: AR[:, 2048 + c * 128: 2048 + c * 128 + 128]
        T1t_s = lambda c: AR[:, 4096 + c * 128: 4096 + c * 128 + 128]
        T2t_s = lambda c: AR[:, 6144 + c * 128: 6144 + c * 128 + 128]
        P1m_s = lambda c: AR[:, 8192 + c * 128: 8192 + c * 128 + 128]
        Ph1m_s = lambda c: AR[:, 10240 + c * 128: 10240 + c * 128 + 128]
        Ktm_s = lambda c: AR[:, 12288 + c * DS: 12288 + c * DS + DS]
        Wbtm_s = lambda c: AR[:, 13312 + c * DS: 13312 + c * DS + DS]
        Vtmb_s = lambda c: AR[:, 14336 + c * DS: 14336 + c * DS + DS]
        Vtmf_s = lambda c: ARF[:, 7680 + c * DS: 7680 + c * DS + DS]
        diagG_s = lambda c: ARF[0:64, 8704 + c * DS: 8704 + c * DS + DS]
        ET1h_s = lambda c: AR[0:64, 19456 + c * DS: 19456 + c * DS + DS]
        ET1l_s = lambda c: AR[0:64, 20480 + c * DS: 20480 + c * DS + DS]
        F1h_s = lambda c: AR[0:64, 21504 + c * DS: 21504 + c * DS + DS]
        F1l_s = lambda c: AR[0:64, 22528 + c * DS: 22528 + c * DS + DS]
        ET2h_s = lambda c: AR[0:64, 23552 + c * DS: 23552 + c * DS + DS]
        ET2l_s = lambda c: AR[0:64, 24576 + c * DS: 24576 + c * DS + DS]
        F2h_s = lambda c: AR[0:64, 25600 + c * DS: 25600 + c * DS + DS]
        F2l_s = lambda c: AR[0:64, 26624 + c * DS: 26624 + c * DS + DS]
        ET1_s = (ET1h_s, ET1l_s, F1h_s, F1l_s)
        ET2_s = (ET2h_s, ET2l_s, F2h_s, F2l_s)
        x2_s = lambda o: ARF[:, o * 1024: o * 1024 + 1024]
        Bn_b = AR[:, 27648:27776]
        R0n_b = AR[:, 27776:27904]
        R0t_b = AR[:, 27904:28032]
        B2n_b = AR[:, 28032:28160]
        B2t_b = AR[:, 28160:28288]
        B4n_b = AR[:, 28288:28416]
        R1n_b = AR[:, 28416:28544]
        R1t_b = AR[:, 28544:28672]
        Xn_b = AR[:, 28672:28800]
        T2n_b = AR[:, 28800:28928]
        Dlt_b = AR[:, 28928:29056]
        H0b_s = lambda c: AR[0:64, 29056 + c * DS: 29056 + c * DS + DS]
        H0l_s = lambda c: AR[0:64, 30080 + c * DS: 30080 + c * DS + DS]
        stmp = ARF[0:64, 15552:15616]
        Db_x = [D_b[:, :], AR[:, 31232:31296]]
        Ub_x = [U_b[:, :], AR[:, 31296:31360]]
        ef_x = [e_f[:, :], ARF[:, 15680:15744]]
        err_x = [errcol[:, :], ARF[:, 15744:15745]]
        Dlt_x = [Dlt_b, AR[:, 31490:31618]]
        Xn_x = [Xn_b, AR[:, 31618:31746]]
        T2n_x = [T2n_b, AR[:, 31746:31874]]
        Wb2_x = [Wb2tm_b[:, :], AR[:, 31874:31938]]
        Gt_x = [Gt_b[:, :], AR[:, 31938:32002]]
        sm1_x = [sm1c[:, :], ARF[:, 16001:16002]]
        t64b_x = [tmp64b[:, :], ARF[0:64, 16002:16066]]
        t64c_x = [tmp64c[:, :], ARF[0:64, 16066:16130]]
        t64a_x = [tmp64a[:, :], ARF[0:64, 16130:16194]]
        ytm_x = [ytm_t[:, :], ARF[:, 16194:16258]]
        yn1_x = [yn1[:, :], ARF[:, 16258:16322]]
        ynb_x = [ynb[:, :], AR[:, 32644:32708]]
        ms_x = [mscol[:, :], ARF[:, 16354:16355]]
        rs_x = [rstdc[:, :], ARF[:, 16355:16356]]
        xsq = [AR[:, 2048:3072], AR[:, 3072:4096]]
        ksq_b = AR[0:64, 0:2048]
        gath = rowA[0:64, 2048:3072]

        SCR = sb("SCR", [128, 10240], F32)
        SCRB = SCR[:].bitcast(BF16)
        kraw = SCR[0:64, 0:T]
        qraw = SCR[0:64, 2048:2048 + T]
        vs_f = SCR[0:64, 4096:4096 + T]
        knn = SCR[0:64, 6144:6144 + T]
        qnn = SCR[0:64, 8192:8192 + T]
        vraw_h = SCR[0:64, 6144:6144 + 1024]
        prraw_h = SCR[0:64, 7168:7168 + 1024]
        rs1b_h = SCR[0:64, 8192:8192 + 1024]
        x2n_v = lambda o: SCRB[:, o * 1024: o * 1024 + 1024]
        x2b_t = [SCRB[:, 16384 + i * 512: 16384 + (i + 1) * 512] for i in range(2)]
        x2sq_t = [SCRB[:, 17408 + i * 512: 17408 + (i + 1) * 512] for i in range(2)]
        rs2b = SCR[:, 9216:9216 + 512]
        xrl = [xbf[i][:].bitcast(F32)[:, j * 512:(j + 1) * 512]
               for i in range(2) for j in range(2)]
        out_bufs = [SCR[:, 9728:9728 + 512], rowA[:, 3072:3584],
                    rowA[:, 3584:4096]]

        semld = Sem(E(nc.semaphore("semld")))
        semw = Sem(E(nc.semaphore("semw")))
        semx = [Sem(E(nc.semaphore(f"semx{i}"))) for i in range(4)]
        semw2 = [Sem(E(nc.semaphore(f"semw2{i}"))) for i in range(2)]
        semxr = [Sem(E(nc.semaphore(f"semxr{i}"))) for i in range(4)]
        semo = [Sem(E(nc.semaphore(f"semo{i}"))) for i in range(3)]
        semw2b = [Sem(E(nc.semaphore(f"semw2b{i}"))) for i in range(2)]
        semst = Sem(E(nc.semaphore("semst")))
        semcc = Sem(E(nc.semaphore("semcc")))
        semcl = Sem(E(nc.semaphore("semcl")))
        sempe = Sem(E(nc.semaphore("sempe")))
        semv = Sem(E(nc.semaphore("semv")))
        sema = Sem(E(nc.semaphore("sema")))
        semg = Sem(E(nc.semaphore("semg")))
        bsem = [Sem(E(nc.semaphore(f"bk{i}"))) for i in range(8)]
        PB0 = Bank(E(nc.psum_tensor("PB0", [128, 512], F32)), bsem[0])
        PB1 = Bank(E(nc.psum_tensor("PB1", [128, 512], F32)), bsem[1])
        PS0 = Bank(E(nc.psum_tensor("PS0", [128, 512], F32)), bsem[2])
        PS1 = Bank(E(nc.psum_tensor("PS1", [128, 512], F32)), bsem[3])
        PS2 = Bank(E(nc.psum_tensor("PS2", [128, 512], F32)), bsem[4])
        PROW = Bank(E(nc.psum_tensor("PROW", [128, 512], F32)), bsem[5])
        PTR = Bank(E(nc.psum_tensor("PTR", [128, 512], F32)), bsem[6])
        PC = Bank(E(nc.psum_tensor("PC", [64, 512], F32)), bsem[7])

        q = {e: [] for e in ENGINES}
        OWN = {"vector": semv, "scalar": sema, "gpsimd": semg, "tensor": sempe}
        bank_by_sem = {id(b.sem): b
                       for b in (PB0, PB1, PS0, PS1, PS2, PROW, PTR, PC)}

        def wown(eng):
            wait(eng, OWN[eng])

        def wait(eng, sem, val=None):
            v = sem.n if val is None else val
            if v > 0:
                q[eng].append(lambda e, s=sem, v=v: e.wait_ge(s.h, v))

        def emit(eng, make, inc=None, inck=1):
            own = OWN.get(eng)
            dma_like = inck == 16 or (inc is not None and inc in
                                      (semcc, semcl, semst, semld))
            if eng in ("vector", "scalar") or (eng == "gpsimd" and not dma_like):
                if OWN_WAIT:
                    wait(eng, own)
                own.n += 1
                tgt, k = own, 1
                if inc is not None and id(inc) in bank_by_sem:
                    bank_by_sem[id(inc)].pending[id(own)] = (own, own.n)
            elif eng == "tensor":
                own.n += 1
                tgt, k = own, 1
            else:
                if inc is not None:
                    inc.n += inck
                tgt, k = inc, inck

            def f(e, make=make, tgt=tgt, k=k):
                ins = make(e)
                if tgt is not None:
                    ins.then_inc(tgt.h, k)
            q[eng].append(f)
            return tgt.n if tgt is not None else None

        def pe_bank(bank, ncons=1):
            for s, v in bank.pending.values():
                wait("tensor", s, v)
            bank.pending = {}
            return bank.t

        def csl(t, c, w=L):
            return t[:, c * w:(c + 1) * w]

        def hilo(src_ap, hi_ap, lo_ap, tmpf_ap):
            wait("scalar", semv)
            emit("scalar", lambda e, s=src_ap, h=hi_ap: e.activation(
                h, s, AF.Copy), sema, 1)
            wown("scalar")
            emit("scalar", lambda e, h=hi_ap, t=tmpf_ap: e.activation(
                t, h, AF.Copy), sema, 1)
            wait("vector", sema)
            wown("vector")
            emit("vector", lambda e, s=src_ap, t=tmpf_ap, l=lo_ap: e.tensor_tensor(
                l, s, t, ALU.subtract), semv, 1)
            return semv.n

        emit("sync", lambda e: e.dma_start(out=identf_t[:], in_=identf[0:64, 0:64]),
             semld, 16)
        emit("sync", lambda e: e.dma_start(out=maskS[:], in_=mask_s), semld, 16)
        emit("sync", lambda e: e.dma_start(out=maskI[:], in_=mask_i), semld, 16)
        emit("sync", lambda e: e.dma_start(out=asel[:], in_=alphasel), semld, 16)
        emit("sync", lambda e: e.dma_start(out=normw_t[:], in_=normw), semld, 16)
        emit("sync", lambda e: e.dma_start(out=sw2_t[:], in_=sw2c), semld, 16)
        emit("sync", lambda e: e.dma_start(out=b2_t[:], in_=b2c), semld, 16)
        emit("sync", lambda e: e.dma_start(out=wab_t[:], in_=wab), semld, 16)
        emit("sync", lambda e: e.dma_start(out=lam_t[:], in_=lam), semld, 16)
        consts_ld = semld.n
        emit("gpsimd", lambda e: e.dma_start(out=identb[:], in_=identf), semw, 16)
        emit("gpsimd", lambda e: e.dma_start(out=wkq_b[:], in_=wkq), semw, 16)
        emit("gpsimd", lambda e: e.dma_start(out=wav_b[:], in_=wav), semw, 16)
        emit("gpsimd", lambda e: e.dma_start(out=wo_b[:], in_=wo), semw, 16)
        wsmall_ld = semw.n
        emit("gpsimd", lambda e: e.memset(onescol_b[:], 1.0), semg, 1)
        emit("gpsimd", lambda e: e.memset(onesrow_f[:], 1.0), semg, 1)
        emit("gpsimd", lambda e: e.memset(zeros64[:], 0.0), semg, 1)
        emit("gpsimd", lambda e: e.memset(cbias[:, 0:1], 1e-8), semg, 1)
        emit("gpsimd", lambda e: e.memset(cbias[:, 1:2], LN_EPS), semg, 1)
        emit("gpsimd", lambda e: e.memset(cbias[:, 2:3], RMS_EPS), semg, 1)
        emit("gpsimd", lambda e: e.memset(onesrow_b[:], 1.0), semg, 1)
        memset_done = semg.n

        wait("scalar", semld, consts_ld)
        wait("scalar", semg, memset_done)
        emit("scalar", lambda e: e.activation(loga8[:], lam_t[:], AF.Sigmoid), sema, 1)
        wown("scalar")
        emit("scalar", lambda e: e.activation(loga8[:], loga8[:], AF.Ln,
                                              bias=cbias[0:64, 0:1]), sema, 1)
        wown("scalar")
        emit("scalar", lambda e: e.activation(loga8[:], loga8[:], AF.Copy,
                                              scale=C_GATE), sema, 1)
        wait("vector", semld, consts_ld)
        emit("vector", lambda e: e.tensor_scalar(oasel[:], asel[:], -1.0, 1.0,
                                                 ALU.mult, ALU.add), semv, 1)

        for wtile, outrow in ((wkq_b, negS_kq), (wav_b, negS_av)):
            ps = pe_bank(PROW, 1)
            wait("tensor", semw, wsmall_ld)
            wait("tensor", semg, memset_done)
            for i in range(NDCH):
                emit("tensor", lambda e, i=i, wtile=wtile, ps=ps: e.matmul(
                    ps[0:1, 0:128], onescol_b[:], wtile[:, i * 128:(i + 1) * 128],
                    start=(i == 0), stop=(i == NDCH - 1)), sempe, 1)
            wait("scalar", sempe)
            emit("scalar", lambda e, ps=ps, outrow=outrow: e.activation(
                outrow[:], ps[0:1, 0:128], AF.Copy, scale=-1.0 / D), PROW.sem, 1)

        xcast_pe = {}
        for h in range(2):
            t0 = h * 1024
            ps_kq = [pe_bank(PB0, 2), pe_bank(PB1, 2)]
            ps_va = [pe_bank(PS0, 2), pe_bank(PS1, 2)]
            ps_sx = [pe_bank(PROW, 2), pe_bank(PS2, 2)]
            ps_sq = [pe_bank(PC, 1), pe_bank(PTR, 1)]
            for i in range(NDCH):
                g = h * NDCH + i
                slot = g % 4
                buf = xbf[slot % 2]
                b0 = (slot // 2) * 1024
                if g >= 4:
                    wait("gpsimd", sempe, xcast_pe[g - 4])
                emit("gpsimd", lambda e, i=i, t0=t0, b0=b0, buf=buf: e.dma_start(
                    out=buf[:, b0:b0 + 1024],
                    in_=xf[i * 128:(i + 1) * 128, t0:t0 + 1024]), semx[slot], 16)
                xldv = semx[slot].n
                wait("scalar", semx[slot], xldv)
                if g >= 2:
                    wait("scalar", sempe, xcast_pe[g - 2])
                emit("scalar", lambda e, i=i, b0=b0, buf=buf: e.activation(
                    xsq[i % 2][:], buf[:, b0:b0 + 1024], AF.Square), sema, 1)
                sq_v = sema.n
                wait("tensor", semx[slot], xldv)
                first = (i == 0)
                last = (i == NDCH - 1)
                for tt in range(2):
                    emit("tensor", lambda e, tt=tt, b0=b0, buf=buf, first=first,
                         last=last, ps=ps_sx[tt]: e.matmul(
                        ps[0:1, :], onescol_b[:],
                        buf[:, b0 + tt * 512: b0 + tt * 512 + 512],
                        start=first, stop=last), sempe, 1)
                wait("tensor", sema, sq_v)
                for tt in range(2):
                    emit("tensor", lambda e, i=i, tt=tt, first=first, last=last,
                         ps=ps_sq[tt]: e.matmul(
                        ps[0:1, 0:512], onescol_b[:],
                        xsq[i % 2][:, tt * 512: tt * 512 + 512],
                        start=first, stop=last), sempe, 1)
                for tt in range(2):
                    emit("tensor", lambda e, i=i, tt=tt, b0=b0, buf=buf, first=first,
                         ps=ps_kq[tt]: e.matmul(
                        ps[:, :], wkq_b[:, i * 128:(i + 1) * 128],
                        buf[:, b0 + tt * 512: b0 + tt * 512 + 512],
                        start=first, stop=False), sempe, 1)
                    emit("tensor", lambda e, i=i, tt=tt, b0=b0, buf=buf, first=first,
                         ps=ps_va[tt]: e.matmul(
                        ps[:, :], wav_b[:, i * 128:(i + 1) * 128],
                        buf[:, b0 + tt * 512: b0 + tt * 512 + 512],
                        start=first, stop=False), sempe, 1)
                xcast_pe[g] = sempe.n
            stats_pe = sempe.n
            wait("scalar", sempe, stats_pe)
            for tt in range(2):
                c0 = t0 + tt * 512
                sxbank = PROW if tt == 0 else PS2
                sqbank = PC if tt == 0 else PTR
                emit("scalar", lambda e, c0=c0, ps=ps_sx[tt]: e.activation(
                    rowA[0:1, c0: c0 + 512],
                    ps[0:1, :], AF.Copy), sxbank.sem, 1)
                emit("scalar", lambda e, c0=c0, ps=ps_sx[tt]: e.activation(
                    rowC[0:1, c0: c0 + 512],
                    ps[0:1, :], AF.Copy), sxbank.sem, 1)
                wown("scalar")
                emit("scalar", lambda e, c0=c0: e.activation(
                    rowA[0:1, 2048 + c0: 2048 + c0 + 512],
                    rowA[0:1, c0: c0 + 512],
                    AF.Square, scale=1.0 / D), sema, 1)
                m1sq_v = sema.n
                wait("vector", sema, m1sq_v)
                wait("vector", sempe, stats_pe)
                emit("vector", lambda e, c0=c0, ps=ps_sq[tt]:
                     e.scalar_tensor_tensor(
                    rowA[0:1, 2048 + c0: 2048 + c0 + 512],
                    ps[0:1, 0:512], 1.0 / D,
                    rowA[0:1, 2048 + c0: 2048 + c0 + 512],
                    ALU.mult, ALU.subtract), sqbank.sem, 1)
                var_v = semv.n
                wait("scalar", semv, var_v)
                emit("scalar", lambda e, c0=c0: e.activation(
                    rowA[64:65, c0: c0 + 512],
                    rowA[0:1, 2048 + c0: 2048 + c0 + 512],
                    AF.Ln, bias=cbias[64:65, 1:2]), sema, 1)
                wown("scalar")
                emit("scalar", lambda e, c0=c0: e.activation(
                    rowA[64:65, c0: c0 + 512],
                    rowA[64:65, c0: c0 + 512],
                    AF.Exp, scale=-0.5), sema, 1)
            rows_v = sema.n
            wait("tensor", sema, rows_v)
            for tt in range(2):
                c0 = t0 + tt * 512
                emit("tensor", lambda e, tt=tt, c0=c0, ps=ps_kq[tt]: e.matmul(
                    ps[:, :], negS_kq[:],
                    rowC[0:1, c0: c0 + 512], start=False, stop=True), sempe, 1)
                emit("tensor", lambda e, tt=tt, c0=c0, ps=ps_va[tt]: e.matmul(
                    ps[:, :], negS_av[:],
                    rowC[0:1, c0: c0 + 512], start=False, stop=True), sempe, 1)
            corr_pe = sempe.n
            for tt in range(2):
                c0 = t0 + tt * 512
                wown("scalar")
                emit("scalar", lambda e, c0=c0, t0=t0: e.activation(
                    rowC[0:1, 2048 + c0 - t0: 2048 + c0 - t0 + 512],
                    rowA[64:65, c0: c0 + 512], AF.Copy), sema, 1)
                rs1bf_a = sema.n
                psb = pe_bank(PC, 1)
                wait("tensor", sema, rs1bf_a)
                emit("tensor", lambda e, c0=c0, t0=t0, psb=psb: e.matmul(
                    psb[:, 0:512], onesrow_b[0:1, 0:64],
                    rowC[0:1, 2048 + c0 - t0: 2048 + c0 - t0 + 512],
                    start=True, stop=True), sempe, 1)
                wait("vector", sempe)
                emit("vector", lambda e, tt=tt, psb=psb: e.tensor_copy(
                    rs1b_h[:, tt * 512: tt * 512 + 512], psb[:, 0:512]), PC.sem, 1)
            rs1b_v = semv.n
            wait("scalar", sempe, corr_pe)
            wait("scalar", semv)
            for tt in range(2):
                c0 = t0 + tt * 512
                bk = PB0 if tt == 0 else PB1
                bv = PS0 if tt == 0 else PS1
                emit("scalar", lambda e, tt=tt, c0=c0, ps=ps_kq[tt]: e.activation(
                    kraw[:, c0: c0 + 512], ps[0:64, :], AF.Copy), bk.sem, 1)
                emit("scalar", lambda e, tt=tt, c0=c0, ps=ps_kq[tt]: e.activation(
                    qraw[:, c0: c0 + 512], ps[64:128, :], AF.Copy), bk.sem, 1)
                emit("scalar", lambda e, tt=tt, c0=c0, t0=t0, ps=ps_va[tt]:
                     e.activation(
                    prraw_h[:, c0 - t0: c0 - t0 + 512], ps[0:64, :], AF.Copy),
                    bv.sem, 1)
                emit("scalar", lambda e, tt=tt, c0=c0, t0=t0, ps=ps_va[tt]:
                     e.activation(
                    vraw_h[:, c0 - t0: c0 - t0 + 512], ps[64:128, :], AF.Copy),
                    bv.sem, 1)
            evac_a = sema.n
            wait("vector", sema, evac_a)
            wait("vector", semv, rs1b_v)
            emit("vector", lambda e, t0=t0: e.tensor_tensor(
                vs_f[:, t0:t0 + 1024], vraw_h[:, :], rs1b_h[:, :], ALU.mult), semv, 1)
            emit("vector", lambda e, t0=t0: e.tensor_tensor(
                la_t[:, t0:t0 + 1024], prraw_h[:, :], rs1b_h[:, :], ALU.mult), semv, 1)
        sweeps_v = semv.n
        sweeps_a = sema.n

        pcst = [AR[:, 23552:25600], AR[:, 25600:27648]]
        for t in range(NDCH):
            if t >= 2:
                wait("gpsimd", semw2b[t % 2], semw2b[t % 2].n)
            emit("gpsimd", lambda e, t=t: e.dma_start(
                out=pcst[t % 2], in_=w2[t * 128:(t + 1) * 128, :]),
                semw2[t % 2], 16)
            wait("gpsimd", semw2[t % 2], semw2[t % 2].n)
            emit("gpsimd", lambda e, t=t: e.dma_start(
                out=w2bf[t * 128:(t + 1) * 128, :], in_=pcst[t % 2]),
                semw2b[t % 2], 16)

        wait("scalar", semv, sweeps_v)
        wown("scalar")
        emit("scalar", lambda e: e.activation(ksq_b[:, :], kraw[:, :], AF.Square),
             sema, 1)
        ksq_a = sema.n
        rbank = (PROW, PS2, PB0, PB1)
        bbank = (PC, PTR, PS0, PS1)
        kmm_pe = 0
        for phase, (roff, raw, pb) in enumerate(
                ((0, kraw, 32), (2048, qraw, 64))):
            if phase == 1:
                wait("scalar", sempe, kmm_pe)
                wown("scalar")
                emit("scalar", lambda e: e.activation(
                    ksq_b[:, :], qraw[:, :], AF.Square), sema, 1)
            sq_a = sema.n
            pss = []
            wait("tensor", sema, sq_a)
            for tt in range(4):
                ps = pe_bank(rbank[tt], 1)
                pss.append(ps)
                emit("tensor", lambda e, tt=tt, ps=ps: e.matmul(
                    ps[0:1, :], onescol_b[0:64, :],
                    ksq_b[:, tt * 512: tt * 512 + 512],
                    start=True, stop=True), sempe, 1)
            kmm_pe = sempe.n
            wait("scalar", sempe, kmm_pe)
            for tt in range(4):
                emit("scalar", lambda e, tt=tt, roff=roff, ps=pss[tt]:
                     e.activation(
                    rowA[32:33, roff + tt * 512: roff + tt * 512 + 512],
                    ps[0:1, :], AF.Ln), rbank[tt].sem, 1)
            wown("scalar")
            for tt in range(4):
                emit("scalar", lambda e, tt=tt, roff=roff: e.activation(
                    rowA[32:33, roff + tt * 512: roff + tt * 512 + 512],
                    rowA[32:33, roff + tt * 512: roff + tt * 512 + 512],
                    AF.Exp, scale=-0.5), sema, 1)
            wown("scalar")
            for tt in range(4):
                emit("scalar", lambda e, tt=tt, roff=roff, pb=pb: e.activation(
                    rowC[pb:pb + 1, roff % 2048 + tt * 512:
                         roff % 2048 + tt * 512 + 512],
                    rowA[32:33, roff + tt * 512: roff + tt * 512 + 512],
                    AF.Copy), sema, 1)
        rn_a = sema.n
        wait("tensor", sema, rn_a)
        for pb, raw, dst in ((32, kraw, knn), (64, qraw, qnn)):
            pss2 = []
            for tt in range(4):
                psb = pe_bank(bbank[tt], 1)
                pss2.append(psb)
                emit("tensor", lambda e, tt=tt, pb=pb, psb=psb: e.matmul(
                    psb[0:64, 0:512], onesrow_b[pb:pb + 1, 0:64],
                    rowC[pb:pb + 1, tt * 512: tt * 512 + 512],
                    start=True, stop=True), sempe, 1)
            bc_pe = sempe.n
            wait("vector", sempe, bc_pe)
            for tt in range(4):
                emit("vector", lambda e, tt=tt, raw=raw, dst=dst, psb=pss2[tt]:
                     e.tensor_tensor(
                    dst[:, tt * 512: tt * 512 + 512], psb[0:64, 0:512],
                    raw[:, tt * 512: tt * 512 + 512], ALU.mult), bbank[tt].sem, 1)
        knn_v = semv.n

        wait("scalar", semv, sweeps_v)
        emit("scalar", lambda e: e.activation(alp[:, :], la_t[:, :], AF.Sigmoid,
                                              bias=wab_t[:]), sema, 1)
        wown("scalar")
        emit("scalar", lambda e: e.activation(la_t[:, :], alp[:, :], AF.Copy,
                                              scale=loga8[:]), sema, 1)
        wown("scalar")
        emit("scalar", lambda e: e.activation(alp[:, :], la_t[:, :], AF.Exp), sema, 1)
        alpha_a = sema.n
        wait("vector", sema, alpha_a)
        for c in range(NCH):
            emit("vector", lambda e, c=c: e.tensor_tensor_scan(
                cums[:, c * L:(c + 1) * L], la_t[:, c * L:(c + 1) * L],
                zeros64[:, :], 0.0, ALU.add, ALU.add), semv, 1)
        cums_v = semv.n
        wait("scalar", semv, cums_v)
        emit("scalar", lambda e: e.activation(lamI[:, :], cums[:, :], AF.Exp), sema, 1)
        lam_a = sema.n
        wait("vector", sema, alpha_a)
        wown("vector")
        emit("vector", lambda e: e.tensor_scalar(la_t[:, :], alp[:, :], -1.0, 1.0,
                                                 ALU.mult, ALU.add), semv, 1)
        wown("vector")
        emit("vector", lambda e: e.tensor_tensor(alp[:, :], la_t[:, :], knn[:, :],
                                                 ALU.mult), semv, 1)
        w1_v = semv.n

        rlam_f = SCR[0:64, 2048:2048 + T]
        wait("scalar", semv, w1_v)
        wait("scalar", sema, lam_a)
        emit("scalar", lambda e: e.activation(
            la_t[:, 1:T], lamI[:, 0:T - 1], AF.Copy), sema, 1)
        emit("scalar", lambda e: e.activation(
            rlam_f[:, :], cums[:, :], AF.Exp, scale=-1.0), sema, 1)
        lamE_a = sema.n
        wait("gpsimd", sema, lamE_a)
        for c in range(NCH):
            emit("gpsimd", lambda e, c=c: e.memset(
                la_t[:, c * L: c * L + 1], 1.0), semg, 1)
        lamE_g = semg.n
        til_piece = []
        wait("vector", sema, lamE_a)
        wait("vector", semg, lamE_g)
        wown("vector")
        for p4 in range(4):
            s4 = slice(p4 * 512, p4 * 512 + 512)
            emit("vector", lambda e, s4=s4: e.tensor_tensor(
                Ktil_s[:, s4], knn[:, s4], la_t[:, s4], ALU.mult), semv, 1)
            emit("vector", lambda e, s4=s4: e.tensor_tensor(
                Qtil_s[:, s4], qnn[:, s4], la_t[:, s4], ALU.mult), semv, 1)
            emit("vector", lambda e, s4=s4: e.tensor_tensor(
                Qhat_s[:, s4], qnn[:, s4], lamI[:, s4], ALU.mult), semv, 1)
            til_piece.append(semv.n)

        for tname, src in (("knn", knn), ("qnn", qnn), ("vs", vs_f),
                           ("alp", alp), ("lamI", lamI)):
            if tname in taps:
                wait("sync", semv, semv.n)
                wait("sync", sema, sema.n)
                emit("sync", lambda e, t=tname, src=src: e.dma_start(
                    out=tap_aps[t], in_=src[:, :]), semst, 16)

        if phases < 1.2:
            _replay(nc, q, semst, semcc)
            return nc
        PP = [
            dict(M1=M1_t[:, :], wbfm=wbfm_b[:, :], vb=vb_t[:, :],
                 Bn=Bn_b, R0n=R0n_b, R0t=R0t_b, B2n=B2n_b, B2t=B2t_b,
                 B4n=B4n_b, R1n=R1n_b, R1t=R1t_b, Gt=Gt_b[:, :],
                 t64b=tmp64b[:, :], t64c=tmp64c[:, :], t64a=tmp64a[:, :],
                 bA=PS0, bR0=PS1, bB2=PS2, bB4=PS0, bT1=PS1, bG=PS2, bEF=PC,
                 ptlo=0),
            dict(M1=AR[0:64, 30080:30208], wbfm=AR[0:64, 30208:30336],
                 vb=AR[0:64, 30336:30464],
                 Bn=AR[:, 29056:29184], R0n=AR[:, 29184:29312],
                 R0t=AR[:, 29312:29440], B2n=AR[:, 29440:29568],
                 B2t=AR[:, 29568:29696], B4n=AR[:, 29696:29824],
                 R1n=AR[:, 29824:29952], R1t=AR[:, 29952:30080],
                 Gt=AR[:, 30464:30528],
                 t64b=ARF[0:64, 15264:15328], t64c=ARF[0:64, 15328:15392],
                 t64a=ARF[0:64, 15392:15456],
                 bA=PB0, bR0=PB1, bB2=PROW, bB4=PB0, bT1=PB1, bG=PROW,
                 bEF=PB0, ptlo=1),
        ]
        pv = [dict(tr=0, psA=0, b2=0, b4=0, t1=0, ef=0, hiloa=0)
              for _ in range(2)]
        wait("scalar", semv, w1_v)

        def prep_s1(c, P, V):
            if V["psA"]:
                wait("vector", sempe, V["psA"])
            emit("vector", lambda e, c=c, P=P: e.tensor_tensor(
                P["M1"], alp[:, c * L:(c + 1) * L],
                rlam_f[:, c * L:(c + 1) * L], ALU.mult), semv, 1)
            m1_v = semv.n
            if V["tr"]:
                wait("scalar", sempe, V["tr"])
            emit("scalar", lambda e, c=c: e.activation(
                diagG_s(c), identf_t[:, :], AF.Copy,
                scale=lamI[:, c * L + L - 1: c * L + L]), sema, 1)
            wait("scalar", semv, m1_v)
            emit("scalar", lambda e, c=c, P=P: e.activation(
                P["wbfm"], P["M1"], AF.Copy,
                scale=lamI[:, c * L + L - 1: c * L + L]), sema, 1)
            emit("scalar", lambda e, c=c, P=P: e.activation(
                P["vb"], vs_f[:, c * L:(c + 1) * L], AF.Copy), sema, 1)
            return m1_v, sema.n

        def prep_s2pair(cs, m1s, scas):
            ptf = pe_bank(PTR, 8)
            pt = ptf[:].bitcast(BF16)
            for (c, P), sc_a in zip(cs, scas):
                o = P["ptlo"] * 192
                wait("tensor", sema, sc_a)
                emit("tensor", lambda e, c=c, pt=pt, o=o: e.transpose(
                    pt[:, o: o + 64], csl(Ktil_s, c), identb[0:64, 0:64]),
                    sempe, 1)
                emit("tensor", lambda e, P=P, pt=pt, o=o: e.transpose(
                    pt[:, o + 64: o + 128], P["wbfm"], identb[0:64, 0:64]),
                    sempe, 1)
                emit("tensor", lambda e, P=P, pt=pt, o=o: e.transpose(
                    pt[:, o + 128: o + 192], P["vb"], identb[0:64, 0:64]),
                    sempe, 1)
                pv[P["ptlo"]]["tr"] = sempe.n
            wait("scalar", sempe)
            for (c, P), sc_a in zip(cs, scas):
                o = P["ptlo"] * 192
                emit("scalar", lambda e, c=c, pt=pt, o=o: e.activation(
                    Ktm_s(c), pt[:, o: o + 64], AF.Copy), PTR.sem, 1)
                emit("scalar", lambda e, c=c, pt=pt, o=o: e.activation(
                    Wbtm_s(c), pt[:, o + 64: o + 128], AF.Copy), PTR.sem, 1)
                emit("scalar", lambda e, c=c, pt=pt, o=o: e.activation(
                    Vtmb_s(c), pt[:, o + 128: o + 192], AF.Copy), PTR.sem, 1)
                emit("scalar", lambda e, c=c, pt=pt, o=o: e.activation(
                    Vtmf_s(c), pt[:, o + 128: o + 192], AF.Copy), PTR.sem, 1)

        def prep_s3(c, P, V, m1_v):
            psA = pe_bank(P["bA"], 3)
            wait("tensor", semv, m1_v)
            emit("tensor", lambda e, c=c, P=P, psA=psA: e.matmul(
                psA[:, 0:128], P["M1"], csl(Ktil_s, c), start=True, stop=False),
                sempe, 1)
            emit("tensor", lambda e, c=c, P=P, psA=psA: e.matmul(
                psA[:, 128:256], P["M1"], csl(Qtil_s, c), start=False,
                stop=False), sempe, 1)
            emit("tensor", lambda e, c=c, P=P, psA=psA: e.matmul(
                psA[:, 256:384], P["M1"], csl(Qhat_s, c), start=False,
                stop=True), sempe, 1)
            V["psA"] = sempe.n
            wait("vector", sempe, V["psA"])
            emit("vector", lambda e, c=c, psA=psA: e.scalar_tensor_tensor(
                Bt_s(c), psA[:, 0:128], -1.0, maskS[:, :], ALU.mult, ALU.mult),
                P["bA"].sem, 1)
            emit("vector", lambda e, c=c, psA=psA: e.tensor_tensor(
                P1m_s(c), psA[:, 128:256], maskS[:, :], ALU.mult),
                P["bA"].sem, 1)
            emit("vector", lambda e, c=c, psA=psA: e.tensor_tensor(
                Ph1m_s(c), psA[:, 256:384], maskI[:, :], ALU.mult),
                P["bA"].sem, 1)
            return semv.n

        def prep_s4pair(cs, bvs):
            pt2 = pe_bank(PTR, 2)[:].bitcast(BF16)
            bn_pes = []
            for (c, P), b_v in zip(cs, bvs):
                o = P["ptlo"] * 128
                wait("tensor", semv, b_v)
                emit("tensor", lambda e, c=c, pt2=pt2, o=o: e.transpose(
                    pt2[:, o: o + 128], Bt_s(c), identb[:, :]), sempe, 1)
                bn_pes.append(sempe.n)
            wait("scalar", sempe)
            for (c, P), _ in zip(cs, bvs):
                o = P["ptlo"] * 128
                emit("scalar", lambda e, P=P, pt2=pt2, o=o: e.activation(
                    P["Bn"], pt2[:, o: o + 128], AF.Copy), PTR.sem, 1)
            return sema.n

        def prep_s4b(c, P, bn_a):
            ps0r = pe_bank(P["bR0"], 2)
            wait("tensor", sema, bn_a)
            emit("tensor", lambda e, P=P, ps0r=ps0r: e.matmul(
                ps0r[:, 0:128], identb[:, :], P["Bn"], start=True, stop=False),
                sempe, 1)
            emit("tensor", lambda e, ps0r=ps0r: e.matmul(
                ps0r[:, 0:128], identb[:, :], identb[:, :], start=False,
                stop=False), sempe, 1)
            emit("tensor", lambda e, c=c, ps0r=ps0r: e.matmul(
                ps0r[:, 128:256], identb[:, :], Bt_s(c), start=False, stop=False),
                sempe, 1)
            emit("tensor", lambda e, ps0r=ps0r: e.matmul(
                ps0r[:, 128:256], identb[:, :], identb[:, :], start=False,
                stop=True), sempe, 1)
            r0_pe = sempe.n
            wait("scalar", sempe, r0_pe)
            emit("scalar", lambda e, P=P, ps0r=ps0r: e.activation(
                P["R0n"], ps0r[:, 0:128], AF.Copy), P["bR0"].sem, 1)
            emit("scalar", lambda e, P=P, ps0r=ps0r: e.activation(
                P["R0t"], ps0r[:, 128:256], AF.Copy), P["bR0"].sem, 1)

        def prep_s5(c, P):
            ps2r = pe_bank(P["bB2"], 2)
            emit("tensor", lambda e, c=c, P=P, ps2r=ps2r: e.matmul(
                ps2r[:, 0:128], Bt_s(c), P["Bn"], start=True, stop=False),
                sempe, 1)
            emit("tensor", lambda e, c=c, P=P, ps2r=ps2r: e.matmul(
                ps2r[:, 128:256], P["Bn"], Bt_s(c), start=False, stop=True),
                sempe, 1)
            V = pv[P["ptlo"]]
            V["b2"] = sempe.n
            wait("scalar", sempe, V["b2"])
            emit("scalar", lambda e, P=P, ps2r=ps2r: e.activation(
                P["B2n"], ps2r[:, 0:128], AF.Copy), P["bB2"].sem, 1)
            emit("scalar", lambda e, P=P, ps2r=ps2r: e.activation(
                P["B2t"], ps2r[:, 128:256], AF.Copy), P["bB2"].sem, 1)
            return sema.n

        def prep_s6(c, P, b2_a):
            ps3r = pe_bank(P["bB4"], 3)
            wait("tensor", sema, b2_a)
            emit("tensor", lambda e, P=P, ps3r=ps3r: e.matmul(
                ps3r[:, 0:128], P["B2t"], P["B2n"], start=True, stop=False),
                sempe, 1)
            emit("tensor", lambda e, P=P, ps3r=ps3r: e.matmul(
                ps3r[:, 128:256], P["R0t"], P["B2n"], start=False, stop=False),
                sempe, 1)
            emit("tensor", lambda e, P=P, ps3r=ps3r: e.matmul(
                ps3r[:, 128:256], identb[:, :], P["R0n"], start=False,
                stop=False), sempe, 1)
            emit("tensor", lambda e, P=P, ps3r=ps3r: e.matmul(
                ps3r[:, 256:384], P["B2n"], P["R0t"], start=False, stop=False),
                sempe, 1)
            emit("tensor", lambda e, P=P, ps3r=ps3r: e.matmul(
                ps3r[:, 256:384], identb[:, :], P["R0t"], start=False,
                stop=True), sempe, 1)
            pv[P["ptlo"]]["b4"] = sempe.n
            wait("scalar", sempe, pv[P["ptlo"]]["b4"])
            emit("scalar", lambda e, P=P, ps3r=ps3r: e.activation(
                P["B4n"], ps3r[:, 0:128], AF.Copy), P["bB4"].sem, 1)
            emit("scalar", lambda e, P=P, ps3r=ps3r: e.activation(
                P["R1n"], ps3r[:, 128:256], AF.Copy), P["bB4"].sem, 1)
            emit("scalar", lambda e, P=P, ps3r=ps3r: e.activation(
                P["R1t"], ps3r[:, 256:384], AF.Copy), P["bB4"].sem, 1)
            return sema.n

        def prep_s7(c, P, r1_a):
            ps4r = pe_bank(P["bT1"], 2)
            wait("tensor", sema, r1_a)
            emit("tensor", lambda e, P=P, ps4r=ps4r: e.matmul(
                ps4r[:, 0:128], P["R1t"], P["B4n"], start=True, stop=False),
                sempe, 1)
            emit("tensor", lambda e, P=P, ps4r=ps4r: e.matmul(
                ps4r[:, 0:128], identb[:, :], P["R1n"], start=False, stop=False),
                sempe, 1)
            emit("tensor", lambda e, P=P, ps4r=ps4r: e.matmul(
                ps4r[:, 128:256], P["B4n"], P["R1t"], start=False, stop=False),
                sempe, 1)
            emit("tensor", lambda e, P=P, ps4r=ps4r: e.matmul(
                ps4r[:, 128:256], identb[:, :], P["R1t"], start=False,
                stop=True), sempe, 1)
            pv[P["ptlo"]]["t1"] = sempe.n
            wait("scalar", sempe, pv[P["ptlo"]]["t1"])
            emit("scalar", lambda e, c=c, ps4r=ps4r: e.activation(
                T1n_s(c), ps4r[:, 0:128], AF.Copy), P["bT1"].sem, 1)
            emit("scalar", lambda e, c=c, ps4r=ps4r: e.activation(
                T1t_s(c), ps4r[:, 128:256], AF.Copy), P["bT1"].sem, 1)
            return sema.n

        def prep_s8(c, P, t1_a):
            V = pv[P["ptlo"]]
            ps5r = pe_bank(P["bG"], 1)
            wait("tensor", sema, t1_a)
            emit("tensor", lambda e, c=c, P=P, ps5r=ps5r: e.matmul(
                ps5r[:, 0:64], T1n_s(c), Wbtm_s(c), start=True, stop=True),
                sempe, 1)
            wait("scalar", sempe)
            if V["ef"]:
                wait("scalar", sempe, V["ef"])
            emit("scalar", lambda e, P=P, ps5r=ps5r: e.activation(
                P["Gt"], ps5r[:, 0:64], AF.Copy), P["bG"].sem, 1)
            g1_a = sema.n
            psc = pe_bank(P["bEF"], 2)
            wait("tensor", sema, g1_a)
            emit("tensor", lambda e, c=c, P=P, psc=psc: e.matmul(
                psc[0:64, 0:64], Ktm_s(c), P["Gt"], start=True, stop=False),
                sempe, 1)
            emit("tensor", lambda e, c=c, P=P, psc=psc: e.matmul(
                psc[0:64, 64:128], P["Gt"], Vtmb_s(c), start=False, stop=True),
                sempe, 1)
            V["ef"] = sempe.n
            wait("vector", sempe, V["ef"])
            if V["hiloa"]:
                wait("vector", sema, V["hiloa"])
            emit("vector", lambda e, c=c, P=P, psc=psc: e.scalar_tensor_tensor(
                P["t64b"], psc[0:64, 0:64], -1.0, diagG_s(c), ALU.mult, ALU.add),
                P["bEF"].sem, 1)
            ef_v = semv.n
            wait("scalar", semv, ef_v)
            emit("scalar", lambda e, P=P, psc=psc: e.activation(
                P["t64c"], psc[0:64, 64:128], AF.Copy), P["bEF"].sem, 1)
            hilo(P["t64b"], ET1h_s(c), ET1l_s(c), P["t64a"])
            hilo(P["t64c"], F1h_s(c), F1l_s(c), P["t64a"])
            V["hiloa"] = sema.n
            hiloa1_of[c] = (sema.n, semv.n)

        cs_a = [0]
        cs_v = [0]

        def compose_init(prev_v, prev_a):
            wait("gpsimd", semv, prev_v)
            wait("gpsimd", sema, prev_a)
            emit("gpsimd", lambda e: e.memset(HS[:, :], 0.0), semg, 1)
            emit("gpsimd", lambda e: e.memset(HSh[:, :], 0.0), semg, 1)
            emit("gpsimd", lambda e: e.memset(HSl[:, :], 0.0), semg, 1)
            hs_g = semg.n
            wait("vector", semg, hs_g)
            emit("vector", lambda e: e.tensor_tensor(
                HS[:, 64:128], zeros64[:, 0:64], identf_t[:, :], ALU.add), semv, 1)
            wait("scalar", semg, hs_g)
            emit("scalar", lambda e: e.activation(
                HSh[:, 64:128], identf_t[:, :], AF.Copy), sema, 1)
            cs_v[0] = semv.n
            cs_a[0] = sema.n

        def compose_step(c, ST, bank, ef_a):
            ETh, ETl, Fh, Fl = ST
            psc = pe_bank(bank)
            wait("tensor", semv, cs_v[0])
            wait("tensor", sema, cs_a[0])
            if ef_a:
                wait("tensor", sema, ef_a[0])
                wait("tensor", semv, ef_a[1])
            emit("tensor", lambda e, c=c, psc=psc: e.matmul(
                psc[0:64, 0:128], ETh(c), HSh[:, :], start=True, stop=False),
                sempe, 1)
            emit("tensor", lambda e, c=c, psc=psc: e.matmul(
                psc[0:64, 0:128], ETh(c), HSl[:, :], start=False, stop=False),
                sempe, 1)
            emit("tensor", lambda e, c=c, psc=psc: e.matmul(
                psc[0:64, 0:128], ETl(c), HSh[:, :], start=False, stop=False),
                sempe, 1)
            emit("tensor", lambda e, c=c, psc=psc: e.matmul(
                psc[0:64, 0:64], identb[0:64, 0:64], Fh(c), start=False,
                stop=False), sempe, 1)
            emit("tensor", lambda e, c=c, psc=psc: e.matmul(
                psc[0:64, 0:64], identb[0:64, 0:64], Fl(c), start=False,
                stop=True), sempe, 1)
            cp_pe = sempe.n
            wait("scalar", sempe, cp_pe)
            wait("scalar", semv, cs_v[0])
            emit("scalar", lambda e, psc=psc: e.activation(
                HS[:, :], psc[0:64, 0:128], AF.Copy), sema, 1)
            wown("scalar")
            emit("scalar", lambda e: e.activation(
                HSh[:, :], HS[:, :], AF.Copy), sema, 1)
            wown("scalar")
            emit("scalar", lambda e: e.activation(
                segT[:, :], HSh[:, :], AF.Copy), sema, 1)
            cs_a[0] = sema.n
            wait("vector", sema, cs_a[0])
            emit("vector", lambda e: e.tensor_tensor(
                HSl[:, :], HS[:, :], segT[:, :], ALU.subtract), bank.sem, 1)
            cs_v[0] = semv.n

        def exchange_finish(passno):
            wait("scalar", semv, cs_v[0])
            wown("scalar")
            emit("scalar", lambda e: e.activation(segT[:, 0:64], HS[:, 0:64],
                                                  AF.Copy), sema, 1)
            seg_a = sema.n
            ptt = pe_bank(PTR)[:].bitcast(BF16)
            wait("tensor", sema, seg_a)
            emit("tensor", lambda e, ptt=ptt: e.transpose(
                ptt[0:64, 0:64], HSh[:, 64:128], identb[0:64, 0:64]), sempe, 1)
            emit("tensor", lambda e, ptt=ptt: e.transpose(
                ptt[0:64, 64:128], HSl[:, 64:128], identb[0:64, 0:64]), sempe, 1)
            wait("scalar", sempe)
            emit("scalar", lambda e, ptt=ptt: e.activation(
                tmp64b[:, :], ptt[0:64, 0:64], AF.Copy), sema, 1)
            emit("scalar", lambda e, ptt=ptt: e.activation(
                tmp64c[:, :], ptt[0:64, 64:128], AF.Copy), PTR.sem, 1)
            wait("vector", sema)
            emit("vector", lambda e: e.tensor_tensor(
                segT[:, 64:128], tmp64b[:, :], tmp64c[:, :], ALU.add), semv, 1)
            seg_v = semv.n
            wait("gpsimd", semv, seg_v)
            wait("gpsimd", sema)
            emit("gpsimd", lambda e, p=passno: e.dma_start(
                out=cc_in[p].ap(), in_=segT[:, :]), semcc, 16)
            wait("gpsimd", semcc, semcc.n)
            emit("gpsimd", lambda e, p=passno: e.collective_compute(
                "AllGather", ALU.bypass,
                replica_groups=[list(range(NCORES))],
                ins=[cc_in[p].ap().opt()],
                outs=[cc_out[p].ap().opt()]), semcl, 1)
            wait("gpsimd", semcl, semcl.n)
            emit("gpsimd", lambda e, p=passno: e.dma_start(
                out=gath[:, :].rearrange("p (r f) -> p r f", r=NCORES),
                in_=cc_out[p].ap().rearrange("(r p) f -> p r f", p=DS)), semcc, 16)
            gath_g = semcc.n
            emit("gpsimd", lambda e: e.memset(Hacc[:, :], 0.0), semg, 1)
            emit("gpsimd", lambda e: e.memset(Hah[:, :], 0.0), semg, 1)
            emit("gpsimd", lambda e: e.memset(Hal[:, :], 0.0), semg, 1)
            hz_g = semg.n
            for eng in ("tensor", "vector", "scalar"):
                wait(eng, semg, hz_g)
                wait(eng, semcc, gath_g)
            for r in range(NCORES):
                wait("scalar", semv)
                wown("scalar")
                emit("scalar", lambda e, r=r: e.activation(
                    HSh[:, :], gath[:, r * 128:(r + 1) * 128], AF.Copy), sema, 1)
                wown("scalar")
                emit("scalar", lambda e: e.activation(
                    segT[:, :], HSh[:, :], AF.Copy), sema, 1)
                wait("vector", sema)
                emit("vector", lambda e, r=r: e.tensor_tensor(
                    HSl[:, :], gath[:, r * 128:(r + 1) * 128], segT[:, :],
                    ALU.subtract), semv, 1)
                sp_v = semv.n
                psc = pe_bank(PC)
                wait("tensor", semv, sp_v)
                emit("tensor", lambda e, psc=psc: e.matmul(
                    psc[:, 0:64], HSh[:, 64:128], Hah[:, :], start=True,
                    stop=False), sempe, 1)
                emit("tensor", lambda e, psc=psc: e.matmul(
                    psc[:, 0:64], HSh[:, 64:128], Hal[:, :], start=False,
                    stop=False), sempe, 1)
                emit("tensor", lambda e, psc=psc: e.matmul(
                    psc[:, 0:64], HSl[:, 64:128], Hah[:, :], start=False,
                    stop=False), sempe, 1)
                emit("tensor", lambda e, psc=psc: e.matmul(
                    psc[:, 0:64], identb[0:64, 0:64], HSh[:, 0:64], start=False,
                    stop=False), sempe, 1)
                emit("tensor", lambda e, psc=psc: e.matmul(
                    psc[:, 0:64], identb[0:64, 0:64], HSl[:, 0:64], start=False,
                    stop=True), sempe, 1)
                wown("vector")
                emit("vector", lambda e, r=r: e.tensor_scalar(
                    tmp64a[:, :], Hacc[:, :], oasel[:, r:r + 1], None, ALU.mult),
                    semv, 1)
                wait("vector", sempe)
                wown("vector")
                emit("vector", lambda e, r=r, psc=psc: e.scalar_tensor_tensor(
                    Hacc[:, :], psc[:, 0:64], asel[:, r:r + 1], tmp64a[:, :],
                    ALU.mult, ALU.add), PC.sem, 1)
                wait("scalar", semv)
                wown("scalar")
                emit("scalar", lambda e: e.activation(Hah[:, :], Hacc[:, :],
                                                      AF.Copy), sema, 1)
                wown("scalar")
                emit("scalar", lambda e: e.activation(tmp64b[:, :], Hah[:, :],
                                                      AF.Copy), sema, 1)
                wait("vector", sema)
                wown("vector")
                emit("vector", lambda e: e.tensor_tensor(
                    Hal[:, :], Hacc[:, :], tmp64b[:, :], ALU.subtract), semv, 1)
                wait("tensor", semv, semv.n)
                wait("tensor", sema, sema.n)
            return semv.n

        hiloa1_of = {}
        hiloa2_of = {}
        compose_init(semv.n, sema.n)
        for k in range(NCH // 2):
            cs = [(2 * k, PP[0]), (2 * k + 1, PP[1])]
            r1 = [prep_s1(c, P, pv[P["ptlo"]]) for c, P in cs]
            prep_s2pair(cs, [r[0] for r in r1], [r[1] for r in r1])
            bvs = [prep_s3(c, P, pv[P["ptlo"]], m1v)
                   for (c, P), (m1v, _) in zip(cs, r1)]
            bn_a = prep_s4pair(cs, bvs)
            for c, P in cs:
                prep_s4b(c, P, bn_a)
            b2a = [prep_s5(c, P) for c, P in cs]
            r1a = [prep_s6(c, P, a) for (c, P), a in zip(cs, b2a)]
            t1a = [prep_s7(c, P, a) for (c, P), a in zip(cs, r1a)]
            for (c, P), a in zip(cs, t1a):
                prep_s8(c, P, a)
            if k >= 1:
                compose_step(2 * (k - 1), ET1_s, PC, hiloa1_of[2 * (k - 1)])
                compose_step(2 * (k - 1) + 1, ET1_s, PC,
                             hiloa1_of[2 * (k - 1) + 1])
        compose_step(NCH - 2, ET1_s, PC, hiloa1_of[NCH - 2])
        compose_step(NCH - 1, ET1_s, PC, hiloa1_of[NCH - 1])

        for tname, src_ap in (("ktil", Ktil_s[:, :]), ("bt", AR[:, 0:2048]),
                              ("t1n", AR[:, 2048:4096]),
                              ("et1h", AR[0:64, 19456:20480]),
                              ("f1h", AR[0:64, 21504:22528]),
                              ("vtmf", ARF[:, 7680:8704]), ("m1", M1_t[:, :])):
            if tname in taps:
                wait("sync", semv, semv.n)
                wait("sync", sema, sema.n)
                wait("sync", sempe, sempe.n)
                emit("sync", lambda e, t=tname, s=src_ap: e.dma_start(
                    out=tap_aps[t], in_=s), semst, 16)
        if phases < 1.6:
            _replay(nc, q, semst, semcc)
            return nc
        hs1_v = exchange_finish(0)
        if "seg1" in taps:
            wait("sync", sema, sema.n)
            emit("sync", lambda e: e.dma_start(out=tap_aps["seg1"], in_=segT[:]),
                 semst, 16)
        if "gath1" in taps:
            wait("sync", semcc, semcc.n)
            emit("sync", lambda e: e.dma_start(out=tap_aps["gath1"], in_=gath[:]),
                 semst, 16)
        if "hstart1" in taps:
            wait("sync", semv, hs1_v)
            emit("sync", lambda e: e.dma_start(out=tap_aps["hstart1"], in_=Hacc[:]),
                 semst, 16)

        slot_v = [0] * NCH
        last_u_rd = [0, 0]
        last_db_rd = [0, 0]
        last_hilo_a = [0, 0]
        last_dlt_rd = [0, 0]
        last_xn_rd = [0, 0]
        last_t2_rd = [0, 0]
        last_wb2_rd = [0, 0]
        last_gt_rd = [0, 0]
        sig_a = [0] * NCH
        pred_pe_of = [0] * NCH
        ttr_v2 = [0] * NCH
        rstd_a2 = [0] * NCH
        last_ynb_rd = [0, 0]
        last_yn1_rd = [0, 0]

        def spine_seed(hs_v):
            wait("scalar", semv, hs_v)
            emit("scalar", lambda e: e.activation(H0b_s(0), Hacc[:, :], AF.Copy),
                 sema, 1)
            wown("scalar")
            emit("scalar", lambda e: e.activation(stmp, H0b_s(0), AF.Copy), sema, 1)
            h_a = sema.n
            wait("vector", sema, h_a)
            emit("vector", lambda e: e.tensor_tensor(
                H0l_s(0), Hacc[:, :], stmp, ALU.subtract), semv, 1)
            slot_v[0] = semv.n

        def spine_step(c, ST):
            ETh, ETl, Fh, Fl = ST
            psc = pe_bank(PC)
            wait("tensor", semv, slot_v[c - 1])
            emit("tensor", lambda e, c=c, psc=psc: e.matmul(
                psc[:, 0:64], ETh(c - 1), H0b_s(c - 1), start=True, stop=False),
                sempe, 1)
            emit("tensor", lambda e, c=c, psc=psc: e.matmul(
                psc[:, 0:64], ETh(c - 1), H0l_s(c - 1), start=False, stop=False),
                sempe, 1)
            emit("tensor", lambda e, c=c, psc=psc: e.matmul(
                psc[:, 0:64], ETl(c - 1), H0b_s(c - 1), start=False, stop=False),
                sempe, 1)
            emit("tensor", lambda e, c=c, psc=psc: e.matmul(
                psc[:, 0:64], identb[0:64, 0:64], Fh(c - 1), start=False,
                stop=False), sempe, 1)
            emit("tensor", lambda e, c=c, psc=psc: e.matmul(
                psc[:, 0:64], identb[0:64, 0:64], Fl(c - 1), start=False,
                stop=True), sempe, 1)
            adv_pe = sempe.n
            wait("scalar", sempe, adv_pe)
            emit("scalar", lambda e, c=c, psc=psc: e.activation(
                H0b_s(c), psc[:, 0:64], AF.Copy), sema, 1)
            wown("scalar")
            emit("scalar", lambda e, c=c: e.activation(stmp, H0b_s(c), AF.Copy),
                 sema, 1)
            h_a = sema.n
            wait("vector", sema, h_a)
            emit("vector", lambda e, c=c, psc=psc: e.tensor_tensor(
                H0l_s(c), psc[:, 0:64], stmp, ALU.subtract), PC.sem, 1)
            slot_v[c] = semv.n

        def tail_DU(c, Tt_s):
            b = c % 2
            psd = pe_bank(PS0)
            wait("tensor", semv, slot_v[c])
            emit("tensor", lambda e, c=c, psd=psd: e.matmul(
                psd[:, 0:64], csl(Ktil_s, c), H0b_s(c), start=True, stop=False),
                sempe, 1)
            emit("tensor", lambda e, c=c, psd=psd: e.matmul(
                psd[:, 0:64], csl(Ktil_s, c), H0l_s(c), start=False, stop=True),
                sempe, 1)
            d_pe = sempe.n
            wait("vector", sempe, d_pe)
            if last_db_rd[b]:
                wait("vector", sempe, last_db_rd[b])
            emit("vector", lambda e, c=c, b=b, psd=psd: e.scalar_tensor_tensor(
                Db_x[b], psd[:, 0:64], -1.0, Vtmf_s(c), ALU.mult, ALU.add),
                PS0.sem, 1)
            d_v = semv.n
            psu = pe_bank(PS1)
            wait("tensor", semv, d_v)
            emit("tensor", lambda e, c=c, b=b, psu=psu: e.matmul(
                psu[:, 0:64], Tt_s(c), Db_x[b], start=True, stop=True), sempe, 1)
            u_pe = sempe.n
            last_db_rd[b] = u_pe
            wait("scalar", sempe, u_pe)
            if last_u_rd[b]:
                wait("scalar", sempe, last_u_rd[b])
            emit("scalar", lambda e, b=b, psu=psu: e.activation(
                Ub_x[b], psu[:, 0:64], AF.Copy), PS1.sem, 1)
            return sema.n

        def tail_pred(c):
            b = c % 2
            u_a = tail_DU(c, T1t_s)
            psp = pe_bank(PS2, 1)
            wait("tensor", sema, u_a)
            emit("tensor", lambda e, c=c, psp=psp: e.matmul(
                psp[:, 0:64], csl(Qtil_s, c), H0b_s(c), start=True, stop=False),
                sempe, 1)
            emit("tensor", lambda e, c=c, psp=psp: e.matmul(
                psp[:, 0:64], csl(Qtil_s, c), H0l_s(c), start=False, stop=False),
                sempe, 1)
            emit("tensor", lambda e, c=c, b=b, psp=psp: e.matmul(
                psp[:, 0:64], P1m_s(c), Ub_x[b], start=False, stop=True), sempe, 1)
            p_pe = sempe.n
            last_u_rd[b] = p_pe
            pred_pe_of[c] = p_pe
            wait("vector", sempe, p_pe)
            emit("vector", lambda e, c=c, b=b, psp=psp: e.scalar_tensor_tensor(
                ef_x[b], psp[:, 0:64], -1.0, Vtmf_s(c), ALU.mult, ALU.add),
                PS2.sem, 1)
            wown("vector")
            emit("vector", lambda e, b=b: e.tensor_tensor(
                ysq[:, :], ef_x[b], ef_x[b], ALU.mult), semv, 1)
            if c >= 2 and sig_a[c - 2]:
                wait("vector", sema, sig_a[c - 2])
            wown("vector")
            emit("vector", lambda e, b=b: e.tensor_reduce(
                err_x[b], ysq[:, :], mybir.AxisListType.X, ALU.add), semv, 1)
            err_v = semv.n
            wait("scalar", semv, err_v)
            emit("scalar", lambda e, c=c, b=b: e.activation(
                scol[:, c:c + 1], err_x[b], AF.Sigmoid, scale=1.0 / STEMP),
                sema, 1)
            sig_a[c] = sema.n

        def prep2(c):
            b = c % 2
            wait("vector", sema, sig_a[c])
            emit("vector", lambda e, c=c, b=b: e.tensor_scalar(
                sm1_x[b], scol[:, c:c + 1], 1.0, None, ALU.subtract), semv, 1)
            if last_dlt_rd[b]:
                wait("vector", sempe, last_dlt_rd[b])
            wown("vector")
            emit("vector", lambda e, c=c, b=b: e.tensor_scalar(
                Dlt_x[b], Bt_s(c), sm1_x[b], None, ALU.mult), semv, 1)
            if last_wb2_rd[b]:
                wait("vector", sempe, last_wb2_rd[b])
            emit("vector", lambda e, c=c, b=b: e.tensor_scalar(
                Wb2_x[b], Wbtm_s(c), scol[:, c:c + 1], None, ALU.mult), semv, 1)
            wait("vector", sempe, pred_pe_of[c])
            emit("vector", lambda e, c=c: e.tensor_scalar(
                P1m_s(c), Ph1m_s(c), scol[:, c:c + 1], None, ALU.mult), semv, 1)
            dl_v = semv.n
            psx = pe_bank(PS0, 1)
            wait("tensor", semv, dl_v)
            emit("tensor", lambda e, c=c, b=b, psx=psx: e.matmul(
                psx[:, 0:128], Dlt_x[b], T1n_s(c), start=True, stop=True),
                sempe, 1)
            x_pe = sempe.n
            last_dlt_rd[b] = x_pe
            wait("scalar", sempe, x_pe)
            if last_xn_rd[b]:
                wait("scalar", sempe, last_xn_rd[b])
            emit("scalar", lambda e, b=b, psx=psx: e.activation(
                Xn_x[b], psx[:, 0:128], AF.Copy), PS0.sem, 1)
            x_a = sema.n
            pst = pe_bank(PS1, 1)
            wait("tensor", sema, x_a)
            emit("tensor", lambda e, c=c, b=b, pst=pst: e.matmul(
                pst[:, 0:128], T1t_s(c), Xn_x[b], start=True, stop=False), sempe, 1)
            emit("tensor", lambda e, c=c, pst=pst: e.matmul(
                pst[:, 0:128], identb[:, :], T1n_s(c), start=False, stop=True),
                sempe, 1)
            t2_pe = sempe.n
            last_xn_rd[b] = t2_pe
            wait("scalar", sempe, t2_pe)
            if last_t2_rd[b]:
                wait("scalar", sempe, last_t2_rd[b])
            emit("scalar", lambda e, b=b, pst=pst: e.activation(
                T2n_x[b], pst[:, 0:128], AF.Copy), PS1.sem, 1)
            t2n_a = sema.n
            pt2 = pe_bank(PTR, 1)[:].bitcast(BF16)
            wait("tensor", sema, t2n_a)
            emit("tensor", lambda e, b=b, pt2=pt2: e.transpose(
                pt2[:, 0:128], T2n_x[b], identb[:, :]), sempe, 1)
            wait("scalar", sempe)
            emit("scalar", lambda e, c=c, pt2=pt2: e.activation(
                T2t_s(c), pt2[:, 0:128], AF.Copy), PTR.sem, 1)
            psg = pe_bank(PS2, 1)
            emit("tensor", lambda e, b=b, psg=psg: e.matmul(
                psg[:, 0:64], T2n_x[b], Wb2_x[b], start=True, stop=True), sempe, 1)
            g_pe = sempe.n
            last_t2_rd[b] = g_pe
            last_wb2_rd[b] = g_pe
            wait("scalar", sempe, g_pe)
            if last_gt_rd[b]:
                wait("scalar", sempe, last_gt_rd[b])
            emit("scalar", lambda e, b=b, psg=psg: e.activation(
                Gt_x[b], psg[:, 0:64], AF.Copy), PS2.sem, 1)
            g_a = sema.n
            psc2 = pe_bank(PB0, 2)
            wait("tensor", sema, g_a)
            emit("tensor", lambda e, c=c, b=b, psc2=psc2: e.matmul(
                psc2[0:64, 0:64], Ktm_s(c), Gt_x[b], start=True, stop=False),
                sempe, 1)
            emit("tensor", lambda e, c=c, b=b, psc2=psc2: e.matmul(
                psc2[0:64, 64:128], Gt_x[b], Vtmb_s(c), start=False, stop=True),
                sempe, 1)
            ef_pe = sempe.n
            last_gt_rd[b] = ef_pe
            wait("vector", sempe, ef_pe)
            if last_hilo_a[b]:
                wait("vector", sema, last_hilo_a[b])
            emit("vector", lambda e, c=c, b=b, psc2=psc2: e.scalar_tensor_tensor(
                t64b_x[b], psc2[0:64, 0:64], -1.0, diagG_s(c), ALU.mult, ALU.add),
                PB0.sem, 1)
            emit("vector", lambda e, b=b, psc2=psc2: e.tensor_copy(
                t64c_x[b], psc2[0:64, 64:128]), PB0.sem, 1)
            hilo(t64b_x[b], ET2h_s(c), ET2l_s(c), t64a_x[b])
            hilo(t64c_x[b], F2h_s(c), F2l_s(c), t64a_x[b])
            last_hilo_a[b] = sema.n
            hiloa2_of[c] = (sema.n, semv.n)

        def tail_y(c):
            b = c % 2
            u_a = tail_DU(c, T2t_s)
            psy = pe_bank(PS2, 1)
            wait("tensor", sema, u_a)
            emit("tensor", lambda e, c=c, psy=psy: e.matmul(
                psy[:, 0:64], csl(Qhat_s, c), H0b_s(c), start=True, stop=False),
                sempe, 1)
            emit("tensor", lambda e, c=c, psy=psy: e.matmul(
                psy[:, 0:64], csl(Qhat_s, c), H0l_s(c), start=False, stop=False),
                sempe, 1)
            emit("tensor", lambda e, c=c, b=b, psy=psy: e.matmul(
                psy[:, 0:64], P1m_s(c), Ub_x[b], start=False, stop=True), sempe, 1)
            y_pe = sempe.n
            last_u_rd[b] = y_pe
            wait("scalar", sempe, y_pe)
            if c >= 2 and ttr_v2[c - 2]:
                wait("scalar", semv, ttr_v2[c - 2])
            if last_yn1_rd[b]:
                wait("scalar", semv, last_yn1_rd[b])
            emit("scalar", lambda e, b=b, psy=psy: e.activation(
                ytm_x[b], psy[:, 0:64], AF.Copy), PS2.sem, 1)
            ytm_a = sema.n
            if "ytm" in taps:
                wait("sync", sema, ytm_a)
                emit("sync", lambda e, c=c, b=b: e.dma_start(
                    out=tap_aps["ytm"][:, c * DS:(c + 1) * DS], in_=ytm_x[b]),
                    semst, 16)
            wait("vector", sema, ytm_a)
            wown("vector")
            emit("vector", lambda e, b=b: e.tensor_tensor(
                ysq[:, :], ytm_x[b], ytm_x[b], ALU.mult), semv, 1)
            wown("vector")
            emit("vector", lambda e, b=b: e.tensor_reduce(
                ms_x[b], ysq[:, :], mybir.AxisListType.X, ALU.add), semv, 1)
            ttr_v2[c] = semv.n
            wait("scalar", semv, ttr_v2[c])
            emit("scalar", lambda e, b=b: e.activation(
                rs_x[b], ms_x[b], AF.Ln, scale=1.0 / DS, bias=cbias[:, 2:3]),
                sema, 1)
            wown("scalar")
            emit("scalar", lambda e, b=b: e.activation(
                rs_x[b], rs_x[b], AF.Exp, scale=-0.5), sema, 1)
            wown("scalar")
            emit("scalar", lambda e, b=b: e.activation(
                yn1_x[b], ytm_x[b], AF.Copy, scale=rs_x[b]), sema, 1)
            rstd_a2[c] = sema.n
            wait("vector", sema, rstd_a2[c])
            if last_ynb_rd[b]:
                wait("vector", sempe, last_ynb_rd[b])
            emit("vector", lambda e, b=b: e.tensor_tensor(
                ynb_x[b], yn1_x[b], normw_t[:, :], ALU.mult), semv, 1)
            ynb_v = semv.n
            last_yn1_rd[b] = ynb_v
            pty = pe_bank(PTR, 1)[:].bitcast(BF16)
            wait("tensor", semv, ynb_v)
            emit("tensor", lambda e, b=b, pty=pty: e.transpose(
                pty[0:64, 0:128], ynb_x[b], identb[:, :]), sempe, 1)
            ytr = sempe.n
            last_ynb_rd[b] = ytr
            wait("scalar", sempe, ytr)
            emit("scalar", lambda e, c=c, pty=pty: e.activation(
                ynfm[:, c * L:(c + 1) * L], pty[0:64, 0:128], AF.Copy), PTR.sem, 1)

        if phases < 1.9:
            _replay(nc, q, semst, semcc)
            return nc
        wait("scalar", semw2b[0], semw2b[0].n)
        wait("scalar", semw2b[1], semw2b[1].n)
        wait("vector", semw2b[0], semw2b[0].n)
        wait("vector", semw2b[1], semw2b[1].n)
        d_v1 = [0] * NCH
        u_a1 = [0] * NCH
        dl_v1 = [0] * NCH
        x_a1 = [0] * NCH
        t2n_a1 = [0] * NCH
        g_a1 = [0] * NCH

        def s1_D(c, Tt_s):
            b = c % 2
            psd = pe_bank(PS0)
            wait("tensor", semv, slot_v[c])
            emit("tensor", lambda e, c=c, psd=psd: e.matmul(
                psd[:, 0:64], csl(Ktil_s, c), H0b_s(c), start=True, stop=False),
                sempe, 1)
            emit("tensor", lambda e, c=c, psd=psd: e.matmul(
                psd[:, 0:64], csl(Ktil_s, c), H0l_s(c), start=False, stop=True),
                sempe, 1)
            d_pe = sempe.n
            wait("vector", sempe, d_pe)
            if last_db_rd[b]:
                wait("vector", sempe, last_db_rd[b])
            emit("vector", lambda e, c=c, b=b, psd=psd: e.scalar_tensor_tensor(
                Db_x[b], psd[:, 0:64], -1.0, Vtmf_s(c), ALU.mult, ALU.add),
                PS0.sem, 1)
            d_v1[c] = semv.n

        def s2_U(c, Tt_s):
            b = c % 2
            psu = pe_bank(PS1)
            wait("tensor", semv, d_v1[c])
            emit("tensor", lambda e, c=c, b=b, psu=psu: e.matmul(
                psu[:, 0:64], Tt_s(c), Db_x[b], start=True, stop=True), sempe, 1)
            u_pe = sempe.n
            last_db_rd[b] = u_pe
            wait("scalar", sempe, u_pe)
            if last_u_rd[b]:
                wait("scalar", sempe, last_u_rd[b])
            emit("scalar", lambda e, b=b, psu=psu: e.activation(
                Ub_x[b], psu[:, 0:64], AF.Copy), PS1.sem, 1)
            u_a1[c] = sema.n

        def s3_pred(c):
            b = c % 2
            psp = pe_bank((PS2, PB1)[b], 1)
            wait("tensor", sema, u_a1[c])
            emit("tensor", lambda e, c=c, psp=psp: e.matmul(
                psp[:, 0:64], csl(Qtil_s, c), H0b_s(c), start=True, stop=False),
                sempe, 1)
            emit("tensor", lambda e, c=c, psp=psp: e.matmul(
                psp[:, 0:64], csl(Qtil_s, c), H0l_s(c), start=False, stop=False),
                sempe, 1)
            emit("tensor", lambda e, c=c, b=b, psp=psp: e.matmul(
                psp[:, 0:64], P1m_s(c), Ub_x[b], start=False, stop=True), sempe, 1)
            p_pe = sempe.n
            last_u_rd[b] = p_pe
            pred_pe_of[c] = p_pe

        def s4_err(c):
            b = c % 2
            psp = (PS2, PB1)[b].t
            wait("vector", sempe, pred_pe_of[c])
            emit("vector", lambda e, c=c, b=b, psp=psp: e.scalar_tensor_tensor(
                ef_x[b], psp[:, 0:64], -1.0, Vtmf_s(c), ALU.mult, ALU.add),
                (PS2, PB1)[b].sem, 1)
            wown("vector")
            emit("vector", lambda e, b=b: e.tensor_tensor(
                ysq[:, :], ef_x[b], ef_x[b], ALU.mult), semv, 1)
            if c >= 2 and sig_a[c - 2]:
                wait("vector", sema, sig_a[c - 2])
            wown("vector")
            emit("vector", lambda e, b=b: e.tensor_reduce(
                err_x[b], ysq[:, :], mybir.AxisListType.X, ALU.add), semv, 1)
            err_v = semv.n
            wait("scalar", semv, err_v)
            emit("scalar", lambda e, c=c, b=b: e.activation(
                scol[:, c:c + 1], err_x[b], AF.Sigmoid, scale=1.0 / STEMP),
                sema, 1)
            sig_a[c] = sema.n

        def s5_dl(c):
            b = c % 2
            wait("vector", sema, sig_a[c])
            emit("vector", lambda e, c=c, b=b: e.tensor_scalar(
                sm1_x[b], scol[:, c:c + 1], 1.0, None, ALU.subtract), semv, 1)
            if last_dlt_rd[b]:
                wait("vector", sempe, last_dlt_rd[b])
            wown("vector")
            emit("vector", lambda e, c=c, b=b: e.tensor_scalar(
                Dlt_x[b], Bt_s(c), sm1_x[b], None, ALU.mult), semv, 1)
            wait("vector", sempe, pred_pe_of[c])
            emit("vector", lambda e, c=c: e.tensor_scalar(
                P1m_s(c), Ph1m_s(c), scol[:, c:c + 1], None, ALU.mult), semv, 1)
            dl_v1[c] = semv.n

        def s6_X(c):
            b = c % 2
            psx = pe_bank(PB0, 1)
            wait("tensor", semv, dl_v1[c])
            emit("tensor", lambda e, c=c, b=b, psx=psx: e.matmul(
                psx[:, 0:128], Dlt_x[b], T1n_s(c), start=True, stop=True),
                sempe, 1)
            x_pe = sempe.n
            last_dlt_rd[b] = x_pe
            wait("scalar", sempe, x_pe)
            if last_xn_rd[b]:
                wait("scalar", sempe, last_xn_rd[b])
            emit("scalar", lambda e, b=b, psx=psx: e.activation(
                Xn_x[b], psx[:, 0:128], AF.Copy), PB0.sem, 1)
            x_a1[c] = sema.n

        def s7_T2(c):
            b = c % 2
            pst = pe_bank(PS0, 1)
            wait("tensor", sema, x_a1[c])
            emit("tensor", lambda e, c=c, b=b, pst=pst: e.matmul(
                pst[:, 0:128], T1t_s(c), Xn_x[b], start=True, stop=False),
                sempe, 1)
            emit("tensor", lambda e, c=c, pst=pst: e.matmul(
                pst[:, 0:128], identb[:, :], T1n_s(c), start=False, stop=True),
                sempe, 1)
            t2_pe = sempe.n
            last_xn_rd[b] = t2_pe
            wait("scalar", sempe, t2_pe)
            if last_t2_rd[b]:
                wait("scalar", sempe, last_t2_rd[b])
            emit("scalar", lambda e, b=b, pst=pst: e.activation(
                T2n_x[b], pst[:, 0:128], AF.Copy), PS0.sem, 1)
            if last_wb2_rd[b]:
                wait("scalar", sempe, last_wb2_rd[b])
            wait("scalar", sema, sig_a[c])
            emit("scalar", lambda e, c=c, b=b: e.activation(
                Wb2_x[b], Wbtm_s(c), AF.Copy, scale=scol[:, c:c + 1]), sema, 1)
            t2n_a1[c] = sema.n

        def s8_TRG(c):
            b = c % 2
            pt2 = pe_bank(PTR, 1)[:].bitcast(BF16)
            wait("tensor", sema, t2n_a1[c])
            emit("tensor", lambda e, b=b, pt2=pt2: e.transpose(
                pt2[:, 0:128], T2n_x[b], identb[:, :]), sempe, 1)
            wait("scalar", sempe)
            emit("scalar", lambda e, c=c, pt2=pt2: e.activation(
                T2t_s(c), pt2[:, 0:128], AF.Copy), PTR.sem, 1)
            psg = pe_bank(PROW, 1)
            emit("tensor", lambda e, b=b, psg=psg: e.matmul(
                psg[:, 0:64], T2n_x[b], Wb2_x[b], start=True, stop=True), sempe, 1)
            g_pe = sempe.n
            last_t2_rd[b] = g_pe
            last_wb2_rd[b] = g_pe
            wait("scalar", sempe, g_pe)
            if last_gt_rd[b]:
                wait("scalar", sempe, last_gt_rd[b])
            emit("scalar", lambda e, b=b, psg=psg: e.activation(
                Gt_x[b], psg[:, 0:64], AF.Copy), PROW.sem, 1)
            g_a1[c] = sema.n

        def s9_EF(c):
            b = c % 2
            psc2 = pe_bank(PS1, 2)
            wait("tensor", sema, g_a1[c])
            emit("tensor", lambda e, c=c, b=b, psc2=psc2: e.matmul(
                psc2[0:64, 0:64], Ktm_s(c), Gt_x[b], start=True, stop=False),
                sempe, 1)
            emit("tensor", lambda e, c=c, b=b, psc2=psc2: e.matmul(
                psc2[0:64, 64:128], Gt_x[b], Vtmb_s(c), start=False, stop=True),
                sempe, 1)
            ef_pe = sempe.n
            last_gt_rd[b] = ef_pe
            wait("vector", sempe, ef_pe)
            if last_hilo_a[b]:
                wait("vector", sema, last_hilo_a[b])
            emit("vector", lambda e, c=c, b=b, psc2=psc2: e.scalar_tensor_tensor(
                t64b_x[b], psc2[0:64, 0:64], -1.0, diagG_s(c), ALU.mult, ALU.add),
                PS1.sem, 1)
            ef_v = semv.n
            wait("scalar", semv, ef_v)
            emit("scalar", lambda e, b=b, psc2=psc2: e.activation(
                t64c_x[b], psc2[0:64, 64:128], AF.Copy), PS1.sem, 1)
            hilo(t64b_x[b], ET2h_s(c), ET2l_s(c), t64a_x[b])
            hilo(t64c_x[b], F2h_s(c), F2l_s(c), t64a_x[b])
            last_hilo_a[b] = sema.n
            hiloa2_of[c] = (sema.n, semv.n)

        compose_init(semv.n, sema.n)
        spine_seed(hs1_v)
        for i in range(NCH + 9):
            if i + 1 < NCH:
                spine_step(i + 1, ET1_s)
            if i < NCH:
                s1_D(i, T1t_s)
            if 1 <= i < NCH + 1:
                s2_U(i - 1, T1t_s)
            if 2 <= i < NCH + 2:
                s3_pred(i - 2)
            if 3 <= i < NCH + 3:
                s4_err(i - 3)
            if 4 <= i < NCH + 4:
                s5_dl(i - 4)
            if 5 <= i < NCH + 5:
                s6_X(i - 5)
            if 6 <= i < NCH + 6:
                s7_T2(i - 6)
            if 7 <= i < NCH + 7:
                s8_TRG(i - 7)
            if 8 <= i < NCH + 8:
                s9_EF(i - 8)
            if 9 <= i < NCH + 9:
                compose_step(i - 9, ET2_s, PROW, hiloa2_of[i - 9])
        s_a = sema.n
        if "scol" in taps:
            wait("sync", sema, s_a)
            emit("sync", lambda e: e.dma_start(out=tap_aps["scol"], in_=scol[:]),
                 semst, 16)

        if phases < 3:
            _replay(nc, q, semst, semcc)
            return nc
        hs2_v = exchange_finish(1)
        u_a2 = [0] * NCH
        y_pe2 = [0] * NCH
        ytm_a2 = [0] * NCH

        def s2_U2(c):
            b = c % 2
            psu = pe_bank(PS1)
            wait("tensor", semv, d_v1[c])
            emit("tensor", lambda e, c=c, b=b, psu=psu: e.matmul(
                psu[:, 0:64], T2t_s(c), Db_x[b], start=True, stop=True), sempe, 1)
            u_pe = sempe.n
            last_db_rd[b] = u_pe
            wait("scalar", sempe, u_pe)
            if last_u_rd[b]:
                wait("scalar", sempe, last_u_rd[b])
            emit("scalar", lambda e, b=b, psu=psu: e.activation(
                Ub_x[b], psu[:, 0:64], AF.Copy), PS1.sem, 1)
            u_a2[c] = sema.n

        def s3_y(c):
            b = c % 2
            psy = pe_bank((PS2, PB1)[b], 1)
            wait("tensor", sema, u_a2[c])
            emit("tensor", lambda e, c=c, psy=psy: e.matmul(
                psy[:, 0:64], csl(Qhat_s, c), H0b_s(c), start=True, stop=False),
                sempe, 1)
            emit("tensor", lambda e, c=c, psy=psy: e.matmul(
                psy[:, 0:64], csl(Qhat_s, c), H0l_s(c), start=False, stop=False),
                sempe, 1)
            emit("tensor", lambda e, c=c, b=b, psy=psy: e.matmul(
                psy[:, 0:64], P1m_s(c), Ub_x[b], start=False, stop=True), sempe, 1)
            y_pe2[c] = sempe.n
            last_u_rd[b] = y_pe2[c]

        def s4_ms(c):
            b = c % 2
            psy = (PS2, PB1)[b].t
            wait("scalar", sempe, y_pe2[c])
            if c >= 2 and ttr_v2[c - 2]:
                wait("scalar", semv, ttr_v2[c - 2])
            if last_yn1_rd[b]:
                wait("scalar", semv, last_yn1_rd[b])
            if c >= 2 and rstd_a2[c - 2]:
                wait("scalar", sema, rstd_a2[c - 2])
            emit("scalar", lambda e, b=b, psy=psy: e.activation(
                ytm_x[b], psy[:, 0:64], AF.Copy), (PS2, PB1)[b].sem, 1)
            ytm_a2[c] = sema.n
            if "ytm" in taps:
                wait("sync", sema, ytm_a2[c])
                emit("sync", lambda e, c=c, b=b: e.dma_start(
                    out=tap_aps["ytm"][:, c * DS:(c + 1) * DS], in_=ytm_x[b]),
                    semst, 16)
            wait("vector", sema, ytm_a2[c])
            wown("vector")
            emit("vector", lambda e, b=b: e.tensor_tensor(
                ysq[:, :], ytm_x[b], ytm_x[b], ALU.mult), semv, 1)
            wown("vector")
            emit("vector", lambda e, b=b: e.tensor_reduce(
                ms_x[b], ysq[:, :], mybir.AxisListType.X, ALU.add), semv, 1)
            ttr_v2[c] = semv.n

        def s5_rs(c):
            b = c % 2
            wait("scalar", semv, ttr_v2[c])
            emit("scalar", lambda e, b=b: e.activation(
                rs_x[b], ms_x[b], AF.Ln, scale=1.0 / DS, bias=cbias[:, 2:3]),
                sema, 1)
            wown("scalar")
            emit("scalar", lambda e, b=b: e.activation(
                rs_x[b], rs_x[b], AF.Exp, scale=-0.5), sema, 1)
            wown("scalar")
            emit("scalar", lambda e, b=b: e.activation(
                yn1_x[b], ytm_x[b], AF.Copy, scale=rs_x[b]), sema, 1)
            rstd_a2[c] = sema.n

        def s6_out(c):
            b = c % 2
            wait("vector", sema, rstd_a2[c])
            if last_ynb_rd[b]:
                wait("vector", sempe, last_ynb_rd[b])
            emit("vector", lambda e, b=b: e.tensor_tensor(
                ynb_x[b], yn1_x[b], normw_t[:, :], ALU.mult), semv, 1)
            ynb_v = semv.n
            last_yn1_rd[b] = ynb_v
            pty = pe_bank(PTR, 1)[:].bitcast(BF16)
            wait("tensor", semv, ynb_v)
            emit("tensor", lambda e, b=b, pty=pty: e.transpose(
                pty[0:64, 0:128], ynb_x[b], identb[:, :]), sempe, 1)
            ytr = sempe.n
            last_ynb_rd[b] = ytr
            wait("scalar", sempe, ytr)
            emit("scalar", lambda e, c=c, pty=pty: e.activation(
                ynfm[:, c * L:(c + 1) * L], pty[0:64, 0:128], AF.Copy), PTR.sem, 1)

        spine_seed(hs2_v)
        for i in range(NCH + 6):
            if i + 1 < NCH:
                spine_step(i + 1, ET2_s)
            if i < NCH:
                s1_D(i, T2t_s)
            if 1 <= i < NCH + 1:
                s2_U2(i - 1)
            if 2 <= i < NCH + 2:
                s3_y(i - 2)
            if 3 <= i < NCH + 3:
                s4_ms(i - 3)
            if 4 <= i < NCH + 4:
                s5_rs(i - 4)
            if 5 <= i < NCH + 5:
                s6_out(i - 5)
        scan_a = sema.n
        scan_v = semv.n
        scan_pe = sempe.n
        if "ynfm" in taps:
            wait("sync", sema, scan_a)
            emit("sync", lambda e: e.dma_start(out=tap_aps["ynfm"], in_=ynfm[:]),
                 semst, 16)

        if phases < 4:
            _replay(nc, q, semst, semcc)
            return nc
        for eng in ("tensor", "vector", "scalar", "gpsimd", "sync"):
            wait(eng, sema, scan_a)
            wait(eng, semv, scan_v)
            wait(eng, sempe, scan_pe)

        w2_pe = {}
        xr_tt = {}
        ob_last = [0, 0, 0]
        cast_a_q = [0] * 4
        x2n_rd_pe = [0] * 4
        mrs2_rd_pe = [0] * 4
        bcast_pe_q = [0] * 4
        stats_fin_v = [0] * 4
        rs2hl_q = [0] * 4
        FQ = 4
        x2q = lambda qq, o: ARF[:, (qq % 2) * 8192 + o * 512: (qq % 2) * 8192
                                + o * 512 + 512]
        x2nq = lambda qq, o: SCRB[:, (qq % 2) * 8192 + o * 512: (qq % 2) * 8192
                                  + o * 512 + 512]

        def attn_stats_tile(qq, o, ps_sx, ps_sq):
            t0 = qq * 512
            idx = qq * NDCH + o
            bank = (PB0, PB1, PTR)[idx % 3]
            ps = pe_bank(bank, 1)
            emit("tensor", lambda e, o=o, t0=t0, ps=ps: e.matmul(
                ps[:, :], wo_b[:, o * 128:(o + 1) * 128],
                ynfm[:, t0: t0 + 512], start=True, stop=True), sempe, 1)
            mm_pe = sempe.n
            rot = idx % 4
            xq_eng = "sync"
            if idx >= 4:
                wait(xq_eng, semv, xr_tt[idx - 4])
            emit(xq_eng, lambda e, o=o, t0=t0, rot=rot: e.dma_start(
                out=xrl[rot],
                in_=xf[o * 128:(o + 1) * 128, t0: t0 + 512]), semxr[rot], 16)
            xr_ld = semxr[rot].n
            wait("vector", sempe, mm_pe)
            wait("vector", semxr[rot], xr_ld)
            if qq >= 2 and o == 0:
                wait("vector", sema, cast_a_q[qq - 2])
                wown("vector")
            emit("vector", lambda e, qq=qq, o=o, rot=rot, ps=ps:
                 e.scalar_tensor_tensor(
                x2q(qq, o), ps[:, :], 1.0, xrl[rot], ALU.mult, ALU.add),
                bank.sem, 1)
            xr_tt[idx] = semv.n
            rot2 = idx % 2
            wait("scalar", semv, xr_tt[idx])
            if idx >= 2:
                wait("scalar", sempe, w2_pe.get(("cast", idx - 2), 0))
            emit("scalar", lambda e, qq=qq, o=o, rot2=rot2: e.activation(
                x2b_t[rot2], x2q(qq, o), AF.Copy), sema, 1)
            emit("scalar", lambda e, qq=qq, o=o, rot2=rot2: e.activation(
                x2sq_t[rot2], x2q(qq, o), AF.Square), sema, 1)
            w2_pe[("casta", idx)] = sema.n
            if o == NDCH - 1:
                cast_a_q[qq] = sema.n

        def stats_mms(qq, o, ps_sx, ps_sq):
            idx = qq * NDCH + o
            rot2 = idx % 2
            wait("tensor", sema, w2_pe[("casta", idx)])
            emit("tensor", lambda e, o=o, rot2=rot2, ps_sx=ps_sx: e.matmul(
                ps_sx[0:1, 0:512], onescol_b[:], x2b_t[rot2],
                start=(o == 0), stop=(o == NDCH - 1)), sempe, 1)
            emit("tensor", lambda e, o=o, rot2=rot2, ps_sq=ps_sq: e.matmul(
                ps_sq[0:1, 0:512], onescol_b[:], x2sq_t[rot2],
                start=(o == 0), stop=(o == NDCH - 1)), sempe, 1)
            w2_pe[("cast", idx)] = sempe.n

        def stats_finish(qq, ps_sx, ps_sq):
            st_pe = w2_pe[("cast", qq * NDCH + NDCH - 1)]
            wait("scalar", sempe, st_pe)
            if qq >= 1:
                wait("scalar", semv, rs2hl_q[qq - 1])
            emit("scalar", lambda e, ps_sx=ps_sx: e.activation(
                rowA[0:1, 0:512], ps_sx[0:1, 0:512], AF.Copy), PROW.sem, 1)
            wown("scalar")
            emit("scalar", lambda e: e.activation(
                rowA[0:1, 2048:2560], rowA[0:1, 0:512], AF.Square,
                scale=1.0 / D), sema, 1)
            m2_a = sema.n
            wait("vector", sema, m2_a)
            wait("vector", sempe, st_pe)
            emit("vector", lambda e, ps_sq=ps_sq: e.scalar_tensor_tensor(
                rowA[0:1, 2048:2560], ps_sq[0:1, 0:512], 1.0 / D,
                rowA[0:1, 2048:2560], ALU.mult, ALU.subtract), PC.sem, 1)
            var_v = semv.n
            wait("scalar", semv, var_v)
            emit("scalar", lambda e: e.activation(
                rowA[0:1, 2560:3072], rowA[0:1, 2048:2560], AF.Ln,
                bias=cbias[0:1, 1:2]), sema, 1)
            wown("scalar")
            emit("scalar", lambda e: e.activation(
                rowA[0:1, 2560:3072], rowA[0:1, 2560:3072], AF.Exp,
                scale=-0.5), sema, 1)
            rs2_a = sema.n
            wait("vector", sema, rs2_a)
            if qq >= 1 and mrs2_rd_pe[qq - 1]:
                wait("vector", sempe, mrs2_rd_pe[qq - 1])
            emit("vector", lambda e: e.tensor_tensor(
                rowC[0:1, 2048:2560], rowA[0:1, 0:512],
                rowA[0:1, 2560:3072], ALU.mult), semv, 1)
            if qq >= 1 and bcast_pe_q[qq - 1]:
                wait("scalar", sempe, bcast_pe_q[qq - 1])
            wown("scalar")
            emit("scalar", lambda e: e.activation(
                rowC[0:1, 3072:3584], rowA[0:1, 2560:3072], AF.Copy), sema, 1)
            wown("scalar")
            emit("scalar", lambda e: e.activation(
                rowA[0:1, 512:1024], rowC[0:1, 3072:3584], AF.Copy), sema, 1)
            wait("vector", sema)
            emit("vector", lambda e: e.tensor_tensor(
                rowC[0:1, 3584:4096], rowA[0:1, 2560:3072],
                rowA[0:1, 512:1024], ALU.subtract), semv, 1)
            rs2hl_q[qq] = semv.n
            psb = pe_bank(PTR, 1)
            wait("tensor", semv, rs2hl_q[qq])
            wait("tensor", sema)
            emit("tensor", lambda e, psb=psb: e.matmul(
                psb[:, 0:512], onesrow_b[0:1, :], rowC[0:1, 3072:3584],
                start=True, stop=False), sempe, 1)
            emit("tensor", lambda e, psb=psb: e.matmul(
                psb[:, 0:512], onesrow_b[0:1, :], rowC[0:1, 3584:4096],
                start=False, stop=True), sempe, 1)
            bcast_pe_q[qq] = sempe.n
            wait("vector", sempe, bcast_pe_q[qq])
            emit("vector", lambda e, psb=psb: e.tensor_copy(
                rs2b[:, :], psb[:, 0:512]), PTR.sem, 1)
            if qq >= 2 and x2n_rd_pe[qq - 2]:
                wait("vector", sempe, x2n_rd_pe[qq - 2])
            wown("vector")
            for o in range(NDCH):
                emit("vector", lambda e, qq=qq, o=o: e.tensor_tensor(
                    x2nq(qq, o), x2q(qq, o), rs2b[:, :], ALU.mult), semv, 1)
            stats_fin_v[qq] = semv.n

        def flush_store():
            if pending_store:
                o_v, sq, so, sj, sov = pending_store.pop(0)
                wait("sync", semv, o_v)
                emit("sync", lambda e, sq=sq, so=so, sov=sov: e.dma_start(
                    out=outfm[so * 128:(so + 1) * 128,
                              sq * 512:(sq + 1) * 512],
                    in_=sov[:, :]), semo[sj], 16)
                ob_last[sj] = semo[sj].n

        def ffn_tile(qq, o):
            flush_store()
            t = qq * NDCH + o
            wt = w2t[t % 2]
            if t == 0:
                wait("gpsimd", semw2b[0], semw2b[0].n)
                wait("gpsimd", semw2b[1], semw2b[1].n)
            if t >= 1:
                prev = w2_pe.get(("ffn", t - 2))
                if prev:
                    wait("gpsimd", sempe, prev)
            emit("gpsimd", lambda e, o=o, wt=wt: e.dma_start(
                out=wt[:], in_=w2bf[o * 128:(o + 1) * 128, :]), semw2[t % 2], 16)
            wld = semw2[t % 2].n
            wsem = semw2[t % 2]
            nsw_a = sema.n
            if qq == 0:
                psn = pe_bank(PTR, 1)
                wait("tensor", wsem, wld)
                for i in range(NDCH):
                    emit("tensor", lambda e, i=i, wt=wt, psn=psn: e.matmul(
                        psn[0:1, 0:128], onescol_b[:],
                        wt[:, i * 128:(i + 1) * 128],
                        start=(i == 0), stop=(i == NDCH - 1)), sempe, 1)
                wait("scalar", sempe)
                emit("scalar", lambda e, o=o, psn=psn: e.activation(
                    rowC[0:1, o * 128:(o + 1) * 128], psn[0:1, 0:128],
                    AF.Copy, scale=-1.0 / D), PTR.sem, 1)
                nsw_a = sema.n
            bank3 = (PS0, PS1, PS2)[t % 3]
            ps = pe_bank(bank3, 1)
            wait("tensor", wsem, wld)
            wait("tensor", semv, stats_fin_v[qq])
            if qq == 0:
                wait("tensor", sema, nsw_a)
            for d in range(NDCH):
                emit("tensor", lambda e, qq=qq, d=d, wt=wt, ps=ps: e.matmul(
                    ps[:, :], wt[:, d * 128:(d + 1) * 128], x2nq(qq, d),
                    start=(d == 0), stop=False), sempe, 1)
            emit("tensor", lambda e, o=o, ps=ps: e.matmul(
                ps[:, :], rowC[0:1, o * 128:(o + 1) * 128],
                rowC[0:1, 2048:2560], start=False, stop=True), sempe, 1)
            ffn_pe = sempe.n
            w2_pe[("ffn", t)] = ffn_pe
            x2n_rd_pe[qq] = ffn_pe
            mrs2_rd_pe[qq] = ffn_pe
            j = t % 3
            ov = out_bufs[j]
            wait("scalar", sempe, ffn_pe)
            if ob_last[j]:
                wait("scalar", semo[j], ob_last[j])
            emit("scalar", lambda e, o=o, ps=ps, ov=ov: e.activation(
                ov[:, :], ps[:, :], AF.Copy, scale=sw2_t[:, o:o + 1]),
                bank3.sem, 1)
            f1_a = sema.n
            wait("vector", sema, f1_a)
            emit("vector", lambda e, qq=qq, o=o, ov=ov: e.scalar_tensor_tensor(
                ov[:, :], ov[:, :], b2_t[:, o:o + 1], x2q(qq, o),
                ALU.add, ALU.add), semv, 1)
            o_v = semv.n
            pending_store.append((o_v, qq, o, j, ov))

        pending_store = []
        for qq in range(FQ + 1):
            ps_sx = ps_sq = None
            if qq < FQ:
                ps_sx = pe_bank(PROW, 1)
                ps_sq = pe_bank(PC, 1)
            for o in range(NDCH):
                if qq < FQ:
                    attn_stats_tile(qq, o, ps_sx, ps_sq)
                if qq >= 1:
                    ffn_tile(qq - 1, o)
                if qq < FQ and o >= 1:
                    stats_mms(qq, o - 1, ps_sx, ps_sq)
            if qq < FQ:
                stats_mms(qq, NDCH - 1, ps_sx, ps_sq)
                stats_finish(qq, ps_sx, ps_sq)
        flush_store()

        for j in range(3):
            wait("gpsimd", semo[j], semo[j].n)
        _replay(nc, q, semst, semcc)

    return nc



def _prep_shared(inputs):
    f32 = np.float32
    Wk = np.asarray(inputs["L0_attn_Wk"], f32)
    Wv = np.asarray(inputs["L0_attn_Wv"], f32)
    Wq = np.asarray(inputs["L0_attn_Wq"], f32)
    Wa = np.asarray(inputs["L0_attn_Wa_w"], f32)
    def tile_lhs(wt):
        F = wt.shape[1]
        return np.ascontiguousarray(
            wt.reshape(NDCH, 128, F).transpose(1, 0, 2).reshape(128, NDCH * F))

    w2w = np.asarray(inputs["L0_ffn_weight"], f32)
    w2h = np.ascontiguousarray(
        w2w.reshape(NDCH, 128, NDCH, 128).transpose(0, 3, 2, 1).reshape(
            NDCH * 128, NDCH * 128))
    shared = {
        "wkq": tile_lhs(np.concatenate([Wk, Wq], 0).T),
        "wav": tile_lhs(np.concatenate([Wa, Wv], 0).T),
        "wo": np.ascontiguousarray(np.asarray(inputs["L0_attn_Wo"], f32).T),
        "w2": w2h,
        "wab": np.ascontiguousarray(
            np.asarray(inputs["L0_attn_Wa_b"], f32).reshape(DS, 1)),
        "lam": np.ascontiguousarray(
            np.asarray(inputs["L0_attn_lam"], f32).reshape(DS, 1)),
        "normw": np.ascontiguousarray(np.broadcast_to(
            np.asarray(inputs["L0_attn_norm_w"], f32), (128, DS))),
        "sw2c": np.ascontiguousarray(
            np.asarray(inputs["L0_ffn_scale_weight"], f32).reshape(NDCH, 128).T),
        "b2c": np.ascontiguousarray(
            np.asarray(inputs["L0_ffn_bias"], f32).reshape(NDCH, 128).T),
        "identf": np.eye(128, dtype=f32),
        "mask_s": np.ascontiguousarray(np.triu(np.ones((128, 128), f32), 1)),
        "mask_i": np.ascontiguousarray(np.triu(np.ones((128, 128), f32), 0)),
    }
    return shared


def kernel(**inputs):
    x = np.asarray(inputs["L0_attn_x"], np.float32)[0]
    shared = _prep_shared(inputs)
    nc = build_program(taps=DEBUG_TAPS)
    in_maps = []
    for c in range(NCORES):
        sel = np.zeros((DS, NCORES), np.float32)
        sel[:, :c] = 1.0
        m = dict(shared)
        m["xf"] = np.ascontiguousarray(x[c * T:(c + 1) * T, :].T)
        m["alphasel"] = sel
        in_maps.append(m)
    res = run_bass_kernel_spmd(nc, in_maps, core_ids=list(range(NCORES)))
    out = np.concatenate(
        [np.asarray(res.results[c]["outfm"]).T for c in range(NCORES)], 0)
    return out.reshape(1, NCORES * T, D)
